# version 03
# baseline (speedup 1.0000x reference)
import math
from contextlib import ExitStack

import numpy as np
import ml_dtypes

import concourse.bass as bass
import concourse.mybir as mybir
from concourse.bass_utils import run_bass_kernel_spmd

F32 = mybir.dt.float32
BF16 = mybir.dt.bfloat16
AF = mybir.ActivationFunctionType
ALU = mybir.AluOpType
AX = mybir.AxisListType
BF = ml_dtypes.bfloat16

NCORE = 8
DBG = {}
P7SUB = 99
TL = 2048
NTC = 16
D = 1024
DC = 8
E = 2048
EPS = 1e-5


def _pp(b):
    b = np.asarray(b, np.float32)
    return np.ascontiguousarray(b.reshape(b.size // 128, 128).T)


def _row(b):
    return np.ascontiguousarray(np.asarray(b, np.float32).reshape(1, -1))


def _rope_tables(core):
    tau = (core % 2) * 2048 + np.arange(2048, dtype=np.float64)
    inv = 1.0 / 10000.0 ** (np.arange(0, 64, 2, dtype=np.float64) / 64.0)
    fr = tau[:, None] * inv[None, :]
    sin, cos = np.sin(fr), np.cos(fr)
    p = np.arange(128)
    j = (p % 64) % 32
    sgn = np.where((p % 64) < 32, -1.0, 1.0)
    cos_t = cos[:, j].T
    sin_t = sin[:, j].T * sgn[:, None]
    return {"rope_cos": np.ascontiguousarray(cos_t.astype(BF)),
            "rope_sin": np.ascontiguousarray(sin_t.astype(BF))}


def _host_prep(I):
    x = np.ascontiguousarray(np.asarray(I["L0_attn_x"], np.float32).reshape(16384, D))
    shared = {
        "l0_wq": np.asarray(I["L0_attn_Wq"], np.float32),
        "l0_bq": _pp(I["L0_attn_bq"]),
        "l0_wdown": np.asarray(I["L0_attn_Wdown"], np.float32),
        "l0_bdown": _pp(I["L0_attn_bdown"]),
        "l0_wup": np.asarray(I["L0_attn_Wup"], np.float32),
        "l0_bup": _pp(I["L0_attn_bup"]),
        "l0_wout": np.asarray(I["L0_attn_Wout"], np.float32),
        "l0_bout": _row(I["L0_attn_bout"]),
        "phi_wT": np.ascontiguousarray(np.asarray(I["L0_ffn_phi_w"], np.float32).T),
        "phi_b": _row(I["L0_ffn_phi_b"]),
        "l1_wq": np.ascontiguousarray(np.asarray(I["L1_attn_Wq"], np.float32).T),
        "l1_bq": _pp(I["L1_attn_bq"]),
        "l1_wk": np.ascontiguousarray(np.asarray(I["L1_attn_Wk"], np.float32).T),
        "l1_bk": _pp(I["L1_attn_bk"]),
        "l1_wv": np.ascontiguousarray(np.asarray(I["L1_attn_Wv"], np.float32).T),
        "l1_bv": _row(I["L1_attn_bv"]),
        "l1_wo": np.ascontiguousarray(np.asarray(I["L1_attn_Wo"], np.float32).T),
        "l1_bo": _row(I["L1_attn_bo"]),
        "cn_w1": np.asarray(I["L1_ffn_cn_w1"], np.float32), "cn_b1": _pp(I["L1_ffn_cn_b1"]),
        "cn_w2": np.asarray(I["L1_ffn_cn_w2"], np.float32), "cn_b2": _pp(I["L1_ffn_cn_b2"]),
        "cn_w3": np.asarray(I["L1_ffn_cn_w3"], np.float32), "cn_b3": _pp(I["L1_ffn_cn_b3"]),
        "cn_w4": _pp(I["L1_ffn_cn_w4"]), "cn_b4": _row(I["L1_ffn_cn_b4"]),
        "gc_w1": np.asarray(I["L1_ffn_gc_w1"], np.float32), "gc_b1": _pp(I["L1_ffn_gc_b1"]),
        "gc_w2": np.asarray(I["L1_ffn_gc_w2"], np.float32), "gc_b2": _pp(I["L1_ffn_gc_b2"]),
        "gc_w3": _pp(I["L1_ffn_gc_w3"]), "gc_b3": _row(I["L1_ffn_gc_b3"]),
        "a_range": _row(I["L1_ffn_adaptive_range"]),
    }
    exp_w = np.asarray(I["L0_ffn_expert_w"], np.float32)
    exp_b = np.asarray(I["L0_ffn_expert_b"], np.float32)
    maps = []
    for c in range(NCORE):
        m = dict(shared)
        m["x"] = np.ascontiguousarray(x[TL * c: TL * (c + 1)])
        m["exp_w"] = np.ascontiguousarray(exp_w[8 * c: 8 * c + 8])
        m["exp_b"] = np.ascontiguousarray(exp_b[8 * c: 8 * c + 8])
        m.update(_rope_tables(c))
        maps.append(m)
    return maps



class Arena:
    def __init__(self, nc, stack, name, words, dtype):
        self.t = stack.enter_context(nc.sbuf_tensor(name, [128, words], dtype))
        self.free_list = [(0, words)]
        self.live = {}
        self.hiwater = 0

    def alloc(self, name, shape):
        n = int(np.prod(shape))
        for i, (off, sz) in enumerate(self.free_list):
            if sz >= n:
                if sz == n:
                    self.free_list.pop(i)
                else:
                    self.free_list[i] = (off + n, sz - n)
                self.live[name] = (off, tuple(shape))
                self.hiwater = max(self.hiwater, off + n)
                return self.view(name)
        raise MemoryError(f"arena {self.t.name}: no {n} for {name} free={self.free_list} "
                          f"live={self.live}")

    def view(self, name):
        off, shape = self.live[name]
        n = int(np.prod(shape))
        ap = self.t[:, off:off + n]
        if len(shape) > 1:
            spec = "p (" + " ".join(f"d{i}" for i in range(len(shape))) + ") -> p " + \
                   " ".join(f"d{i}" for i in range(len(shape)))
            ap = ap.rearrange(spec, **{f"d{i}": s for i, s in enumerate(shape)})
        return ap

    def alias(self, src_name, new_name, shape):
        off, _ = self.live[src_name]
        assert int(np.prod(shape)) <= int(np.prod(self.live[src_name][1]))
        self.live[new_name] = (off, tuple(shape))
        return self.view(new_name)

    def freea(self, name):
        off, shape = self.live.pop(name)
        self.free_list.append((off, int(np.prod(shape))))
        self.free_list.sort()
        merged = []
        for o, s in self.free_list:
            if merged and merged[-1][0] + merged[-1][1] == o:
                merged[-1] = (merged[-1][0], merged[-1][1] + s)
            else:
                merged.append((o, s))
        self.free_list = [tuple(m) for m in merged]


class Sem:
    def __init__(self, nc, stack, name):
        self.h = stack.enter_context(nc.semaphore(name))
        self.v = 0

    def inc(self, instr, n=1):
        instr.then_inc(self.h, n)
        self.v += n
        return self.v


def build(taps=(), upto=99, races=True):
    nc = bass.Bass("TRN2", target_bir_lowering=False, detect_race_conditions=races)
    taps = set(taps)

    x_d = nc.dram_tensor("x", [TL, D], F32, kind="ExternalInput")
    out_d = nc.dram_tensor("out", [TL, D], F32, kind="ExternalOutput")

    def din(name, shape):
        return nc.dram_tensor(name, list(shape), F32, kind="ExternalInput")

    w_l0q = din("l0_wq", (D, D)); b_l0q = din("l0_bq", (128, 8))
    w_l0d = din("l0_wdown", (D, 256)); b_l0d = din("l0_bdown", (128, 2))
    w_l0u = din("l0_wup", (256, E)); b_l0u = din("l0_bup", (128, 16))
    w_l0o = din("l0_wout", (D, D)); b_l0o = din("l0_bout", (1, D))
    w_phi = din("phi_wT", (D, E)); b_phi = din("phi_b", (1, E))
    w_expw = din("exp_w", (8, D, D)); w_expb = din("exp_b", (8, D))
    w_l1q = din("l1_wq", (D, D)); b_l1q = din("l1_bq", (128, 8))
    w_l1k = din("l1_wk", (D, 256)); b_l1k = din("l1_bk", (128, 2))
    w_l1v = din("l1_wv", (D, 256)); b_l1v = din("l1_bv", (1, 256))
    w_l1o = din("l1_wo", (D, D)); b_l1o = din("l1_bo", (1, D))
    w_cn1 = din("cn_w1", (D, D)); b_cn1 = din("cn_b1", (128, 8))
    w_cn2 = din("cn_w2", (D, 512)); b_cn2 = din("cn_b2", (128, 4))
    w_cn3 = din("cn_w3", (512, 256)); b_cn3 = din("cn_b3", (128, 2))
    w_cn4 = din("cn_w4", (128, 2)); b_cn4 = din("cn_b4", (1, 1))
    w_gc1 = din("gc_w1", (D, 512)); b_gc1 = din("gc_b1", (128, 4))
    w_gc2 = din("gc_w2", (512, 256)); b_gc2 = din("gc_b2", (128, 2))
    w_gc3 = din("gc_w3", (128, 2)); b_gc3 = din("gc_b3", (1, 1))
    a_rng = din("a_range", (1, 1))
    rcos_d = nc.dram_tensor("rope_cos", [128, TL], BF16, kind="ExternalInput")
    rsin_d = nc.dram_tensor("rope_sin", [128, TL], BF16, kind="ExternalInput")

    ident_d = nc.inline_tensor(np.eye(128, dtype=BF), "ident")
    r_t = np.arange(128) % 8
    bd = (r_t[:, None] == r_t[None, :]).astype(BF)
    mask_d = nc.inline_tensor(np.ascontiguousarray(bd), "bdmask")
    ones_row_d = nc.inline_tensor(np.ones((1, 128), dtype=BF), "onesrow")
    _pj = np.arange(128)
    _j64 = _pj % 64
    _h64 = (_pj // 64) * 64
    _src = np.where(_j64 < 32, _h64 + 2 * _j64 + 1, _h64 + 2 * (_j64 - 32))
    _pm = np.zeros((128, 128), dtype=BF)
    _pm[_src, _pj] = 1.0
    pmat_d = nc.inline_tensor(np.ascontiguousarray(_pm), "permmat")

    c_dram = nc.dram_tensor("c_scr", [TL, E], BF16)
    d_dram = nc.dram_tensor("d_scr", [TL, E], BF16)
    xpart_dram = nc.dram_tensor("xpart_scr", [E, D], F32)
    xown_dram = nc.dram_tensor("xown_scr", [E // 8, D], F32)
    yown_dram = nc.dram_tensor("yown_scr", [E // 8, D], BF16)
    ytil_dram = nc.dram_tensor("ytil_scr", [E, D], BF16)
    k_ag_in = nc.dram_tensor("k_ag_in", [256, 2048], BF16)
    k_ag_out = nc.dram_tensor("k_ag_out", [512, 2048], BF16)
    v_ag_in = nc.dram_tensor("v_ag_in", [2048, 256], BF16)
    v_ag_out = nc.dram_tensor("v_ag_out", [4096, 256], BF16)
    gm_ag_in = nc.dram_tensor("gm_ag_in", [1, D], F32)
    gm_ag_out = nc.dram_tensor("gm_ag_out", [2, D], F32)

    tap_outs = {}
    stack = ExitStack()
    with stack:
        ab = Arena(nc, stack, "AB", 62 * 1024, BF16)
        af = Arena(nc, stack, "AF", 4 * 1024, F32)
        xs_t = stack.enter_context(nc.sbuf_tensor("XS", [128, NTC, D], F32))
        XS = xs_t[:, :, :]

        ps_f = stack.enter_context(nc.psum_tensor("PSF", [128, 2048], F32))
        ps_g = stack.enter_context(nc.psum_tensor("PSG", [128, 1024], F32))
        pst_stack = ExitStack()
        ps_t = pst_stack.enter_context(nc.psum_tensor("PST", [128, 2048], BF16))

        sld = Sem(nc, stack, "sld")
        sl2 = Sem(nc, stack, "sl2")
        sl3 = Sem(nc, stack, "sl3")
        smm = Sem(nc, stack, "smm")
        ssv = Sem(nc, stack, "ssv")
        sss = Sem(nc, stack, "sss")
        ssg = Sem(nc, stack, "ssg")
        sst = Sem(nc, stack, "sst")
        sdt = [Sem(nc, stack, f"sdt{i}") for i in range(4)]

        ident = ab.alloc("ident", (128,))
        maskbd = ab.alloc("maskbd", (128,))
        onesrow = ab.alloc("onesrow", (128,))
        rcos = ab.alloc("rcos", (TL,))
        rsin = ab.alloc("rsin", (TL,))

        stats = af.alloc("stats", (16, 16))
        fscr = af.alloc("fscr", (1024,))

        def tap(name, ap, shape, engine, waits=()):
            if name not in taps:
                return
            t_d = nc.dram_tensor("tap_" + name, list(shape), ap.dtype, kind="ExternalOutput")
            tap_outs[name] = t_d
            for s, v in waits:
                engine.wait_ge(s.h, v)
            sst.inc(engine.dma_start(out=t_d.ap(), in_=ap), 16)

        with nc.Block() as blk:
            @blk.sync
            def _(sy):
                sl2.inc(sy.dma_start(out=XS, in_=x_d.ap().rearrange("(c p) f -> p c f", p=128)), 16)

            @blk.gpsimd
            def _(g):
                for dst, src in ((ident, ident_d[:, :]), (maskbd, mask_d[:, :]),
                                 (onesrow[0:1, :], ones_row_d[:, :]),
                                 (rcos, rcos_d[:, :]), (rsin, rsin_d[:, :])):
                    sld.inc(g.dma_start(out=dst, in_=src), 16)
        consts_ld = sld.v

        def emit_ln_stats(xdep=None):
            with nc.Block() as blk:
                sv0, ss0 = ssv.v, sss.v

                @blk.vector
                def _(v):
                    v.wait_ge(sl2.h, 16)
                    if xdep is not None:
                        v.wait_ge(xdep[0].h, xdep[1])
                    st = stats
                    for tc in range(NTC):
                        v.tensor_scalar(fscr, XS[:, tc, :], 1.0, 0.0, ALU.mult,
                                        ALU.add, accum_out=st[:, 0, tc:tc + 1])
                        v.drain()
                        v.scalar_tensor_tensor(fscr, XS[:, tc, :], 1.0, XS[:, tc, :],
                                               ALU.mult, ALU.mult,
                                               accum_out=st[:, 1, tc:tc + 1])
                        v.drain()
                    v.tensor_scalar_mul(st[:, 2, :], st[:, 0, :], 1.0 / D)
                    v.drain()
                    v.scalar_tensor_tensor(st[:, 3, :], st[:, 2, :], 1.0, st[:, 2, :],
                                           ALU.mult, ALU.mult)
                    v.tensor_scalar_mul(st[:, 4, :], st[:, 1, :], 1.0 / D)
                    v.drain()
                    v.tensor_sub(st[:, 5, :], st[:, 4, :], st[:, 3, :])
                    v.drain()
                    v.tensor_scalar_add(st[:, 5, :], st[:, 5, :], EPS)
                    v.drain()
                    ssv.inc(v.reciprocal(st[:, 6, :], st[:, 5, :]))
                    v.wait_ge(sss.h, ss0 + 1)
                    v.tensor_mul(st[:, 8, :], st[:, 2, :], st[:, 7, :])
                    v.drain()
                    ssv.inc(v.tensor_scalar_mul(st[:, 9, :], st[:, 8, :], -1.0))

                @blk.scalar
                def _(s):
                    s.wait_ge(ssv.h, sv0 + 1)
                    sss.inc(s.activation(stats[:, 7, :], stats[:, 6, :], AF.Sqrt))
            return ssv.v

        def emit_xln_slice(tc0, ntc, tokbuf, fmbuf, stats_dep, fm_off=0):
            with nc.Block() as blk:
                sv0, mm0 = ssv.v, smm.v

                @blk.vector
                def _(v):
                    v.wait_ge(ssv.h, stats_dep)
                    for i in range(ntc):
                        tc = tc0 + i
                        ssv.inc(v.tensor_scalar(tokbuf[:, i, :], XS[:, tc, :],
                                                stats[:, 7, tc:tc + 1], stats[:, 9, tc:tc + 1],
                                                ALU.mult, ALU.add))
                    for i in range(ntc):
                        v.wait_ge(smm.h, mm0 + i + 1)
                        bank = (i % 2) * 1024
                        ssv.inc(v.tensor_copy(
                            fmbuf[:, :, fm_off + i * 128:fm_off + (i + 1) * 128],
                            ps_t[:, bank:bank + DC * 128].rearrange("p (f q) -> p f q", q=128)))

                @blk.tensor
                def _(t):
                    t.wait_ge(sld.h, sld.v)
                    t.wait_ge(ssv.h, sv0)
                    for i in range(ntc):
                        t.wait_ge(ssv.h, sv0 + i + 1)
                        if i >= 2:
                            t.wait_ge(ssv.h, sv0 + ntc + i - 1)
                        bank = (i % 2) * 1024
                        for fc in range(DC):
                            ins = t.transpose(ps_t[:, bank + fc * 128: bank + (fc + 1) * 128],
                                              tokbuf[:, i, fc * 128:(fc + 1) * 128], ident)
                        smm.inc(ins)
            return ssv.v

        def emit_l0_attn(stats_dep):
            TS = 256
            NSL = TL // TS
            wq_sb = ab.alloc("wq_sb", (DC, D))
            wdn_sb = ab.alloc("wdn_sb", (DC, 256))
            wup_sb = ab.alloc("wup_sb", (2, E))
            wout_sb = ab.alloc("wout_sb", (DC, D))
            boutr = ab.alloc("boutr", (D,))
            bq_f = af.alloc("bq_f", (8,))
            bq8_f = af.alloc("bq8_f", (8,))
            bdn_f = af.alloc("bdn_f", (2,))
            bup_f = af.alloc("bup_f", (16,))
            with nc.Block() as blk:
                sd0 = sl2.v

                @blk.gpsimd
                def _(g):
                    sld.inc(g.dma_start(out=wq_sb, in_=w_l0q.ap().rearrange("(i p) o -> p i o", p=128)), 16)
                    sld.inc(g.dma_start(out=wdn_sb, in_=w_l0d.ap().rearrange("(i p) o -> p i o", p=128)), 16)
                    sld.inc(g.dma_start(out=wup_sb, in_=w_l0u.ap().rearrange("(i p) o -> p i o", p=128)), 16)
                    sld.inc(g.dma_start(out=wout_sb, in_=w_l0o.ap().rearrange("(i p) o -> p i o", p=128)), 16)
                    sld.inc(g.dma_start(out=boutr[0:1, :], in_=b_l0o[:, :]), 16)

                @blk.sync
                def _(sy):
                    sl2.inc(sy.dma_start(out=bq_f, in_=b_l0q[:, :]), 16)
                    sl2.inc(sy.dma_start(out=bdn_f, in_=b_l0d[:, :]), 16)
                    sl2.inc(sy.dma_start(out=bup_f, in_=b_l0u[:, :]), 16)

                @blk.vector
                def _(v):
                    v.wait_ge(sl2.h, sd0 + 48)
                    ssv.inc(v.tensor_scalar_mul(bq8_f, bq_f, 0.125))
            wld, bld, b8 = sld.v, sl2.v, ssv.v

            xtok_sl = ab.alloc("xtok_sl", (2, D))
            xlnT_sl = ab.alloc("xlnT_sl", (DC, TS))
            q_fm = ab.alloc("q_fm", (DC, TS))
            kv_fm = ab.alloc("kv_fm", (16, TS))
            lat_fm = ab.alloc("lat_fm", (2, TS))
            rt1 = ab.alloc("rt1", (16, TS))
            rt2 = ab.alloc("rt2", (16, TS))
            bufA = ab.alloc("bufA", (2, 1024))
            bufB = ab.alloc("bufB", (2, 1024))
            ptb = ab.alloc("ptb", (2, 1024))
            vgb = ab.alloc("vgb", (2, 8, 64))
            odmj = ab.alloc("odmj", (2, 1024))
            o_fm = ab.alloc("o_fm", (DC, TS))
            submr = af.alloc("submr", (2, 1024))
            mrow = af.alloc("mrow", (2, 8))
            zrow = af.alloc("zrow", (2, 8))
            zrec = af.alloc("zrec", (2, 8))
            q_grp = ab.alloc("q_grp", (32, 64))
            kg2 = ab.alias("rt1", "kg2", (32, 128))
            vgrpb = ab.alias("rt2", "vgrpb", (32, 128))

            for sl in range(NSL):
                t0 = sl * TS
                emit_xln_slice(sl * 2, 2, ab.view("xtok_sl"), ab.view("xlnT_sl"), stats_dep)
                xlnT_v = ab.view("xlnT_sl")

                with nc.Block() as blk:
                    mm0, sv0 = smm.v, ssv.v

                    @blk.tensor
                    def _(t):
                        t.wait_ge(sld.h, wld)
                        t.wait_ge(ssv.h, sv0)
                        for oc in range(DC):
                            if oc >= 4:
                                t.wait_ge(ssv.h, sv0 + (oc - 4) + 1)
                            pb = ps_f[:, (oc % 4) * 512:(oc % 4) * 512 + TS]
                            for ic in range(DC):
                                ins = t.matmul(pb, wq_sb[:, ic, oc * 128:(oc + 1) * 128],
                                               xlnT_v[:, ic, :],
                                               start=(ic == 0), stop=(ic == DC - 1))
                            smm.inc(ins)
                        for oc in range(2):
                            pb = ps_g[:, oc * 512: oc * 512 + TS]
                            for ic in range(DC):
                                ins = t.matmul(pb, wdn_sb[:, ic, oc * 128:(oc + 1) * 128],
                                               xlnT_v[:, ic, :],
                                               start=(ic == 0), stop=(ic == DC - 1))
                            smm.inc(ins)
                        t.wait_ge(ssv.h, sv0 + 10)
                        for oc in range(16):
                            if oc >= 4:
                                t.wait_ge(ssv.h, sv0 + 10 + (oc - 4) + 1)
                            pb = ps_f[:, (oc % 4) * 512:(oc % 4) * 512 + TS]
                            for ic in range(2):
                                ins = t.matmul(pb, wup_sb[:, ic, oc * 128:(oc + 1) * 128],
                                               lat_fm[:, ic, :],
                                               start=(ic == 0), stop=(ic == 1))
                            smm.inc(ins)

                    @blk.vector
                    def _(v):
                        v.wait_ge(sl2.h, bld)
                        v.wait_ge(ssv.h, b8)
                        for oc in range(DC):
                            v.wait_ge(smm.h, mm0 + oc + 1)
                            pb = ps_f[:, (oc % 4) * 512:(oc % 4) * 512 + TS]
                            ssv.inc(v.tensor_scalar(q_fm[:, oc, :], pb, 0.125,
                                                    bq8_f[:, oc:oc + 1], ALU.mult, ALU.add))
                        for oc in range(2):
                            v.wait_ge(smm.h, mm0 + 8 + oc + 1)
                            pb = ps_g[:, oc * 512: oc * 512 + TS]
                            ssv.inc(v.tensor_scalar(lat_fm[:, oc, :], pb,
                                                    bdn_f[:, oc:oc + 1], 0.0, ALU.add, ALU.add))
                        for oc in range(16):
                            v.wait_ge(smm.h, mm0 + 10 + oc + 1)
                            pb = ps_f[:, (oc % 4) * 512:(oc % 4) * 512 + TS]
                            ssv.inc(v.tensor_scalar(kv_fm[:, oc, :], pb,
                                                    bup_f[:, oc:oc + 1], 0.0, ALU.add, ALU.add))

                with nc.Block() as blk:
                    sv0, sd0 = ssv.v, sl2.v

                    @blk.sync
                    def _(sy):
                        sy.wait_ge(ssv.h, sv0)
                        for dst0, src0 in ((0, 32), (32, 0), (64, 96), (96, 64)):
                            sl2.inc(sy.dma_start(out=rt1[dst0:dst0 + 32, 0:8, :],
                                                 in_=q_fm[src0:src0 + 32, :, :]), 16)
                        for dst0, src0 in ((0, 32), (32, 0)):
                            sl2.inc(sy.dma_start(out=rt2[dst0:dst0 + 32, :, :],
                                                 in_=kv_fm[src0:src0 + 32, :, :]), 16)

                    @blk.vector
                    def _(v):
                        cq = rcos[:, t0:t0 + TS].unsqueeze(1)
                        sq = rsin[:, t0:t0 + TS].unsqueeze(1)
                        ck = rcos[0:64, t0:t0 + TS].unsqueeze(1)
                        sk = rsin[0:64, t0:t0 + TS].unsqueeze(1)
                        v.wait_ge(sl2.h, sd0 + 96)
                        v.tensor_mul(rt1[:, 8:16, :], rt1[:, 0:8, :],
                                     sq.broadcast_to((128, 8, TS)))
                        v.drain()
                        v.tensor_mul(rt1[:, 0:8, :], q_fm, cq.broadcast_to((128, 8, TS)))
                        v.drain()
                        ssv.inc(v.tensor_add(q_fm, rt1[:, 0:8, :], rt1[:, 8:16, :]))
                        v.drain()
                        v.tensor_mul(rt1[0:64, :, :], rt2[0:64, :, :],
                                     sk.broadcast_to((64, 16, TS)))
                        v.drain()
                        v.tensor_mul(rt2[0:64, :, :], kv_fm[0:64, :, :],
                                     ck.broadcast_to((64, 16, TS)))
                        v.drain()
                        ssv.inc(v.tensor_add(kv_fm[0:64, :, :], rt2[0:64, :, :],
                                             rt1[0:64, :, :]))
                        v.drain()
                        v.tensor_copy(q_grp[0:64, :, :].rearrange("p g (c t) -> p g c t", t=8),
                                      q_fm[0:64, :, :].rearrange("p c (g t) -> p g c t", t=8))
                        v.tensor_copy(q_grp[64:128, :, :].rearrange("p g (c t) -> p g c t", t=8),
                                      q_fm[64:128, :, :].rearrange("p c (g t) -> p g c t", t=8))
                        ssv.inc(v.tensor_copy(
                            kg2[0:64, :, :].rearrange("p g (gg t) -> p g gg t", t=8),
                            kv_fm[0:64, :, :].rearrange("p gg (g t) -> p g gg t", t=8)))
                        ssv.inc(v.tensor_copy(
                            vgrpb[0:64, :, :].rearrange("p g (gg t) -> p g gg t", t=8),
                            kv_fm[64:128, :, :].rearrange("p gg (g t) -> p g gg t", t=8)))

                    @blk.gpsimd
                    def _(g):
                        g.wait_ge(ssv.h, sv0 + 3)
                        sld.inc(g.dma_start(out=kg2[64:128, :, :], in_=kg2[0:64, :, :]), 16)

                rope_sv, rope_ld = ssv.v, sld.v

                NB = 4
                with nc.Block() as blk:
                    mm0, sv0, ss0 = smm.v, ssv.v, sss.v

                    @blk.tensor
                    def _(t):
                        t.wait_ge(ssv.h, rope_sv)
                        t.wait_ge(sld.h, rope_ld)
                        for b in range(NB):
                            b2 = b % 2
                            if b >= 2:
                                t.wait_ge(ssv.h, sv0 + 5 * (b - 2) + 1)
                            for j in range(8):
                                grp = b * 8 + j
                                pb = ps_f[:, b2 * 1024 + j * 128: b2 * 1024 + (j + 1) * 128]
                                t.matmul(pb[0:64, :], q_grp[0:64, grp, :],
                                         kg2[0:64, grp, :],
                                         start=True, stop=True, tile_position=(0, 0))
                                ins = t.matmul(pb[64:128, :], q_grp[64:128, grp, :],
                                               kg2[64:128, grp, :],
                                               start=True, stop=True, tile_position=(64, 64))
                            smm.inc(ins)
                            t.wait_ge(ssv.h, sv0 + 5 * b + 2)
                            if b >= 1:
                                t.wait_ge(ssv.h, sv0 + 5 * (b - 1) + 3)
                            for j in range(8):
                                ins = t.transpose(ps_t[:, j * 128:(j + 1) * 128],
                                                  bufA[:, b2, j * 128:(j + 1) * 128], ident)
                            smm.inc(ins)
                            if b >= 1:
                                t.wait_ge(ssv.h, sv0 + 5 * (b - 1) + 4)
                            for j in range(8):
                                grp = b * 8 + j
                                ins = t.transpose(ps_t[:, 1024 + j * 64: 1024 + (j + 1) * 64],
                                                  vgrpb[0:64, grp, :],
                                                  ident[0:64, 0:64])
                            smm.inc(ins)
                            t.wait_ge(ssv.h, sv0 + 5 * b + 4)
                            if b >= 1:
                                t.wait_ge(ssv.h, sv0 + 5 * (b - 1) + 5)
                            for j in range(8):
                                ins = t.matmul(ps_g[0:64, j * 128:(j + 1) * 128],
                                               vgb[:, b2, j, :],
                                               ptb[:, b2, j * 128:(j + 1) * 128],
                                               start=True, stop=True)
                            smm.inc(ins)

                    @blk.vector
                    def _(v):
                        v.wait_ge(sld.h, rope_ld)
                        for b in range(NB):
                            b2 = b % 2
                            if b >= 2:
                                v.wait_ge(sss.h, ss0 + (b - 2) + 1)
                            v.wait_ge(smm.h, mm0 + 4 * b + 1)
                            for hh in range(2):
                                pbv = ps_f[:, b2 * 1024 + hh * 512: b2 * 1024 + (hh + 1) * 512]\
                                    .rearrange("p (g q) -> p g q", q=128)
                                v.reduce_max(mrow[:, b2, hh * 4:(hh + 1) * 4], pbv, axis=AX.X,
                                             negate=True)
                                v.drain()
                                ins = v.tensor_add(
                                    submr[:, b2, hh * 512:(hh + 1) * 512]
                                    .rearrange("p (g q) -> p g q", q=128), pbv,
                                    mrow[:, b2, hh * 4:(hh + 1) * 4].unsqueeze(2)
                                    .broadcast_to((128, 4, 128)))
                            ssv.inc(ins)
                            v.wait_ge(sss.h, ss0 + b + 1)
                            va = bufA[:, b2, :].rearrange("p (g q) -> p g q", q=128)
                            vb = bufB[:, b2, :].rearrange("p (g q) -> p g q", q=128)
                            v.tensor_mul(vb, va, maskbd.unsqueeze(1).broadcast_to((128, 8, 128)))
                            v.drain()
                            v.reduce_sum(zrow[:, b2, :], vb, axis=AX.X)
                            v.drain()
                            v.reciprocal(zrec[:, b2, :], zrow[:, b2, :])
                            v.drain()
                            ssv.inc(v.tensor_mul(va, vb, zrec[:, b2, :].unsqueeze(2)
                                                 .broadcast_to((128, 8, 128))))
                            v.wait_ge(smm.h, mm0 + 4 * b + 2)
                            ssv.inc(v.tensor_copy(ptb[:, b2, :], ps_t[:, 0:1024]))
                            v.wait_ge(smm.h, mm0 + 4 * b + 3)
                            ssv.inc(v.tensor_copy(
                                vgb[:, b2, :, :],
                                ps_t[:, 1024:1536].rearrange("p (j dd) -> p j dd", dd=64)))
                            v.wait_ge(smm.h, mm0 + 4 * b + 4)
                            bp = (b % 2) * 64
                            ssv.inc(v.tensor_copy(
                                odmj[bp:bp + 64, b // 2, :], ps_g[0:64, 0:1024]))

                    @blk.scalar
                    def _(s):
                        for b in range(NB):
                            b2 = b % 2
                            s.wait_ge(ssv.h, sv0 + 5 * b + 1)
                            if b >= 2:
                                s.wait_ge(smm.h, mm0 + 4 * (b - 2) + 2)
                            sss.inc(s.activation(bufA[:, b2, :], submr[:, b2, :], AF.Exp))

                att_sv = ssv.v

                with nc.Block() as blk:
                    mm0, sv0 = smm.v, ssv.v

                    @blk.vector
                    def _(v):
                        v.wait_ge(ssv.h, att_sv)
                        for h in range(16):
                            off = 64 * (h % 2) + (h // 2) * 8
                            hb = (h % 2) * 64
                            for bp in range(2):
                                src_ap = odmj[bp * 64:bp * 64 + 64, :, :].rearrange(
                                    "p bb (j r) -> p bb j r", r=128)[:, :, :, off:off + 8]
                                dst_ap = o_fm[hb:hb + 64, h // 2, :].rearrange(
                                    "p (bb q) -> p bb q", bb=2)[:, :, 64 * bp:64 * bp + 64].rearrange(
                                    "p bb (j t) -> p bb j t", t=8)
                                ssv.inc(v.tensor_copy(dst_ap, src_ap))
                        for ts_ in range(2):
                            v.wait_ge(smm.h, mm0 + ts_ + 1)
                            tci = sl * 2 + ts_
                            pg = ps_g[:, 0:1024] if ts_ == 0 else ps_f[:, 0:1024]
                            ssv.inc(v.tensor_add(XS[:, tci, :], XS[:, tci, :], pg))

                    @blk.tensor
                    def _(t):
                        t.wait_ge(ssv.h, sv0 + 32)
                        for ts_ in range(2):
                            pg = ps_g if ts_ == 0 else ps_f
                            for nn in range(2):
                                pb = pg[:, nn * 512:(nn + 1) * 512]
                                for ic in range(DC):
                                    t.matmul(pb, o_fm[:, ic, ts_ * 128:(ts_ + 1) * 128],
                                             wout_sb[:, ic, nn * 512:(nn + 1) * 512],
                                             start=(ic == 0), stop=False)
                                ins = t.matmul(pb, onesrow[0:1, 0:128],
                                               boutr[0:1, nn * 512:(nn + 1) * 512],
                                               start=False, stop=True)
                            smm.inc(ins)

            DBG["ab"] = dict(ab.live)
            DBG["af"] = dict(af.live)
            with nc.Block() as blk:
                @blk.sync
                def _(sy):
                    w = [(ssv, ssv.v), (smm, smm.v), (sss, sss.v)]
                    tap("p1_qfm", ab.view("q_fm"), (128, DC, 256), sy, waits=w)
                    tap("p1_kvfm", ab.view("kv_fm"), (128, 16, 256), sy, waits=w)
                    tap("p1_qgrp", ab.view("q_grp"), (128, 32, 64), sy, waits=w)
                    tap("p1_kg2", ab.view("kg2"), (128, 32, 128), sy, waits=w)
                    tap("p1_vgrpb", ab.view("vgrpb"), (128, 32, 128), sy, waits=w)
                    tap("p1_ofm", ab.view("o_fm"), (128, DC, 256), sy, waits=w)
                    tap("p1_odmj", ab.view("odmj"), (128, 2, 1024), sy, waits=w)
                    tap("p1_bufA", ab.view("bufA"), (128, 2, 1024), sy, waits=w)
            for nm in ("xtok_sl", "xlnT_sl", "q_fm", "kv_fm", "lat_fm", "rt1", "rt2",
                       "bufA", "bufB", "ptb", "vgb", "odmj", "o_fm", "q_grp",
                       "wq_sb", "wdn_sb", "wup_sb", "wout_sb", "boutr"):
                ab.freea(nm)
            ab.live.pop("kg2", None); ab.live.pop("vgrpb", None)
            for nm in ("bq_f", "bq8_f", "bdn_f", "bup_f", "submr", "mrow", "zrow", "zrec"):
                af.freea(nm)

        def emit_l0_ffn(stats_dep):
            xtok2 = ab.alloc("xtok2", (NTC, D))
            phi_sb = ab.alloc("phi_sb", (DC, E))
            phib_sb = ab.alloc("phib_sb", (E,))
            ebuf = ab.alloc("ebuf", (2, E))
            cstage = ab.alloc("cstage", (2, E))
            xlnT_tc = ab.alloc("xlnT_tc", (2, DC, 128))
            neg10 = af.alloc("neg10", (1,))
            sdrec = af.alloc("sdrec", (2, 64))
            screc = af.alloc("screc", (2, 32))

            with nc.Block() as blk:
                @blk.gpsimd
                def _(g):
                    sld.inc(g.dma_start(out=phi_sb, in_=w_phi.ap().rearrange("(i p) o -> p i o", p=128)), 16)
                    sld.inc(g.dma_start(out=phib_sb[0:1, :], in_=b_phi[:, :]), 16)

                @blk.vector
                def _(v):
                    v.wait_ge(ssv.h, stats_dep)
                    v.memset(neg10, -10.0)
                    for tc in range(NTC):
                        ssv.inc(v.tensor_scalar(xtok2[:, tc, :], XS[:, tc, :],
                                                stats[:, 7, tc:tc + 1], stats[:, 9, tc:tc + 1],
                                                ALU.mult, ALU.add))
            wld3 = sld.v
            xln2_done = ssv.v

            with nc.Block() as blk:
                mm0, sv0, ss0, sd0, sg0 = smm.v, ssv.v, sss.v, sl2.v, sld.v

                @blk.tensor
                def _(t):
                    t.wait_ge(sld.h, wld3)
                    t.wait_ge(ssv.h, xln2_done)
                    for tc in range(NTC):
                        tp = tc % 2
                        if tc >= 2:
                            t.wait_ge(ssv.h, sv0 + 3 * (tc - 2) + 1)
                        for fc in range(DC):
                            ins = t.transpose(ps_t[:, tp * 1024 + fc * 128: tp * 1024 + (fc + 1) * 128],
                                              xtok2[:, tc, fc * 128:(fc + 1) * 128], ident)
                        smm.inc(ins)
                        t.wait_ge(ssv.h, sv0 + 3 * tc + 1)
                        if tc >= 1:
                            t.wait_ge(sss.h, ss0 + tc)
                        for nn in range(4):
                            pb = ps_f[:, nn * 512:(nn + 1) * 512]
                            for ic in range(DC):
                                t.matmul(pb, xlnT_tc[:, tp, ic, :],
                                         phi_sb[:, ic, nn * 512:(nn + 1) * 512],
                                         start=(ic == 0), stop=False)
                            ins = t.matmul(pb, onesrow[0:1, 0:128],
                                           phib_sb[0:1, nn * 512:(nn + 1) * 512],
                                           start=False, stop=True)
                        smm.inc(ins)

                @blk.scalar
                def _(s):
                    for tc in range(NTC):
                        tp = tc % 2
                        s.wait_ge(smm.h, mm0 + 2 * tc + 2)
                        if tc >= 2:
                            s.wait_ge(sld.h, sg0 + 16 * (tc - 1))
                        sss.inc(s.activation(ebuf[:, tp, :], ps_f[:, 0:2048], AF.Exp,
                                             bias=neg10))

                @blk.vector
                def _(v):
                    for tc in range(NTC):
                        tp = tc % 2
                        v.wait_ge(smm.h, mm0 + 2 * tc + 1)
                        ssv.inc(v.tensor_copy(
                            xlnT_tc[:, tp, :, :],
                            ps_t[:, tp * 1024:tp * 1024 + 1024].rearrange("p (f q) -> p f q", q=128)))
                        v.wait_ge(sss.h, ss0 + tc + 1)
                        ev = ebuf[:, tp, :].rearrange("p (n q) -> p n q", q=32)
                        ev2 = ebuf[:, tp, :].rearrange("p (n q) -> p q n", q=32)
                        v.reduce_sum(sdrec[:, tp, :], ev, axis=AX.X)
                        v.reduce_sum(screc[:, tp, :], ev2, axis=AX.X)
                        v.drain()
                        v.reciprocal(sdrec[:, tp, :], sdrec[:, tp, :])
                        v.reciprocal(screc[:, tp, :], screc[:, tp, :])
                        if tc >= 2:
                            v.wait_ge(sl2.h, sd0 + 16 * (tc - 1))
                        v.drain()
                        ssv.inc(v.tensor_mul(
                            cstage[:, tp, :].rearrange("p (n q) -> p n q", q=32),
                            ev, screc[:, tp, :].unsqueeze(1).broadcast_to((128, 64, 32))))
                        v.drain()
                        ssv.inc(v.tensor_mul(
                            ev, ev, sdrec[:, tp, :].unsqueeze(2).broadcast_to((128, 64, 32))))

                @blk.sync
                def _(sy):
                    for tc in range(NTC):
                        tp = tc % 2
                        sy.wait_ge(ssv.h, sv0 + 3 * tc + 2)
                        sy.wait_ge(sl2.h, sl2.v)
                        sl2.inc(sy.dma_start(
                            out=c_dram.ap().rearrange("(c p) e -> p c e", p=128)[:, tc, :],
                            in_=cstage[:, tp, :]), 16)

                @blk.gpsimd
                def _(g):
                    for tc in range(NTC):
                        tp = tc % 2
                        g.wait_ge(ssv.h, sv0 + 3 * tc + 3)
                        g.wait_ge(sld.h, sld.v)
                        sld.inc(g.dma_start(
                            out=d_dram.ap().rearrange("(c p) e -> p c e", p=128)[:, tc, :],
                            in_=ebuf[:, tp, :]), 16)

            d_dmas = (sl2.v, sld.v)
            for nm in ("phi_sb", "phib_sb", "ebuf", "cstage", "xlnT_tc"):
                ab.freea(nm)

            dtile = ab.alloc("dtile", (4, 256))
            xstg = ab.alloc("xstg", (2, 2048))
            ab.freea("xstg")
            xstg = af.alloc("xstg", (2, 1024))
            with nc.Block() as blk:
                mm0, sv0, sd0, st0 = smm.v, ssv.v, sl2.v, sl3.v

                @blk.sync
                def _(sy):
                    sy.wait_ge(sld.h, d_dmas[1])
                    for ps in range(8):
                        for tc in range(NTC):
                            idx = ps * NTC + tc
                            if idx >= 4:
                                sy.wait_ge(smm.h, mm0 + (idx - 3))
                            sdt[idx % 4].inc(sy.dma_start(
                                out=dtile[:, idx % 4, :],
                                in_=d_dram[tc * 128:(tc + 1) * 128, ps * 256:(ps + 1) * 256]), 16)

                @blk.tensor
                def _(t):
                    for ps in range(8):
                        if ps >= 1:
                            t.wait_ge(ssv.h, sv0 + 2 * ps)
                        for tc in range(NTC):
                            idx = ps * NTC + tc
                            t.wait_ge(sdt[idx % 4].h, 16 * (idx // 4 + 1))
                            for ei in range(2):
                                for nn in range(2):
                                    ins = t.matmul(
                                        ps_f[:, (ei * 2 + nn) * 512:(ei * 2 + nn + 1) * 512],
                                        dtile[:, idx % 4, ei * 128:(ei + 1) * 128],
                                        xtok2[:, tc, nn * 512:(nn + 1) * 512],
                                        start=(tc == 0), stop=(tc == NTC - 1))
                            smm.inc(ins)

                @blk.vector
                def _(v):
                    for ps in range(8):
                        v.wait_ge(smm.h, mm0 + NTC * (ps + 1))
                        for ei in range(2):
                            if ps >= 1:
                                v.wait_ge(sl3.h, st0 + 16 * (2 * (ps - 1) + ei + 1))
                            ssv.inc(v.tensor_copy(xstg[:, ei, :],
                                                  ps_f[:, ei * 1024:(ei + 1) * 1024]))


                @blk.scalar
                def _(s):
                    for ps in range(8):
                        for ei in range(2):
                            s.wait_ge(ssv.h, sv0 + 2 * ps + ei + 1)
                            s.wait_ge(sl3.h, sl3.v)
                            sl3.inc(s.dma_start(
                                out=xpart_dram[(2 * ps + ei) * 128:(2 * ps + ei + 1) * 128, :],
                                in_=xstg[:, ei, :]), 16)

            xpart_done = sl3.v
            ab.freea("dtile"); ab.freea("xtok2")
            af.freea("xstg"); af.freea("neg10"); af.freea("sdrec"); af.freea("screc")

            xn = ab.alloc("xn", (2, D))
            xnT = ab.alloc("xnT", (DC, 256))
            expw = ab.alloc("expw", (2, DC, D))
            expb_sb = ab.alloc("expb_sb", (8 * D,))
            ystg = ab.alloc("ystg", (2, D))
            xow = af.alloc("xow", (2, D))
            xst = af.alloc("xst", (12, 2))

            with nc.Block() as blk:
                sg0, ld0 = ssg.v, sld.v

                @blk.gpsimd
                def _(g):
                    g.wait_ge(sl3.h, xpart_done)
                    ssg.inc(g.collective_compute(
                        "ReduceScatter", ALU.add, replica_groups=[list(range(NCORE))],
                        ins=[xpart_dram.ap().opt()], outs=[xown_dram.ap().opt()]))
                    g.wait_ge(ssg.h, sg0 + 1)
                    sld.inc(g.dma_start(out=xow,
                                        in_=xown_dram.ap().rearrange("(c p) f -> p c f", p=128)), 16)
                    sld.inc(g.dma_start(out=expb_sb[0:1, :],
                                        in_=w_expb.ap().rearrange("n h -> (n h)").unsqueeze(0)), 16)
                    g.wait_ge(sld.h, sld.v)
                    sld.inc(g.dma_start(out=expw[:, 0, :, :],
                                        in_=w_expw[0, :, :].rearrange("(i p) o -> p i o", p=128)), 16)
                    g.wait_ge(sld.h, sld.v)
                    sld.inc(g.dma_start(out=expw[:, 1, :, :],
                                        in_=w_expw[1, :, :].rearrange("(i p) o -> p i o", p=128)), 16)
            rs_ld = sld.v

            with nc.Block() as blk:
                sv0, ss0, mm0 = ssv.v, sss.v, smm.v

                @blk.vector
                def _(v):
                    v.wait_ge(sld.h, rs_ld)
                    v.tensor_scalar(xow, xow, 65000.0, -33000.0, ALU.min, ALU.max)
                    v.drain()
                    for i in range(2):
                        v.tensor_scalar(fscr, xow[:, i, :], 1.0, 0.0, ALU.mult,
                                        ALU.add, accum_out=xst[:, 0, i:i + 1])
                        v.drain()
                        v.scalar_tensor_tensor(fscr, xow[:, i, :], 1.0, xow[:, i, :],
                                               ALU.mult, ALU.mult,
                                               accum_out=xst[:, 1, i:i + 1])
                        v.drain()
                    v.tensor_scalar_mul(xst[:, 2, :], xst[:, 0, :], 1.0 / D)
                    v.drain()
                    v.scalar_tensor_tensor(xst[:, 3, :], xst[:, 2, :], 1.0, xst[:, 2, :],
                                           ALU.mult, ALU.mult)
                    v.tensor_scalar_mul(xst[:, 4, :], xst[:, 1, :], 1.0 / D)
                    v.drain()
                    v.tensor_sub(xst[:, 5, :], xst[:, 4, :], xst[:, 3, :])
                    v.drain()
                    v.tensor_scalar_add(xst[:, 5, :], xst[:, 5, :], EPS)
                    v.drain()
                    ssv.inc(v.reciprocal(xst[:, 6, :], xst[:, 5, :]))
                    v.wait_ge(sss.h, ss0 + 1)
                    v.tensor_mul(xst[:, 8, :], xst[:, 2, :], xst[:, 7, :])
                    v.drain()
                    v.tensor_scalar_mul(xst[:, 9, :], xst[:, 8, :], -1.0)
                    v.drain()
                    for i in range(2):
                        ssv.inc(v.tensor_scalar(xn[:, i, :], xow[:, i, :],
                                                xst[:, 7, i:i + 1], xst[:, 9, i:i + 1],
                                                ALU.mult, ALU.add))
                    for i in range(2):
                        v.wait_ge(smm.h, mm0 + i + 1)
                        ssv.inc(v.tensor_copy(
                            xnT[:, :, i * 128:(i + 1) * 128],
                            ps_t[:, (i % 2) * 1024:(i % 2) * 1024 + 1024]
                            .rearrange("p (f q) -> p f q", q=128)))

                @blk.scalar
                def _(s):
                    s.wait_ge(ssv.h, sv0 + 1)
                    sss.inc(s.activation(xst[:, 7, :], xst[:, 6, :], AF.Sqrt))
                    for i in range(2):
                        s.wait_ge(ssv.h, sv0 + 1 + i + 1)
                        sss.inc(s.activation(xn[:, i, :], xn[:, i, :], AF.Tanh))

                @blk.tensor
                def _(t):
                    for i in range(2):
                        t.wait_ge(sss.h, ss0 + 1 + i + 1)
                        for fc in range(DC):
                            ins = t.transpose(
                                ps_t[:, (i % 2) * 1024 + fc * 128:(i % 2) * 1024 + (fc + 1) * 128],
                                xn[:, i, fc * 128:(fc + 1) * 128], ident)
                        smm.inc(ins)
            xnt_done = ssv.v

            with nc.Block() as blk:
                mm0, sv0, st0, ld0 = smm.v, ssv.v, sl3.v, sld.v

                @blk.gpsimd
                def _(g):
                    for n in range(2, 8):
                        g.wait_ge(smm.h, mm0 + (n - 2) + 1)
                        g.wait_ge(sld.h, sld.v)
                        sld.inc(g.dma_start(out=expw[:, n % 2, :, :],
                                            in_=w_expw[n, :, :].rearrange("(i p) o -> p i o", p=128)), 16)

                @blk.tensor
                def _(t):
                    t.wait_ge(ssv.h, xnt_done)
                    for n in range(8):
                        t.wait_ge(sld.h, ld0 + 16 * max(0, n - 1) if n >= 2 else ld0)
                        if n >= 1:
                            t.wait_ge(ssv.h, sv0 + n)
                        for nn in range(2):
                            pb = ps_g[0:32, nn * 512:(nn + 1) * 512]
                            for ic in range(DC):
                                t.matmul(pb, xnT[:, ic, n * 32:(n + 1) * 32],
                                         expw[:, n % 2, ic, nn * 512:(nn + 1) * 512],
                                         start=(ic == 0), stop=False)
                            ins = t.matmul(pb, onesrow[0:1, 0:32],
                                           expb_sb[0:1, n * D + nn * 512: n * D + (nn + 1) * 512],
                                           start=False, stop=True)
                        smm.inc(ins)

                @blk.vector
                def _(v):
                    for n in range(8):
                        v.wait_ge(smm.h, mm0 + n + 1)
                        if n >= 2:
                            v.wait_ge(sl3.h, st0 + 16 * (n - 1))
                        ssv.inc(v.tensor_copy(ystg[0:32, n % 2, :], ps_g[0:32, 0:1024]))

                @blk.scalar
                def _(s):
                    for n in range(8):
                        s.wait_ge(ssv.h, sv0 + n + 1)
                        s.wait_ge(sl3.h, sl3.v)
                        sl3.inc(s.dma_start(out=yown_dram[n * 32:(n + 1) * 32, :],
                                            in_=ystg[0:32, n % 2, :]), 16)
            yown_done = sl3.v
            for nm in ("xn", "xnT", "expw", "expb_sb", "ystg"):
                ab.freea(nm)
            af.freea("xow"); af.freea("xst")

            ytil = ab.alloc("ytil", (NTC, D))
            ctt = ab.alloc("ctt", (NTC, E))
            with nc.Block() as blk:
                sg0, ld0, sd0 = ssg.v, sld.v, sl2.v

                @blk.gpsimd
                def _(g):
                    g.wait_ge(sl3.h, yown_done)
                    ssg.inc(g.collective_compute(
                        "AllGather", ALU.bypass, replica_groups=[list(range(NCORE))],
                        ins=[yown_dram.ap().opt()], outs=[ytil_dram.ap().opt()]))
                    g.wait_ge(ssg.h, sg0 + 1)
                    sld.inc(g.dma_start(out=ytil,
                                        in_=ytil_dram.ap().rearrange("(c p) f -> p c f", p=128)), 16)

                @blk.sync
                def _(sy):
                    sy.wait_ge(sl2.h, d_dmas[0])
                    sy.wait_ge(sl3.h, yown_done)
                    for ec in range(16):
                        sl2.inc(sy.dma_start_transpose(
                            out=ctt[:, ec, :], in_=c_dram[:, ec * 128:(ec + 1) * 128]), 16)
            yc_ld = (sld.v, sl2.v)

            with nc.Block() as blk:
                mm0, sv0 = smm.v, ssv.v

                @blk.tensor
                def _(t):
                    t.wait_ge(sld.h, yc_ld[0])
                    t.wait_ge(sl2.h, yc_ld[1])
                    for tc in range(NTC):
                        tp = tc % 2
                        if tc >= 2:
                            t.wait_ge(ssv.h, sv0 + (tc - 2) + 1)
                        for nn in range(2):
                            pb = ps_f[:, (tp * 2 + nn) * 512:(tp * 2 + nn + 1) * 512]
                            for ec in range(16):
                                ins = t.matmul(pb, ctt[:, ec, tc * 128:(tc + 1) * 128],
                                               ytil[:, ec, nn * 512:(nn + 1) * 512],
                                               start=(ec == 0), stop=(ec == 15))
                        smm.inc(ins)

                @blk.vector
                def _(v):
                    for tc in range(NTC):
                        tp = tc % 2
                        v.wait_ge(smm.h, mm0 + tc + 1)
                        ssv.inc(v.tensor_add(XS[:, tc, :], XS[:, tc, :],
                                             ps_f[:, tp * 1024:(tp + 1) * 1024]))

            ab.freea("ytil"); ab.freea("ctt")
            DBG["ab3"] = dict(ab.live)
            with nc.Block() as blk:
                @blk.sync
                def _(sy):
                    tap("stream_p3", XS, (128, NTC, D), sy, waits=[(ssv, ssv.v)])

        if upto >= 0:
            st_dep = emit_ln_stats()
        if upto >= 2:
            emit_l0_attn(st_dep)
            with nc.Block() as blk:
                @blk.sync
                def _(sy):
                    tap("stream_p1", XS, (128, NTC, D), sy, waits=[(ssv, ssv.v)])
        if upto >= 3:
            st_dep2 = emit_ln_stats(xdep=(ssv, ssv.v))
            emit_l0_ffn(st_dep2)



        def emit_l1_attn(pair):
            xlnT3 = ab.alloc("xlnT3", (DC, TL))
            xtok_s = ab.alloc("xtok_s", (2, D))
            st3 = emit_ln_stats(xdep=(ssv, ssv.v))
            for sl in range(8):
                emit_xln_slice(sl * 2, 2, ab.view("xtok_s"), ab.view("xlnT3"), st3,
                               fm_off=sl * 256)
            ab.freea("xtok_s")
            lnT3 = ssv.v

            wq1 = ab.alloc("wq1", (DC, D))
            q1 = ab.alloc("q1", (DC, TL))
            bq1 = af.alloc("bq1", (8,))
            bq81 = af.alloc("bq81", (8,))
            with nc.Block() as blk:
                @blk.gpsimd
                def _(g):
                    g.wait_ge(sld.h, sld.v)
                    sld.inc(g.dma_start(out=wq1, in_=w_l1q.ap().rearrange("(i p) o -> p i o", p=128)), 16)

                @blk.sync
                def _(sy):
                    sy.wait_ge(sl2.h, sl2.v)
                    sl2.inc(sy.dma_start(out=bq1, in_=b_l1q[:, :]), 16)
            wq_ld, bq_ld = sld.v, sl2.v
            with nc.Block() as blk:
                mm0, sv0 = smm.v, ssv.v

                @blk.tensor
                def _(t):
                    t.wait_ge(sld.h, wq_ld)
                    t.wait_ge(ssv.h, lnT3)
                    for oc in range(DC):
                        for qt in range(4):
                            idx = oc * 4 + qt
                            if idx >= 4:
                                t.wait_ge(ssv.h, sv0 + idx - 3)
                            pb = ps_f[:, (idx % 4) * 512:(idx % 4) * 512 + 512]
                            for ic in range(DC):
                                ins = t.matmul(pb, wq1[:, ic, oc * 128:(oc + 1) * 128],
                                               xlnT3[:, ic, qt * 512:(qt + 1) * 512],
                                               start=(ic == 0), stop=(ic == DC - 1))
                            smm.inc(ins)

                @blk.vector
                def _(v):
                    v.wait_ge(sl2.h, bq_ld)
                    v.tensor_scalar_mul(bq81, bq1, 0.125)
                    v.drain()
                    for oc in range(DC):
                        for qt in range(4):
                            idx = oc * 4 + qt
                            v.wait_ge(smm.h, mm0 + idx + 1)
                            pb = ps_f[:, (idx % 4) * 512:(idx % 4) * 512 + 512]
                            ssv.inc(v.tensor_scalar(q1[:, oc, qt * 512:(qt + 1) * 512], pb,
                                                    0.125, bq81[:, oc:oc + 1],
                                                    ALU.mult, ALU.add))
            ab.freea("wq1")

            wk1 = ab.alloc("wk1", (DC, 256))
            wv1 = ab.alloc("wv1", (DC, 256))
            k1 = ab.alloc("k1", (2, TL))
            vtok = ab.alloc("vtok", (NTC, 256))
            bvr = ab.alloc("bvr", (256,))
            bk1 = af.alloc("bk1", (2,))
            with nc.Block() as blk:
                @blk.gpsimd
                def _(g):
                    g.wait_ge(sld.h, sld.v)
                    sld.inc(g.dma_start(out=wk1, in_=w_l1k.ap().rearrange("(i p) o -> p i o", p=128)), 16)
                    g.wait_ge(sld.h, sld.v)
                    sld.inc(g.dma_start(out=wv1, in_=w_l1v.ap().rearrange("(i p) o -> p i o", p=128)), 16)
                    g.wait_ge(sld.h, sld.v)
                    sld.inc(g.dma_start(out=bvr[0:1, :], in_=b_l1v[:, :]), 16)

                @blk.sync
                def _(sy):
                    sy.wait_ge(sl2.h, sl2.v)
                    sl2.inc(sy.dma_start(out=bk1, in_=b_l1k[:, :]), 16)
            wkv_ld, bk_ld = sld.v, sl2.v
            with nc.Block() as blk:
                mm0, sv0 = smm.v, ssv.v

                @blk.tensor
                def _(t):
                    t.wait_ge(sld.h, wkv_ld)
                    for oc in range(2):
                        for qt in range(4):
                            idx = oc * 4 + qt
                            if idx >= 4:
                                t.wait_ge(ssv.h, sv0 + idx - 3)
                            pb = ps_f[:, (idx % 4) * 512:(idx % 4) * 512 + 512]
                            for ic in range(DC):
                                ins = t.matmul(pb, wk1[:, ic, oc * 128:(oc + 1) * 128],
                                               xlnT3[:, ic, qt * 512:(qt + 1) * 512],
                                               start=(ic == 0), stop=(ic == DC - 1))
                            smm.inc(ins)
                    t.wait_ge(ssv.h, sv0 + 8)
                    for tc in range(NTC):
                        if tc >= 2:
                            t.wait_ge(ssv.h, sv0 + 8 + tc - 1)
                        pb = ps_g[:, (tc % 2) * 512:(tc % 2) * 512 + 256]
                        for ic in range(DC):
                            t.matmul(pb, xlnT3[:, ic, tc * 128:(tc + 1) * 128],
                                     wv1[:, ic, :], start=(ic == 0), stop=False)
                        ins = t.matmul(pb, onesrow[0:1, 0:128], bvr[0:1, :],
                                       start=False, stop=True)
                        smm.inc(ins)

                @blk.vector
                def _(v):
                    v.wait_ge(sl2.h, bk_ld)
                    for oc in range(2):
                        for qt in range(4):
                            idx = oc * 4 + qt
                            v.wait_ge(smm.h, mm0 + idx + 1)
                            pb = ps_f[:, (idx % 4) * 512:(idx % 4) * 512 + 512]
                            ssv.inc(v.tensor_scalar(k1[:, oc, qt * 512:(qt + 1) * 512], pb,
                                                    bk1[:, oc:oc + 1], 0.0, ALU.add, ALU.add))
                    for tc in range(NTC):
                        v.wait_ge(smm.h, mm0 + 8 + tc + 1)
                        pb = ps_g[:, (tc % 2) * 512:(tc % 2) * 512 + 256]
                        ssv.inc(v.tensor_copy(vtok[:, tc, :], pb))
            ab.freea("wk1"); ab.freea("wv1"); ab.freea("xlnT3"); ab.freea("bvr")

            pmt = ab.alloc("pmt", (128,))
            rtmp = ab.alloc("rtmp", (512,))
            with nc.Block() as blk:
                @blk.gpsimd
                def _(g):
                    g.wait_ge(sld.h, sld.v)
                    sld.inc(g.dma_start(out=pmt, in_=pmat_d[:, :]), 16)
            pm_ld = sld.v
            rp_slices = [("q", oc, qt) for oc in range(DC) for qt in range(4)] + \
                        [("k", oc, qt) for oc in range(2) for qt in range(4)]
            with nc.Block() as blk:
                mm0, sv0 = smm.v, ssv.v

                @blk.tensor
                def _(t):
                    t.wait_ge(sld.h, pm_ld)
                    t.wait_ge(ssv.h, sv0)
                    for idx, (which, oc, qt) in enumerate(rp_slices):
                        if idx >= 4:
                            t.wait_ge(ssv.h, sv0 + idx - 3)
                        src_ = (q1 if which == "q" else k1)[:, oc, qt * 512:(qt + 1) * 512]
                        ins = t.matmul(ps_f[:, (idx % 4) * 512:(idx % 4) * 512 + 512],
                                       pmt, src_, start=True, stop=True)
                        smm.inc(ins)

                @blk.vector
                def _(v):
                    for idx, (which, oc, qt) in enumerate(rp_slices):
                        v.wait_ge(smm.h, mm0 + idx + 1)
                        sl_ = (q1 if which == "q" else k1)[:, oc, qt * 512:(qt + 1) * 512]
                        pb = ps_f[:, (idx % 4) * 512:(idx % 4) * 512 + 512]
                        v.tensor_mul(sl_, sl_, rcos[:, qt * 512:(qt + 1) * 512])
                        v.tensor_mul(rtmp, pb, rsin[:, qt * 512:(qt + 1) * 512])
                        v.drain()
                        ssv.inc(v.tensor_add(sl_, sl_, rtmp))
                        v.drain()
            rope_k = ssv.v
            ab.freea("pmt"); ab.freea("rtmp")
            ab.freea("rcos"); ab.freea("rsin")

            qbar = ab.alloc("qbar", (DC, 64))
            kbar0 = ab.alloc("kbar0", (2, 64))
            kbarg = ab.alloc("kbarg", (4, 64))
            m8b = ab.alloc("m8b", (8,))
            mg = ab.alloc("mg", (1024,))
            mr8 = ab.alloc("mr8", (8,))
            m8f = af.alloc("m8f", (8,))
            mr8f = af.alloc("mr8f", (8,))
            bias8 = af.alloc("bias8", (8,))
            with nc.Block() as blk:
                mm0, sv0, sd0 = smm.v, ssv.v, sl2.v

                @blk.vector
                def _(v):
                    v.wait_ge(ssv.h, rope_k)
                    with nc.allow_low_precision(reason="coarse softmax-shift estimate only"):
                        v.reduce_sum(qbar, q1.rearrange("p h (bb t) -> p h bb t", t=32),
                                     axis=AX.X)
                        ssv.inc(v.reduce_sum(kbar0,
                                             k1.rearrange("p h (bb t) -> p h bb t", t=32),
                                             axis=AX.X))
                    v.wait_ge(smm.h, mm0 + 1)
                    v.reduce_max(m8f, ps_g[:, 0:512].rearrange("p (h q) -> p h q", q=64),
                                 axis=AX.X)
                    v.drain()
                    ssv.inc(v.tensor_copy(m8b, m8f))
                    v.wait_ge(sl2.h, sd0 + 9 * 16)
                    v.reduce_max(mr8f[0:1, :], mg.rearrange("p (q h) -> p h q", h=8)[0:1, :, :],
                                 axis=AX.X)
                    v.drain()
                    ssv.inc(v.tensor_copy(mr8[0:1, :], mr8f[0:1, :]))
                    v.wait_ge(smm.h, mm0 + 2)
                    ssv.inc(v.tensor_scalar(bias8, ps_g[:, 512:520], -1.0 / 1024.0, -8.0,
                                            ALU.mult, ALU.add))

                @blk.sync
                def _(sy):
                    sy.wait_ge(ssv.h, sv0 + 1)
                    for g_ in range(4):
                        for half in range(2):
                            sy.wait_ge(sl2.h, sl2.v)
                            sl2.inc(sy.dma_start(
                                out=kbarg[half * 64:half * 64 + 64, g_, :],
                                in_=kbar0[(g_ % 2) * 64:(g_ % 2) * 64 + 64, g_ // 2, :]), 16)
                    sy.wait_ge(ssv.h, sv0 + 2)
                    sy.wait_ge(sl2.h, sl2.v)
                    sl2.inc(sy.dma_start(out=mg[0:1, :], in_=m8b[:, :]), 16)

                @blk.tensor
                def _(t):
                    t.wait_ge(sl2.h, sd0 + 8 * 16)
                    for hp in range(8):
                        g_ = hp // 2
                        t.matmul(ps_g[0:64, hp * 64:(hp + 1) * 64],
                                 qbar[0:64, hp, :], kbarg[0:64, g_, :],
                                 start=True, stop=True, tile_position=(0, 0))
                        ins = t.matmul(ps_g[64:128, hp * 64:(hp + 1) * 64],
                                       qbar[64:128, hp, :], kbarg[64:128, g_, :],
                                       start=True, stop=True, tile_position=(64, 64))
                    smm.inc(ins)
                    t.wait_ge(ssv.h, sv0 + 3)
                    ins = t.matmul(ps_g[:, 512:520], onesrow[0:1, 0:128], mr8[0:1, :],
                                   start=True, stop=True)
                    smm.inc(ins)
            bias_done = ssv.v

            with nc.Block() as blk:
                sg0, sd0 = ssg.v, sl2.v

                @blk.sync
                def _(sy):
                    sy.wait_ge(ssv.h, rope_k)
                    sy.wait_ge(sl2.h, sl2.v)
                    sl2.inc(sy.dma_start(out=k_ag_in.ap().rearrange("(c p) s -> p c s", p=128),
                                         in_=k1), 16)
                    sy.wait_ge(sl2.h, sl2.v)
                    sl2.inc(sy.dma_start(out=v_ag_in.ap().rearrange("(c p) f -> p c f", p=128),
                                         in_=vtok), 16)

                @blk.gpsimd
                def _(g):
                    g.wait_ge(sl2.h, sd0 + 32)
                    ssg.inc(g.collective_compute(
                        "AllGather", ALU.bypass, replica_groups=pair,
                        ins=[k_ag_in.ap().opt()], outs=[k_ag_out.ap().opt()]))
                    ssg.inc(g.collective_compute(
                        "AllGather", ALU.bypass, replica_groups=pair,
                        ins=[v_ag_in.ap().opt()], outs=[v_ag_out.ap().opt()]))
            ag_done = ssg.v
            ab.freea("vtok"); ab.freea("k1")
            for nm in ("qbar", "kbar0", "kbarg", "m8b", "mg", "mr8"):
                ab.freea(nm)

            pst_stack.close()
            psx_stack = ExitStack()
            ps_x = psx_stack.enter_context(nc.psum_tensor("PSX", [128, 1024], F32))

            kdup = ab.alloc("kdup", (2, 4096))
            vaug = ab.alloc("vaug", (2, 32, 65))
            bufP = ab.alloc("bufP", (2, 2, 1024))
            obuf = ab.alloc("obuf", (DC, TL))
            zfm = af.alloc("zfm", (2048,))
            g_ld = []

            with nc.Block() as blk:
                ld0 = sld.v

                @blk.gpsimd
                def _(g):
                    g.wait_ge(ssg.h, ag_done)
                    for g_ in range(2):
                        for half in range(2):
                            for bh in range(2):
                                g.wait_ge(sld.h, sld.v)
                                sld.inc(g.dma_start(
                                    out=kdup[half * 64:half * 64 + 64, g_ % 2,
                                             bh * 2048:(bh + 1) * 2048],
                                    in_=k_ag_out[bh * 256 + g_ * 64: bh * 256 + (g_ + 1) * 64, :]), 16)
                        g.wait_ge(sld.h, sld.v)
                        sld.inc(g.dma_start(
                            out=vaug[:, g_ % 2, :, 0:64],
                            in_=v_ag_out.ap().rearrange("(sc p) f -> p sc f", p=128)
                            [:, :, g_ * 64:(g_ + 1) * 64]), 16)
                        g_ld.append(sld.v)

                @blk.vector
                def _(v):
                    v.memset(vaug[:, 0, :, 64:65], 1.0)
                    ssv.inc(v.memset(vaug[:, 1, :, 64:65], 1.0))
            ones_set = ssv.v

            with nc.Block() as blk:
                mm0, sv0, ss0, zd0 = smm.v, ssv.v, sss.v, sl2.v
                gmark = [mm0 + (gg + 1) * 4 * 33 for gg in range(4)]

                @blk.gpsimd
                def _(g):
                    for g_ in range(2, 4):
                        g.wait_ge(smm.h, gmark[g_ - 2])
                        for half in range(2):
                            for bh in range(2):
                                g.wait_ge(sld.h, sld.v)
                                sld.inc(g.dma_start(
                                    out=kdup[half * 64:half * 64 + 64, g_ % 2,
                                             bh * 2048:(bh + 1) * 2048],
                                    in_=k_ag_out[bh * 256 + g_ * 64: bh * 256 + (g_ + 1) * 64, :]), 16)
                        g.wait_ge(sld.h, sld.v)
                        sld.inc(g.dma_start(
                            out=vaug[:, g_ % 2, :, 0:64],
                            in_=v_ag_out.ap().rearrange("(sc p) f -> p sc f", p=128)
                            [:, :, g_ * 64:(g_ + 1) * 64]), 16)
                        g_ld.append(sld.v)

                @blk.tensor
                def _(t):
                    t.wait_ge(ssv.h, ones_set)
                    t.wait_ge(ssv.h, bias_done)
                    for hp in range(8):
                        g_ = hp // 2
                        gp = g_ % 2
                        t.wait_ge(sld.h, g_ld[g_])
                        for qh in range(2):
                            it = hp * 2 + qh
                            base_t = mm0 + it * 33
                            base_s = ss0 + it * 64
                            if it >= 1:
                                t.wait_ge(ssv.h, sv0 + it * 4)
                            for sc in range(32):
                                if sc >= 1:
                                    t.wait_ge(sss.h, base_s + 2 * sc)
                                for hx in range(2):
                                    q_sl = q1[hx * 64:hx * 64 + 64, hp,
                                              qh * 1024:(qh + 1) * 1024]
                                    k_sl = kdup[hx * 64:hx * 64 + 64, gp,
                                                sc * 128:(sc + 1) * 128]
                                    for q2 in range(2):
                                        ins = t.matmul(
                                            ps_f[:, (hx * 2 + q2) * 512:(hx * 2 + q2 + 1) * 512],
                                            k_sl, q_sl[:, q2 * 512:(q2 + 1) * 512],
                                            start=True, stop=True,
                                            tile_position=(hx * 64, 0))
                                if sc >= 1:
                                    for hx in range(2):
                                        for q2 in range(2):
                                            pv = (ps_g if hx == 0 else ps_x)
                                            ins = t.matmul(
                                                pv[0:65, q2 * 512:(q2 + 1) * 512],
                                                vaug[:, gp, sc - 1, :],
                                                bufP[:, (sc - 1) % 2, hx, q2 * 512:(q2 + 1) * 512],
                                                start=(sc - 1 == 0), stop=False)
                                smm.inc(ins)
                            t.wait_ge(sss.h, base_s + 64)
                            for hx in range(2):
                                for q2 in range(2):
                                    pv = (ps_g if hx == 0 else ps_x)
                                    ins = t.matmul(pv[0:65, q2 * 512:(q2 + 1) * 512],
                                                   vaug[:, gp, 31, :],
                                                   bufP[:, 31 % 2, hx, q2 * 512:(q2 + 1) * 512],
                                                   start=False, stop=True)
                            smm.inc(ins)

                @blk.scalar
                def _(s):
                    for hp in range(8):
                        for qh in range(2):
                            it = hp * 2 + qh
                            base_t = mm0 + it * 33
                            base_s = ss0 + it * 64
                            for sc in range(32):
                                s.wait_ge(smm.h, base_t + sc + 1)
                                for hx in range(2):
                                    sss.inc(s.activation(
                                        bufP[:, sc % 2, hx, :],
                                        ps_f[:, hx * 1024:(hx + 1) * 1024], AF.Exp,
                                        bias=bias8[:, hp:hp + 1]))

                @blk.vector
                def _(v):
                    for hp in range(8):
                        for qh in range(2):
                            it = hp * 2 + qh
                            base_t = mm0 + it * 33
                            v.wait_ge(smm.h, base_t + 33)
                            if it >= 1:
                                v.wait_ge(sl2.h, zd0 + it * 32)
                            v.tensor_copy(zfm[64:65, 0:1024], ps_g[64:65, 0:1024])
                            v.tensor_copy(zfm[64:65, 1024:2048], ps_x[64:65, 0:1024])
                            for hx in range(2):
                                pv = (ps_g if hx == 0 else ps_x)
                                for q2 in range(2):
                                    col = qh * 1024 + q2 * 512
                                    ssv.inc(v.tensor_copy(
                                        obuf[hx * 64:hx * 64 + 64, hp, col:col + 512],
                                        pv[0:64, q2 * 512:(q2 + 1) * 512]))

                @blk.sync
                def _(sy):
                    for hp in range(8):
                        for qh in range(2):
                            it = hp * 2 + qh
                            sy.wait_ge(ssv.h, sv0 + it * 4 + 4)
                            for hx in range(2):
                                h = 2 * hp + hx
                                sy.wait_ge(sl2.h, sl2.v)
                                sl2.inc(sy.dma_start(
                                    out=zfm[h:h + 1, qh * 1024:(qh + 1) * 1024],
                                    in_=zfm[64:65, hx * 1024:(hx + 1) * 1024]), 16)
            att5 = ssv.v
            zfm_done = sl2.v
            ab.freea("kdup"); ab.freea("vaug"); ab.freea("bufP")

            wo1 = ab.alloc("wo1", (DC, D))
            bor = ab.alloc("bor", (D,))
            zrb = ab.alloc("zrb", (2048,))
            zr0 = ab.alloc("zr0", (2048,))
            with nc.Block() as blk:
                ld0 = sld.v

                @blk.gpsimd
                def _(g):
                    g.wait_ge(sld.h, sld.v)
                    sld.inc(g.dma_start(out=wo1, in_=w_l1o.ap().rearrange("(i p) o -> p i o", p=128)), 16)
                    g.wait_ge(sld.h, sld.v)
                    sld.inc(g.dma_start(out=bor[0:1, :], in_=b_l1o[:, :]), 16)
            wo_ld = sld.v

            with nc.Block() as blk:
                mm0, sv0, sd0 = smm.v, ssv.v, sl2.v

                @blk.vector
                def _(v):
                    v.wait_ge(ssv.h, att5)
                    v.wait_ge(sl2.h, zfm_done)
                    v.reciprocal(zfm[0:16, :], zfm[0:16, :])
                    v.drain()
                    ssv.inc(v.tensor_copy(zrb[0:16, :], zfm[0:16, :]))
                    for h in range(16):
                        v.wait_ge(smm.h, mm0 + h + 1)
                        hb = (h % 2) * 64
                        for qt in range(4):
                            pv = ps_g if qt < 2 else ps_x
                            ins = v.tensor_mul(
                                obuf[hb:hb + 64, h // 2, qt * 512:(qt + 1) * 512],
                                obuf[hb:hb + 64, h // 2, qt * 512:(qt + 1) * 512],
                                pv[hb:hb + 64, (qt % 2) * 512:(qt % 2) * 512 + 512])
                        ssv.inc(ins)

                @blk.sync
                def _(sy):
                    for h in range(16):
                        sy.wait_ge(ssv.h, sv0 + 1)
                        sy.wait_ge(sl2.h, sl2.v)
                        if h >= 1:
                            sy.wait_ge(smm.h, mm0 + h)
                        sl2.inc(sy.dma_start(out=zr0[0:1, :], in_=zrb[h:h + 1, :]), 16)

                @blk.tensor
                def _(t):
                    t.wait_ge(sld.h, wo_ld)
                    for h in range(16):
                        t.wait_ge(sl2.h, sd0 + 16 * (h + 1))
                        if h >= 1:
                            t.wait_ge(ssv.h, sv0 + h + 1)
                        hb = (h % 2) * 64
                        for qt in range(4):
                            pv = ps_g if qt < 2 else ps_x
                            ins = t.matmul(pv[hb:hb + 64, (qt % 2) * 512:(qt % 2) * 512 + 512],
                                           onesrow[0:1, 0:64],
                                           zr0[0:1, qt * 512:(qt + 1) * 512],
                                           start=True, stop=True,
                                           tile_position=(0, hb))
                        smm.inc(ins)
            zdone = ssv.v
            ab.freea("zrb")

            with nc.Block() as blk:
                mm0, sv0 = smm.v, ssv.v

                @blk.tensor
                def _(t):
                    t.wait_ge(ssv.h, zdone)
                    for tc in range(NTC):
                        tp = tc % 2
                        if tc >= 2:
                            t.wait_ge(ssv.h, sv0 + tc - 1)
                        for nn in range(2):
                            pb = ps_f[:, (tp * 2 + nn) * 512:(tp * 2 + nn + 1) * 512]
                            for ic in range(DC):
                                t.matmul(pb, obuf[:, ic, tc * 128:(tc + 1) * 128],
                                         wo1[:, ic, nn * 512:(nn + 1) * 512],
                                         start=(ic == 0), stop=False)
                            ins = t.matmul(pb, onesrow[0:1, 0:128],
                                           bor[0:1, nn * 512:(nn + 1) * 512],
                                           start=False, stop=True)
                        smm.inc(ins)

                @blk.vector
                def _(v):
                    for tc in range(NTC):
                        tp = tc % 2
                        v.wait_ge(smm.h, mm0 + tc + 1)
                        ssv.inc(v.tensor_add(XS[:, tc, :], XS[:, tc, :],
                                             ps_f[:, tp * 1024:(tp + 1) * 1024]))

            for nm in ("q1", "obuf", "wo1", "bor", "zr0"):
                ab.freea(nm)
            for nm in ("bq1", "bq81", "bk1", "m8f", "mr8f", "bias8", "zfm"):
                af.freea(nm)
            psx_stack.close()
            with nc.Block() as blk:
                @blk.sync
                def _(sy):
                    tap("stream_p5", XS, (128, NTC, D), sy, waits=[(ssv, ssv.v)])


        def emit_l1_ffn(pair):
            pst2 = stack.enter_context(nc.psum_tensor("PST2", [128, 2048], BF16))
            cn1 = ab.alloc("cn1", (DC, D))
            cn2 = ab.alloc("cn2", (DC, 512))
            cn3 = ab.alloc("cn3", (4, 256))
            cn4 = ab.alloc("cn4", (2,))
            gc1 = ab.alloc("gc1", (DC, 512))
            gc2 = ab.alloc("gc2", (4, 256))
            gc3 = ab.alloc("gc3", (2,))
            onescol = ab.alloc("onescol", (1,))
            gxr = ab.alloc("gxr", (1024,))
            gfm = ab.alloc("gfm", (8,))
            g1b = ab.alloc("g1b", (4,))
            g2b = ab.alloc("g2b", (2,))
            abrow = ab.alloc("abrow", (3,))
            xg = ab.alloc("xg", (2, D))
            cb1 = af.alloc("cb1", (8,)); cb2 = af.alloc("cb2", (4,)); cb3 = af.alloc("cb3", (2,))
            b4r = af.alloc("b4r", (1,)); gb1 = af.alloc("gb1", (4,)); gb2 = af.alloc("gb2", (2,))
            gb3 = af.alloc("gb3", (1,)); arr = af.alloc("arr", (1,))
            tiny = af.alloc("tiny", (8,))
            acbc = af.alloc("acbc", (3,))
            garow = af.alloc("garow", (1024,)); gbrow = af.alloc("gbrow", (1024,))

            with nc.Block() as blk:
                @blk.gpsimd
                def _(g):
                    for dst, srcd in ((cn1, w_cn1), (cn2, w_cn2), (cn3, w_cn3),
                                      (gc1, w_gc1), (gc2, w_gc2)):
                        g.wait_ge(sld.h, sld.v)
                        sld.inc(g.dma_start(out=dst,
                                            in_=srcd.ap().rearrange("(i p) o -> p i o", p=128)), 16)
                    for dst, srcd in ((cn4, w_cn4), (gc3, w_gc3)):
                        g.wait_ge(sld.h, sld.v)
                        sld.inc(g.dma_start(out=dst, in_=srcd[:, :]), 16)

                @blk.sync
                def _(sy):
                    for dst, srcd in ((cb1, b_cn1), (cb2, b_cn2), (cb3, b_cn3),
                                      (gb1, b_gc1), (gb2, b_gc2)):
                        sy.wait_ge(sl2.h, sl2.v)
                        sl2.inc(sy.dma_start(out=dst, in_=srcd[:, :]), 16)
                    for dst, srcd in ((b4r, b_cn4), (gb3, b_gc3), (arr, a_rng)):
                        sy.wait_ge(sl2.h, sl2.v)
                        sl2.inc(sy.dma_start(out=dst[0:1, :], in_=srcd[:, :]), 16)

                @blk.vector
                def _(v):
                    ssv.inc(v.memset(onescol, 1.0))
                    v.wait_ge(sld.h, sld.v)
                    v.tensor_scalar_mul(cn4, cn4, 0.5)
                    ssv.inc(v.tensor_scalar_mul(gc3, gc3, 0.5))
            w7_ld, b7_ld = sld.v, sl2.v
            st4 = emit_ln_stats(xdep=(ssv, ssv.v))

            with nc.Block() as blk:
                mm0, sv0, sd0 = smm.v, ssv.v, sl2.v

                @blk.vector
                def _(v):
                    v.wait_ge(ssv.h, st4)
                    for tc in range(NTC):
                        tp = tc % 2
                        if tc >= 2:
                            v.wait_ge(smm.h, mm0 + tc - 1)
                        ssv.inc(v.tensor_scalar(xg[:, tp, :], XS[:, tc, :],
                                                stats[:, 7, tc:tc + 1], stats[:, 9, tc:tc + 1],
                                                ALU.mult, ALU.add))
                    v.wait_ge(smm.h, mm0 + NTC)
                    ssv.inc(v.tensor_copy(garow[0:1, :], ps_g[0:1, 0:1024]))

                @blk.tensor
                def _(t):
                    for tc in range(NTC):
                        tp = tc % 2
                        t.wait_ge(ssv.h, sv0 + tc + 1)
                        for nn in range(2):
                            ins = t.matmul(ps_g[0:1, nn * 512:(nn + 1) * 512],
                                           onescol[:, 0:1], xg[:, tp, nn * 512:(nn + 1) * 512],
                                           start=(tc == 0), stop=(tc == NTC - 1))
                        smm.inc(ins)

                @blk.sync
                def _(sy):
                    sy.wait_ge(ssv.h, sv0 + NTC + 1)
                    sy.wait_ge(sl2.h, sl2.v)
                    sl2.inc(sy.dma_start(out=gm_ag_in[:, :], in_=garow[0:1, :]), 16)

                @blk.gpsimd
                def _(g):
                    g.wait_ge(sl2.h, sd0 + 16)
                    ssg.inc(g.collective_compute(
                        "AllGather", ALU.bypass, replica_groups=pair,
                        ins=[gm_ag_in.ap().opt()], outs=[gm_ag_out.ap().opt()]))
                    g.wait_ge(ssg.h, ssg.v)
                    g.wait_ge(sld.h, sld.v)
                    sld.inc(g.dma_start(out=garow[0:1, :], in_=gm_ag_out[0:1, :]), 16)
                    g.wait_ge(sld.h, sld.v)
                    sld.inc(g.dma_start(out=gbrow[0:1, :], in_=gm_ag_out[1:2, :]), 16)
            gm_ld = sld.v
            if P7SUB < 1:
                return

            with nc.Block() as blk:
                mm0, sv0, ss0, sd0 = smm.v, ssv.v, sss.v, sl2.v

                @blk.vector
                def _(v):
                    v.wait_ge(sld.h, gm_ld)
                    v.wait_ge(sl2.h, b7_ld)
                    v.tensor_add(garow[0:1, :], garow[0:1, :], gbrow[0:1, :])
                    v.drain()
                    ssv.inc(v.tensor_scalar(gxr[0:1, :], garow[0:1, :], 1.0 / 4096.0, 0.0,
                                            ALU.mult, ALU.add))
                    v.tensor_scalar(tiny[0:1, 6:7], b4r[0:1, 0:1], 0.5, 0.0, ALU.mult, ALU.add)
                    ssv.inc(v.tensor_scalar(tiny[0:1, 7:8], gb3[0:1, 0:1], 0.5, 0.0,
                                            ALU.mult, ALU.add))
                    v.wait_ge(smm.h, mm0 + 1)
                    ssv.inc(v.tensor_copy(gfm, ps_g[:, 700:708]))
                    v.wait_ge(sss.h, ss0 + 7)
                    v.tensor_mul(tiny[0:1, 0:1], arr[0:1, 0:1], tiny[0:1, 2:3])
                    v.drain()
                    v.tensor_scalar(tiny[0:1, 4:5], tiny[0:1, 0:1], 0.7, 0.1, ALU.mult, ALU.add)
                    v.tensor_scalar(tiny[0:1, 5:6], arr[0:1, 0:1], 0.3, 0.0, ALU.mult, ALU.add)
                    v.drain()
                    ssv.inc(v.tensor_copy(abrow[0:1, :], tiny[0:1, 4:7]))
                    v.wait_ge(smm.h, mm0 + 5)
                    ssv.inc(v.tensor_copy(acbc, ps_g[:, 600:603]))

                @blk.tensor
                def _(t):
                    t.wait_ge(sld.h, w7_ld)
                    t.wait_ge(ssv.h, sv0 + 1)
                    for j in range(8):
                        ins = t.matmul(ps_g[:, 700 + j:701 + j],
                                       gxr[0:1, j * 128:(j + 1) * 128],
                                       onesrow[0:1, 0:1], start=True, stop=True)
                    smm.inc(ins)
                    t.wait_ge(ssv.h, sv0 + 3)
                    for oc in range(4):
                        for ic in range(DC):
                            ins = t.matmul(ps_f[:, oc * 512:oc * 512 + 1],
                                           gc1[:, ic, oc * 128:(oc + 1) * 128],
                                           gfm[:, ic:ic + 1],
                                           start=(ic == 0), stop=(ic == DC - 1))
                    smm.inc(ins)
                    t.wait_ge(sss.h, ss0 + 4)
                    for oc in range(2):
                        for ic in range(4):
                            ins = t.matmul(ps_g[:, oc * 512:oc * 512 + 1],
                                           gc2[:, ic, oc * 128:(oc + 1) * 128],
                                           g1b[:, ic:ic + 1], start=(ic == 0), stop=(ic == 3))
                    smm.inc(ins)
                    t.wait_ge(sss.h, ss0 + 6)
                    for ic in range(2):
                        ins = t.matmul(ps_f[0:1, 1024:1025], gc3[:, ic:ic + 1],
                                       g2b[:, ic:ic + 1], start=(ic == 0), stop=(ic == 1))
                    smm.inc(ins)
                    t.wait_ge(ssv.h, sv0 + 4)
                    ins = t.matmul(ps_g[:, 600:603], onesrow[0:1, 0:128], abrow[0:1, :],
                                   start=True, stop=True)
                    smm.inc(ins)

                @blk.scalar
                def _(s):
                    s.wait_ge(smm.h, mm0 + 2)
                    for oc in range(4):
                        sss.inc(s.activation(g1b[:, oc:oc + 1], ps_f[:, oc * 512:oc * 512 + 1],
                                             AF.Gelu, bias=gb1[:, oc:oc + 1]))
                    s.wait_ge(smm.h, mm0 + 3)
                    for oc in range(2):
                        sss.inc(s.activation(g2b[:, oc:oc + 1], ps_g[:, oc * 512:oc * 512 + 1],
                                             AF.Gelu, bias=gb2[:, oc:oc + 1]))
                    s.wait_ge(smm.h, mm0 + 4)
                    s.wait_ge(ssv.h, sv0 + 2)
                    sss.inc(s.activation(tiny[0:1, 2:3], ps_f[0:1, 1024:1025], AF.Tanh,
                                         bias=tiny[0:1, 7:8]))
            ab_done = ssv.v
            if P7SUB < 2:
                return

            af.freea("fscr"); af.freea("garow"); af.freea("gbrow")
            osf = af.alloc("osf", (2, 1024))
            otok = ab.alloc("otok", (2, D))
            outT = ab.alloc("outT", (DC, 256))
            h1b = ab.alloc("h1b", (DC, 256))
            h2b = ab.alloc("h2b", (4, 256))
            h3b = ab.alloc("h3b", (2, 256))
            ucol = af.alloc("ucol", (2,))
            ccol = af.alloc("ccol", (2,))
            ncol = af.alloc("ncol", (2,))
            twq = af.alloc("twq", (512,))
            DBG["ab7"] = dict(ab.live); DBG["af7"] = dict(af.live)

            for sl in range(min(8, P7SUB - 1)):
                tc0 = sl * 2
                with nc.Block() as blk:
                    mm0, sv0, ss0 = smm.v, ssv.v, sss.v

                    @blk.vector
                    def _(v):
                        v.wait_ge(ssv.h, ab_done)
                        for i in range(2):
                            ssv.inc(v.tensor_scalar(osf[:, i, :], XS[:, tc0 + i, :],
                                                    stats[:, 7, tc0 + i:tc0 + i + 1],
                                                    stats[:, 9, tc0 + i:tc0 + i + 1],
                                                    ALU.mult, ALU.add))
                        v.drain()
                        for cyc in range(3):
                            cm = mm0 + cyc * 18
                            cs = ss0 + cyc * 16
                            for i in range(2):
                                ssv.inc(v.tensor_copy(otok[:, i, :], osf[:, i, :]))
                            for i in range(2):
                                v.wait_ge(smm.h, cm + i + 1)
                                ssv.inc(v.tensor_copy(
                                    outT[:, :, i * 128:(i + 1) * 128],
                                    pst2[:, (i % 2) * 1024:(i % 2) * 1024 + 1024]
                                    .rearrange("p (f q) -> p f q", q=128)))
                            for i in range(2):
                                v.wait_ge(sss.h, cs + 14 + i + 1)
                                v.tensor_scalar(ccol[:, i:i + 1], ucol[:, i:i + 1],
                                                acbc[:, 1:2], acbc[:, 0:1],
                                                ALU.mult, ALU.add)
                                v.drain()
                                v.tensor_scalar_mul(ncol[:, i:i + 1], ccol[:, i:i + 1], -1.0)
                                os_i = osf[:, i, :]
                                v.tensor_copy(twq, os_i[:, 0:512])
                                v.drain()
                                v.scalar_tensor_tensor(os_i[:, 0:256], os_i[:, 512:768],
                                                       ccol[:, i:i + 1], os_i[:, 0:256],
                                                       ALU.mult, ALU.add)
                                v.scalar_tensor_tensor(os_i[:, 256:512], os_i[:, 768:1024],
                                                       ncol[:, i:i + 1], os_i[:, 256:512],
                                                       ALU.mult, ALU.add)
                                v.drain()
                                v.scalar_tensor_tensor(os_i[:, 512:768], twq[:, 0:256],
                                                       ncol[:, i:i + 1], os_i[:, 512:768],
                                                       ALU.mult, ALU.add)
                                ssv.inc(v.scalar_tensor_tensor(
                                    os_i[:, 768:1024], twq[:, 256:512],
                                    ccol[:, i:i + 1], os_i[:, 768:1024],
                                    ALU.mult, ALU.add))
                                v.drain()
                        for i in range(2):
                            ssv.inc(v.tensor_add(XS[:, tc0 + i, :], XS[:, tc0 + i, :],
                                                 osf[:, i, :]))

                    @blk.tensor
                    def _(t):
                        t.wait_ge(sld.h, w7_ld)
                        for cyc in range(3):
                            cb = sv0 + 2 + cyc * 6
                            cm = mm0 + cyc * 18
                            cs = ss0 + cyc * 16
                            for i in range(2):
                                t.wait_ge(ssv.h, cb + i + 1)
                                for fc in range(DC):
                                    ins = t.transpose(
                                        pst2[:, (i % 2) * 1024 + fc * 128:
                                             (i % 2) * 1024 + (fc + 1) * 128],
                                        otok[:, i, fc * 128:(fc + 1) * 128], ident)
                                smm.inc(ins)
                            t.wait_ge(ssv.h, cb + 4)
                            for oc in range(DC):
                                if oc >= 4:
                                    t.wait_ge(sss.h, cs + oc - 4 + 1)
                                pb = ps_f[:, (oc % 4) * 512:(oc % 4) * 512 + 256]
                                for ic in range(DC):
                                    ins = t.matmul(pb, cn1[:, ic, oc * 128:(oc + 1) * 128],
                                                   outT[:, ic, :], start=(ic == 0),
                                                   stop=(ic == DC - 1))
                                smm.inc(ins)
                            t.wait_ge(sss.h, cs + 8)
                            for oc in range(4):
                                if oc >= 2:
                                    t.wait_ge(sss.h, cs + 8 + oc - 2 + 1)
                                pb = ps_g[:, (oc % 2) * 512:(oc % 2) * 512 + 256]
                                for ic in range(DC):
                                    ins = t.matmul(pb, cn2[:, ic, oc * 128:(oc + 1) * 128],
                                                   h1b[:, ic, :], start=(ic == 0),
                                                   stop=(ic == DC - 1))
                                smm.inc(ins)
                            t.wait_ge(sss.h, cs + 12)
                            for oc in range(2):
                                pb = ps_f[:, (oc % 2) * 512:(oc % 2) * 512 + 256]
                                for ic in range(4):
                                    ins = t.matmul(pb, cn3[:, ic, oc * 128:(oc + 1) * 128],
                                                   h2b[:, ic, :], start=(ic == 0),
                                                   stop=(ic == 3))
                                smm.inc(ins)
                            t.wait_ge(sss.h, cs + 14)
                            for i in range(2):
                                for ic in range(2):
                                    ins = t.matmul(ps_g[:, 300 + i * 600:301 + i * 600],
                                                   h3b[:, ic, i * 128:(i + 1) * 128],
                                                   cn4[:, ic:ic + 1],
                                                   start=(ic == 0), stop=(ic == 1))
                                smm.inc(ins)

                    @blk.scalar
                    def _(s):
                        for cyc in range(3):
                            cm = mm0 + cyc * 18
                            for oc in range(DC):
                                s.wait_ge(smm.h, cm + 2 + oc + 1)
                                sss.inc(s.activation(h1b[:, oc, :],
                                                     ps_f[:, (oc % 4) * 512:(oc % 4) * 512 + 256],
                                                     AF.Gelu, bias=cb1[:, oc:oc + 1]))
                            for oc in range(4):
                                s.wait_ge(smm.h, cm + 10 + oc + 1)
                                sss.inc(s.activation(h2b[:, oc, :],
                                                     ps_g[:, (oc % 2) * 512:(oc % 2) * 512 + 256],
                                                     AF.Gelu, bias=cb2[:, oc:oc + 1]))
                            for oc in range(2):
                                s.wait_ge(smm.h, cm + 14 + oc + 1)
                                sss.inc(s.activation(h3b[:, oc, :],
                                                     ps_f[:, (oc % 2) * 512:(oc % 2) * 512 + 256],
                                                     AF.Gelu, bias=cb3[:, oc:oc + 1]))
                            for i in range(2):
                                s.wait_ge(smm.h, cm + 16 + i + 1)
                                sss.inc(s.activation(ucol[:, i:i + 1],
                                                     ps_g[:, 300 + i * 600:301 + i * 600],
                                                     AF.Tanh, bias=acbc[:, 2:3]))

            for nm in ("cn1", "cn2", "cn3", "cn4", "gc1", "gc2", "gc3", "onescol",
                       "gxr", "gfm", "g1b", "g2b", "abrow", "xg", "otok",
                       "outT", "h1b", "h2b", "h3b"):
                ab.freea(nm)
            for nm in ("cb1", "cb2", "cb3", "b4r", "gb1", "gb2", "gb3", "arr",
                       "tiny", "acbc", "osf", "ucol", "ccol", "ncol", "twq"):
                af.freea(nm)
            with nc.Block() as blk:
                @blk.sync
                def _(sy):
                    tap("stream_p7", XS, (128, NTC, D), sy, waits=[(ssv, ssv.v)])

        pg = [[0, 1], [2, 3], [4, 5], [6, 7]]
        if upto >= 4:
            emit_l1_attn(pg)
        if upto >= 5:
            emit_l1_ffn(pg)

        with nc.Block() as blk:
            @blk.sync
            def _(sy):
                sy.wait_ge(ssv.h, ssv.v)
                sy.wait_ge(smm.h, smm.v)
                sy.wait_ge(sss.h, sss.v)
                sy.wait_ge(sld.h, sld.v)
                sy.wait_ge(sl2.h, sl2.v)
                sst.inc(sy.dma_start(out=out_d.ap().rearrange("(c p) f -> p c f", p=128),
                                     in_=XS), 16)
                sy.wait_ge(sst.h, sst.v)

    return nc, tap_outs


def kernel(**inputs):
    maps = _host_prep(inputs)
    nc, _ = build()
    res = run_bass_kernel_spmd(nc, maps, core_ids=list(range(NCORE)))
    out = np.concatenate([r["out"] for r in res.results], axis=0)
    return out.reshape(4, 4096, D).astype(np.float32)


def dev_run(inputs, taps=(), upto=99):
    maps = _host_prep(inputs)
    nc, tap_outs = build(taps=taps, upto=upto)
    res = run_bass_kernel_spmd(nc, maps, core_ids=list(range(NCORE)))
    return res.results


# version 04
# speedup vs baseline: 1.2927x; 1.2927x over previous
import math
from contextlib import ExitStack

import numpy as np
import ml_dtypes

import concourse.bass as bass
import concourse.mybir as mybir
from concourse.bass_utils import run_bass_kernel_spmd

F32 = mybir.dt.float32
BF16 = mybir.dt.bfloat16
AF = mybir.ActivationFunctionType
ALU = mybir.AluOpType
AX = mybir.AxisListType
BF = ml_dtypes.bfloat16

NCORE = 8
DBG = {}
P7SUB = 99
TL = 2048
NTC = 16
D = 1024
DC = 8
E = 2048
EPS = 1e-5


def _pp(b):
    b = np.asarray(b, np.float32)
    return np.ascontiguousarray(b.reshape(b.size // 128, 128).T)


def _row(b):
    return np.ascontiguousarray(np.asarray(b, np.float32).reshape(1, -1))


def _rope_tables(core):
    tau = (core % 2) * 2048 + np.arange(2048, dtype=np.float64)
    inv = 1.0 / 10000.0 ** (np.arange(0, 64, 2, dtype=np.float64) / 64.0)
    fr = tau[:, None] * inv[None, :]
    sin, cos = np.sin(fr), np.cos(fr)
    p = np.arange(128)
    j = (p % 64) % 32
    sgn = np.where((p % 64) < 32, -1.0, 1.0)
    cos_t = cos[:, j].T
    sin_t = sin[:, j].T * sgn[:, None]
    return {"rope_cos": np.ascontiguousarray(cos_t.astype(BF)),
            "rope_sin": np.ascontiguousarray(sin_t.astype(BF))}


def _host_prep(I):
    x = np.ascontiguousarray(np.asarray(I["L0_attn_x"], np.float32).reshape(16384, D))
    shared = {
        "l0_wq": np.asarray(I["L0_attn_Wq"], np.float32),
        "l0_bq": _pp(I["L0_attn_bq"]),
        "l0_wdown": np.asarray(I["L0_attn_Wdown"], np.float32),
        "l0_bdown": _pp(I["L0_attn_bdown"]),
        "l0_wup": np.asarray(I["L0_attn_Wup"], np.float32),
        "l0_bup": _pp(I["L0_attn_bup"]),
        "l0_wout": np.asarray(I["L0_attn_Wout"], np.float32),
        "l0_bout": _row(I["L0_attn_bout"]),
        "phi_wT": np.ascontiguousarray(np.asarray(I["L0_ffn_phi_w"], np.float32).T),
        "phi_b": _row(I["L0_ffn_phi_b"]),
        "l1_wq": np.ascontiguousarray(np.asarray(I["L1_attn_Wq"], np.float32).T),
        "l1_bq": _pp(I["L1_attn_bq"]),
        "l1_wk": np.ascontiguousarray(np.asarray(I["L1_attn_Wk"], np.float32).T),
        "l1_bk": _pp(I["L1_attn_bk"]),
        "l1_wv": np.ascontiguousarray(np.asarray(I["L1_attn_Wv"], np.float32).T),
        "l1_bv": _row(I["L1_attn_bv"]),
        "l1_wo": np.ascontiguousarray(np.asarray(I["L1_attn_Wo"], np.float32).T),
        "l1_bo": _row(I["L1_attn_bo"]),
        "cn_w1": np.asarray(I["L1_ffn_cn_w1"], np.float32), "cn_b1": _pp(I["L1_ffn_cn_b1"]),
        "cn_w2": np.asarray(I["L1_ffn_cn_w2"], np.float32), "cn_b2": _pp(I["L1_ffn_cn_b2"]),
        "cn_w3": np.asarray(I["L1_ffn_cn_w3"], np.float32), "cn_b3": _pp(I["L1_ffn_cn_b3"]),
        "cn_w4": _pp(I["L1_ffn_cn_w4"]), "cn_b4": _row(I["L1_ffn_cn_b4"]),
        "gc_w1": np.asarray(I["L1_ffn_gc_w1"], np.float32), "gc_b1": _pp(I["L1_ffn_gc_b1"]),
        "gc_w2": np.asarray(I["L1_ffn_gc_w2"], np.float32), "gc_b2": _pp(I["L1_ffn_gc_b2"]),
        "gc_w3": _pp(I["L1_ffn_gc_w3"]), "gc_b3": _row(I["L1_ffn_gc_b3"]),
        "a_range": _row(I["L1_ffn_adaptive_range"]),
    }
    exp_w = np.asarray(I["L0_ffn_expert_w"], np.float32)
    exp_b = np.asarray(I["L0_ffn_expert_b"], np.float32)
    maps = []
    for c in range(NCORE):
        m = dict(shared)
        m["x"] = np.ascontiguousarray(x[TL * c: TL * (c + 1)])
        m["exp_w"] = np.ascontiguousarray(exp_w[8 * c: 8 * c + 8])
        m["exp_b"] = np.ascontiguousarray(exp_b[8 * c: 8 * c + 8])
        m.update(_rope_tables(c))
        maps.append(m)
    return maps



class Arena:
    def __init__(self, nc, stack, name, words, dtype):
        self.t = stack.enter_context(nc.sbuf_tensor(name, [128, words], dtype))
        self.free_list = [(0, words)]
        self.live = {}
        self.hiwater = 0

    def alloc(self, name, shape):
        n = int(np.prod(shape))
        for i, (off, sz) in enumerate(self.free_list):
            if sz >= n:
                if sz == n:
                    self.free_list.pop(i)
                else:
                    self.free_list[i] = (off + n, sz - n)
                self.live[name] = (off, tuple(shape))
                self.hiwater = max(self.hiwater, off + n)
                return self.view(name)
        raise MemoryError(f"arena {self.t.name}: no {n} for {name} free={self.free_list} "
                          f"live={self.live}")

    def view(self, name):
        off, shape = self.live[name]
        n = int(np.prod(shape))
        ap = self.t[:, off:off + n]
        if len(shape) > 1:
            spec = "p (" + " ".join(f"d{i}" for i in range(len(shape))) + ") -> p " + \
                   " ".join(f"d{i}" for i in range(len(shape)))
            ap = ap.rearrange(spec, **{f"d{i}": s for i, s in enumerate(shape)})
        return ap

    def alias(self, src_name, new_name, shape):
        off, _ = self.live[src_name]
        assert int(np.prod(shape)) <= int(np.prod(self.live[src_name][1]))
        self.live[new_name] = (off, tuple(shape))
        return self.view(new_name)

    def freea(self, name):
        off, shape = self.live.pop(name)
        self.free_list.append((off, int(np.prod(shape))))
        self.free_list.sort()
        merged = []
        for o, s in self.free_list:
            if merged and merged[-1][0] + merged[-1][1] == o:
                merged[-1] = (merged[-1][0], merged[-1][1] + s)
            else:
                merged.append((o, s))
        self.free_list = [tuple(m) for m in merged]


class Sem:
    def __init__(self, nc, stack, name):
        self.h = stack.enter_context(nc.semaphore(name))
        self.v = 0

    def inc(self, instr, n=1):
        instr.then_inc(self.h, n)
        self.v += n
        return self.v


def build(taps=(), upto=99, races=True):
    nc = bass.Bass("TRN2", target_bir_lowering=False, detect_race_conditions=races)
    taps = set(taps)

    x_d = nc.dram_tensor("x", [TL, D], F32, kind="ExternalInput")
    out_d = nc.dram_tensor("out", [TL, D], F32, kind="ExternalOutput")

    def din(name, shape):
        return nc.dram_tensor(name, list(shape), F32, kind="ExternalInput")

    w_l0q = din("l0_wq", (D, D)); b_l0q = din("l0_bq", (128, 8))
    w_l0d = din("l0_wdown", (D, 256)); b_l0d = din("l0_bdown", (128, 2))
    w_l0u = din("l0_wup", (256, E)); b_l0u = din("l0_bup", (128, 16))
    w_l0o = din("l0_wout", (D, D)); b_l0o = din("l0_bout", (1, D))
    w_phi = din("phi_wT", (D, E)); b_phi = din("phi_b", (1, E))
    w_expw = din("exp_w", (8, D, D)); w_expb = din("exp_b", (8, D))
    w_l1q = din("l1_wq", (D, D)); b_l1q = din("l1_bq", (128, 8))
    w_l1k = din("l1_wk", (D, 256)); b_l1k = din("l1_bk", (128, 2))
    w_l1v = din("l1_wv", (D, 256)); b_l1v = din("l1_bv", (1, 256))
    w_l1o = din("l1_wo", (D, D)); b_l1o = din("l1_bo", (1, D))
    w_cn1 = din("cn_w1", (D, D)); b_cn1 = din("cn_b1", (128, 8))
    w_cn2 = din("cn_w2", (D, 512)); b_cn2 = din("cn_b2", (128, 4))
    w_cn3 = din("cn_w3", (512, 256)); b_cn3 = din("cn_b3", (128, 2))
    w_cn4 = din("cn_w4", (128, 2)); b_cn4 = din("cn_b4", (1, 1))
    w_gc1 = din("gc_w1", (D, 512)); b_gc1 = din("gc_b1", (128, 4))
    w_gc2 = din("gc_w2", (512, 256)); b_gc2 = din("gc_b2", (128, 2))
    w_gc3 = din("gc_w3", (128, 2)); b_gc3 = din("gc_b3", (1, 1))
    a_rng = din("a_range", (1, 1))
    rcos_d = nc.dram_tensor("rope_cos", [128, TL], BF16, kind="ExternalInput")
    rsin_d = nc.dram_tensor("rope_sin", [128, TL], BF16, kind="ExternalInput")

    ident_d = nc.inline_tensor(np.eye(128, dtype=BF), "ident")
    r_t = np.arange(128) % 8
    bd = (r_t[:, None] == r_t[None, :]).astype(BF)
    mask_d = nc.inline_tensor(np.ascontiguousarray(bd), "bdmask")
    ones_row_d = nc.inline_tensor(np.ones((1, 128), dtype=BF), "onesrow")
    _pj = np.arange(128)
    _j64 = _pj % 64
    _h64 = (_pj // 64) * 64
    _src = np.where(_j64 < 32, _h64 + 2 * _j64 + 1, _h64 + 2 * (_j64 - 32))
    _pm = np.zeros((128, 128), dtype=BF)
    _pm[_src, _pj] = 1.0
    pmat_d = nc.inline_tensor(np.ascontiguousarray(_pm), "permmat")

    c_dram = nc.dram_tensor("c_scr", [TL, E], BF16)
    d_dram = nc.dram_tensor("d_scr", [TL, E], BF16)
    xpart_dram = nc.dram_tensor("xpart_scr", [E, D], F32)
    xown_dram = nc.dram_tensor("xown_scr", [E // 8, D], F32)
    yown_dram = nc.dram_tensor("yown_scr", [E // 8, D], BF16)
    ytil_dram = nc.dram_tensor("ytil_scr", [E, D], BF16)
    k_ag_in = nc.dram_tensor("k_ag_in", [256, 2048], BF16)
    k_ag_out = nc.dram_tensor("k_ag_out", [512, 2048], BF16)
    v_ag_in = nc.dram_tensor("v_ag_in", [2048, 256], BF16)
    v_ag_out = nc.dram_tensor("v_ag_out", [4096, 256], BF16)
    gm_ag_in = nc.dram_tensor("gm_ag_in", [1, D], F32)
    gm_ag_out = nc.dram_tensor("gm_ag_out", [2, D], F32)

    tap_outs = {}
    stack = ExitStack()
    with stack:
        ab = Arena(nc, stack, "AB", 62 * 1024, BF16)
        af = Arena(nc, stack, "AF", 4 * 1024, F32)
        xs_t = stack.enter_context(nc.sbuf_tensor("XS", [128, NTC, D], F32))
        XS = xs_t[:, :, :]

        ps_f = stack.enter_context(nc.psum_tensor("PSF", [128, 2048], F32))
        ps_g = stack.enter_context(nc.psum_tensor("PSG", [128, 1024], F32))
        pst_stack = ExitStack()
        ps_t = pst_stack.enter_context(nc.psum_tensor("PST", [128, 2048], BF16))

        sld = Sem(nc, stack, "sld")
        sl2 = Sem(nc, stack, "sl2")
        sl3 = Sem(nc, stack, "sl3")
        smm = Sem(nc, stack, "smm")
        ssv = Sem(nc, stack, "ssv")
        sss = Sem(nc, stack, "sss")
        ssg = Sem(nc, stack, "ssg")
        sst = Sem(nc, stack, "sst")
        sdt = [Sem(nc, stack, f"sdt{i}") for i in range(4)]

        ident = ab.alloc("ident", (128,))
        maskbd = ab.alloc("maskbd", (128,))
        onesrow = ab.alloc("onesrow", (128,))
        rcos = ab.alloc("rcos", (TL,))
        rsin = ab.alloc("rsin", (TL,))

        stats = af.alloc("stats", (16, 16))
        fscr = af.alloc("fscr", (1024,))

        def tap(name, ap, shape, engine, waits=()):
            if name not in taps:
                return
            t_d = nc.dram_tensor("tap_" + name, list(shape), ap.dtype, kind="ExternalOutput")
            tap_outs[name] = t_d
            for s, v in waits:
                engine.wait_ge(s.h, v)
            sst.inc(engine.dma_start(out=t_d.ap(), in_=ap), 16)

        with nc.Block() as blk:
            @blk.sync
            def _(sy):
                sl2.inc(sy.dma_start(out=XS, in_=x_d.ap().rearrange("(c p) f -> p c f", p=128)), 16)

            @blk.gpsimd
            def _(g):
                for dst, src in ((ident, ident_d[:, :]), (maskbd, mask_d[:, :]),
                                 (onesrow[0:1, :], ones_row_d[:, :]),
                                 (rcos, rcos_d[:, :]), (rsin, rsin_d[:, :])):
                    sld.inc(g.dma_start(out=dst, in_=src), 16)
        consts_ld = sld.v

        def emit_ln_stats(xdep=None):
            with nc.Block() as blk:
                sv0, ss0 = ssv.v, sss.v

                @blk.vector
                def _(v):
                    v.wait_ge(sl2.h, 16)
                    if xdep is not None:
                        v.wait_ge(xdep[0].h, xdep[1])
                    st = stats
                    for tc in range(NTC):
                        v.tensor_scalar(fscr, XS[:, tc, :], 1.0, 0.0, ALU.mult,
                                        ALU.add, accum_out=st[:, 0, tc:tc + 1])
                        v.drain()
                        v.scalar_tensor_tensor(fscr, XS[:, tc, :], 1.0, XS[:, tc, :],
                                               ALU.mult, ALU.mult,
                                               accum_out=st[:, 1, tc:tc + 1])
                        v.drain()
                    v.tensor_scalar_mul(st[:, 2, :], st[:, 0, :], 1.0 / D)
                    v.drain()
                    v.scalar_tensor_tensor(st[:, 3, :], st[:, 2, :], 1.0, st[:, 2, :],
                                           ALU.mult, ALU.mult)
                    v.tensor_scalar_mul(st[:, 4, :], st[:, 1, :], 1.0 / D)
                    v.drain()
                    v.tensor_sub(st[:, 5, :], st[:, 4, :], st[:, 3, :])
                    v.drain()
                    v.tensor_scalar_add(st[:, 5, :], st[:, 5, :], EPS)
                    v.drain()
                    ssv.inc(v.reciprocal(st[:, 6, :], st[:, 5, :]))
                    v.wait_ge(sss.h, ss0 + 1)
                    v.tensor_mul(st[:, 8, :], st[:, 2, :], st[:, 7, :])
                    v.drain()
                    ssv.inc(v.tensor_scalar_mul(st[:, 9, :], st[:, 8, :], -1.0))

                @blk.scalar
                def _(s):
                    s.wait_ge(ssv.h, sv0 + 1)
                    sss.inc(s.activation(stats[:, 7, :], stats[:, 6, :], AF.Sqrt))
            return ssv.v

        def emit_xln_slice(tc0, ntc, tokbuf, fmbuf, stats_dep, fm_off=0):
            with nc.Block() as blk:
                sv0, mm0 = ssv.v, smm.v

                @blk.vector
                def _(v):
                    v.wait_ge(ssv.h, stats_dep)
                    for i in range(ntc):
                        tc = tc0 + i
                        ssv.inc(v.tensor_scalar(tokbuf[:, i, :], XS[:, tc, :],
                                                stats[:, 7, tc:tc + 1], stats[:, 9, tc:tc + 1],
                                                ALU.mult, ALU.add))
                    for i in range(ntc):
                        v.wait_ge(smm.h, mm0 + i + 1)
                        bank = (i % 2) * 1024
                        ssv.inc(v.tensor_copy(
                            fmbuf[:, :, fm_off + i * 128:fm_off + (i + 1) * 128],
                            ps_t[:, bank:bank + DC * 128].rearrange("p (f q) -> p f q", q=128)))

                @blk.tensor
                def _(t):
                    t.wait_ge(sld.h, sld.v)
                    t.wait_ge(ssv.h, sv0)
                    for i in range(ntc):
                        t.wait_ge(ssv.h, sv0 + i + 1)
                        if i >= 2:
                            t.wait_ge(ssv.h, sv0 + ntc + i - 1)
                        bank = (i % 2) * 1024
                        for fc in range(DC):
                            ins = t.transpose(ps_t[:, bank + fc * 128: bank + (fc + 1) * 128],
                                              tokbuf[:, i, fc * 128:(fc + 1) * 128], ident)
                        smm.inc(ins)
            return ssv.v

        def emit_l0_attn(stats_dep):
            TS = 256
            NSL = TL // TS
            wq_sb = ab.alloc("wq_sb", (DC, D))
            wdn_sb = ab.alloc("wdn_sb", (DC, 256))
            wup_sb = ab.alloc("wup_sb", (2, E))
            wout_sb = ab.alloc("wout_sb", (DC, D))
            boutr = ab.alloc("boutr", (D,))
            bq_f = af.alloc("bq_f", (8,))
            bq8_f = af.alloc("bq8_f", (8,))
            bdn_f = af.alloc("bdn_f", (2,))
            bup_f = af.alloc("bup_f", (16,))
            with nc.Block() as blk:
                sd0 = sl2.v

                @blk.gpsimd
                def _(g):
                    sld.inc(g.dma_start(out=wq_sb, in_=w_l0q.ap().rearrange("(i p) o -> p i o", p=128)), 16)
                    sld.inc(g.dma_start(out=wdn_sb, in_=w_l0d.ap().rearrange("(i p) o -> p i o", p=128)), 16)
                    sld.inc(g.dma_start(out=wup_sb, in_=w_l0u.ap().rearrange("(i p) o -> p i o", p=128)), 16)
                    sld.inc(g.dma_start(out=wout_sb, in_=w_l0o.ap().rearrange("(i p) o -> p i o", p=128)), 16)
                    sld.inc(g.dma_start(out=boutr[0:1, :], in_=b_l0o[:, :]), 16)

                @blk.sync
                def _(sy):
                    sl2.inc(sy.dma_start(out=bq_f, in_=b_l0q[:, :]), 16)
                    sl2.inc(sy.dma_start(out=bdn_f, in_=b_l0d[:, :]), 16)
                    sl2.inc(sy.dma_start(out=bup_f, in_=b_l0u[:, :]), 16)

                @blk.vector
                def _(v):
                    v.wait_ge(sl2.h, sd0 + 48)
                    ssv.inc(v.tensor_scalar_mul(bq8_f, bq_f, 0.125))
            wld, bld, b8 = sld.v, sl2.v, ssv.v

            xtok_sl = ab.alloc("xtok_sl", (2, D))
            xlnT_sl = ab.alloc("xlnT_sl", (DC, TS))
            q_fm = ab.alloc("q_fm", (DC, TS))
            kv_fm = ab.alloc("kv_fm", (16, TS))
            lat_fm = ab.alloc("lat_fm", (2, TS))
            rt1 = ab.alloc("rt1", (16, TS))
            rt2 = ab.alloc("rt2", (16, TS))
            bufA = ab.alloc("bufA", (2, 1024))
            bufB = ab.alloc("bufB", (2, 1024))
            ptb = ab.alloc("ptb", (2, 1024))
            vgb = ab.alloc("vgb", (2, 8, 64))
            odmj = ab.alloc("odmj", (2, 1024))
            o_fm = ab.alloc("o_fm", (DC, TS))
            submr = af.alloc("submr", (2, 1024))
            mrow = af.alloc("mrow", (2, 8))
            zrow = af.alloc("zrow", (2, 8))
            zrec = af.alloc("zrec", (2, 8))
            q_grp = ab.alloc("q_grp", (32, 64))
            kg2 = ab.alias("rt1", "kg2", (32, 128))
            vgrpb = ab.alias("rt2", "vgrpb", (32, 128))

            for sl in range(NSL):
                t0 = sl * TS
                emit_xln_slice(sl * 2, 2, ab.view("xtok_sl"), ab.view("xlnT_sl"), stats_dep)
                xlnT_v = ab.view("xlnT_sl")

                with nc.Block() as blk:
                    mm0, sv0 = smm.v, ssv.v

                    @blk.tensor
                    def _(t):
                        t.wait_ge(sld.h, wld)
                        t.wait_ge(ssv.h, sv0)
                        for oc in range(DC):
                            if oc >= 4:
                                t.wait_ge(ssv.h, sv0 + (oc - 4) + 1)
                            pb = ps_f[:, (oc % 4) * 512:(oc % 4) * 512 + TS]
                            for ic in range(DC):
                                ins = t.matmul(pb, wq_sb[:, ic, oc * 128:(oc + 1) * 128],
                                               xlnT_v[:, ic, :],
                                               start=(ic == 0), stop=(ic == DC - 1))
                            smm.inc(ins)
                        for oc in range(2):
                            pb = ps_g[:, oc * 512: oc * 512 + TS]
                            for ic in range(DC):
                                ins = t.matmul(pb, wdn_sb[:, ic, oc * 128:(oc + 1) * 128],
                                               xlnT_v[:, ic, :],
                                               start=(ic == 0), stop=(ic == DC - 1))
                            smm.inc(ins)
                        t.wait_ge(ssv.h, sv0 + 10)
                        for oc in range(16):
                            if oc >= 4:
                                t.wait_ge(ssv.h, sv0 + 10 + (oc - 4) + 1)
                            pb = ps_f[:, (oc % 4) * 512:(oc % 4) * 512 + TS]
                            for ic in range(2):
                                ins = t.matmul(pb, wup_sb[:, ic, oc * 128:(oc + 1) * 128],
                                               lat_fm[:, ic, :],
                                               start=(ic == 0), stop=(ic == 1))
                            smm.inc(ins)

                    @blk.vector
                    def _(v):
                        v.wait_ge(sl2.h, bld)
                        v.wait_ge(ssv.h, b8)
                        for oc in range(DC):
                            v.wait_ge(smm.h, mm0 + oc + 1)
                            pb = ps_f[:, (oc % 4) * 512:(oc % 4) * 512 + TS]
                            ssv.inc(v.tensor_scalar(q_fm[:, oc, :], pb, 0.125,
                                                    bq8_f[:, oc:oc + 1], ALU.mult, ALU.add))
                        for oc in range(2):
                            v.wait_ge(smm.h, mm0 + 8 + oc + 1)
                            pb = ps_g[:, oc * 512: oc * 512 + TS]
                            ssv.inc(v.tensor_scalar(lat_fm[:, oc, :], pb,
                                                    bdn_f[:, oc:oc + 1], 0.0, ALU.add, ALU.add))
                        for oc in range(16):
                            v.wait_ge(smm.h, mm0 + 10 + oc + 1)
                            pb = ps_f[:, (oc % 4) * 512:(oc % 4) * 512 + TS]
                            ssv.inc(v.tensor_scalar(kv_fm[:, oc, :], pb,
                                                    bup_f[:, oc:oc + 1], 0.0, ALU.add, ALU.add))

                with nc.Block() as blk:
                    sv0, sd0 = ssv.v, sl2.v

                    @blk.sync
                    def _(sy):
                        sy.wait_ge(ssv.h, sv0)
                        for dst0, src0 in ((0, 32), (32, 0), (64, 96), (96, 64)):
                            sl2.inc(sy.dma_start(out=rt1[dst0:dst0 + 32, 0:8, :],
                                                 in_=q_fm[src0:src0 + 32, :, :]), 16)
                        for dst0, src0 in ((0, 32), (32, 0)):
                            sl2.inc(sy.dma_start(out=rt2[dst0:dst0 + 32, :, :],
                                                 in_=kv_fm[src0:src0 + 32, :, :]), 16)

                    @blk.vector
                    def _(v):
                        cq = rcos[:, t0:t0 + TS].unsqueeze(1)
                        sq = rsin[:, t0:t0 + TS].unsqueeze(1)
                        ck = rcos[0:64, t0:t0 + TS].unsqueeze(1)
                        sk = rsin[0:64, t0:t0 + TS].unsqueeze(1)
                        v.wait_ge(sl2.h, sd0 + 96)
                        v.tensor_mul(rt1[:, 8:16, :], rt1[:, 0:8, :],
                                     sq.broadcast_to((128, 8, TS)))
                        v.drain()
                        v.tensor_mul(rt1[:, 0:8, :], q_fm, cq.broadcast_to((128, 8, TS)))
                        v.drain()
                        ssv.inc(v.tensor_add(q_fm, rt1[:, 0:8, :], rt1[:, 8:16, :]))
                        v.drain()
                        v.tensor_mul(rt1[0:64, :, :], rt2[0:64, :, :],
                                     sk.broadcast_to((64, 16, TS)))
                        v.drain()
                        v.tensor_mul(rt2[0:64, :, :], kv_fm[0:64, :, :],
                                     ck.broadcast_to((64, 16, TS)))
                        v.drain()
                        ssv.inc(v.tensor_add(kv_fm[0:64, :, :], rt2[0:64, :, :],
                                             rt1[0:64, :, :]))
                        v.drain()
                        v.tensor_copy(q_grp[0:64, :, :].rearrange("p g (c t) -> p g c t", t=8),
                                      q_fm[0:64, :, :].rearrange("p c (g t) -> p g c t", t=8))
                        v.tensor_copy(q_grp[64:128, :, :].rearrange("p g (c t) -> p g c t", t=8),
                                      q_fm[64:128, :, :].rearrange("p c (g t) -> p g c t", t=8))
                        ssv.inc(v.tensor_copy(
                            kg2[0:64, :, :].rearrange("p g (gg t) -> p g gg t", t=8),
                            kv_fm[0:64, :, :].rearrange("p gg (g t) -> p g gg t", t=8)))
                        ssv.inc(v.tensor_copy(
                            vgrpb[0:64, :, :].rearrange("p g (gg t) -> p g gg t", t=8),
                            kv_fm[64:128, :, :].rearrange("p gg (g t) -> p g gg t", t=8)))

                    @blk.gpsimd
                    def _(g):
                        g.wait_ge(ssv.h, sv0 + 3)
                        sld.inc(g.dma_start(out=kg2[64:128, :, :], in_=kg2[0:64, :, :]), 16)

                rope_sv, rope_ld = ssv.v, sld.v

                NB = 4
                with nc.Block() as blk:
                    mm0, sv0, ss0 = smm.v, ssv.v, sss.v

                    @blk.tensor
                    def _(t):
                        t.wait_ge(ssv.h, rope_sv)
                        t.wait_ge(sld.h, rope_ld)
                        for b in range(NB):
                            b2 = b % 2
                            if b >= 2:
                                t.wait_ge(ssv.h, sv0 + 5 * (b - 2) + 1)
                            for j in range(8):
                                grp = b * 8 + j
                                pb = ps_f[:, b2 * 1024 + j * 128: b2 * 1024 + (j + 1) * 128]
                                t.matmul(pb[0:64, :], q_grp[0:64, grp, :],
                                         kg2[0:64, grp, :],
                                         start=True, stop=True, tile_position=(0, 0))
                                ins = t.matmul(pb[64:128, :], q_grp[64:128, grp, :],
                                               kg2[64:128, grp, :],
                                               start=True, stop=True, tile_position=(64, 64))
                            smm.inc(ins)
                            t.wait_ge(ssv.h, sv0 + 5 * b + 2)
                            if b >= 1:
                                t.wait_ge(ssv.h, sv0 + 5 * (b - 1) + 3)
                            for j in range(8):
                                ins = t.transpose(ps_t[:, j * 128:(j + 1) * 128],
                                                  bufA[:, b2, j * 128:(j + 1) * 128], ident)
                            smm.inc(ins)
                            if b >= 1:
                                t.wait_ge(ssv.h, sv0 + 5 * (b - 1) + 4)
                            for j in range(8):
                                grp = b * 8 + j
                                ins = t.transpose(ps_t[:, 1024 + j * 64: 1024 + (j + 1) * 64],
                                                  vgrpb[0:64, grp, :],
                                                  ident[0:64, 0:64])
                            smm.inc(ins)
                            t.wait_ge(ssv.h, sv0 + 5 * b + 4)
                            if b >= 1:
                                t.wait_ge(ssv.h, sv0 + 5 * (b - 1) + 5)
                            for j in range(8):
                                ins = t.matmul(ps_g[0:64, j * 128:(j + 1) * 128],
                                               vgb[:, b2, j, :],
                                               ptb[:, b2, j * 128:(j + 1) * 128],
                                               start=True, stop=True)
                            smm.inc(ins)

                    @blk.vector
                    def _(v):
                        v.wait_ge(sld.h, rope_ld)
                        for b in range(NB):
                            b2 = b % 2
                            if b >= 2:
                                v.wait_ge(sss.h, ss0 + (b - 2) + 1)
                            v.wait_ge(smm.h, mm0 + 4 * b + 1)
                            for hh in range(2):
                                pbv = ps_f[:, b2 * 1024 + hh * 512: b2 * 1024 + (hh + 1) * 512]\
                                    .rearrange("p (g q) -> p g q", q=128)
                                v.reduce_max(mrow[:, b2, hh * 4:(hh + 1) * 4], pbv, axis=AX.X,
                                             negate=True)
                                v.drain()
                                ins = v.tensor_add(
                                    submr[:, b2, hh * 512:(hh + 1) * 512]
                                    .rearrange("p (g q) -> p g q", q=128), pbv,
                                    mrow[:, b2, hh * 4:(hh + 1) * 4].unsqueeze(2)
                                    .broadcast_to((128, 4, 128)))
                            ssv.inc(ins)
                            v.wait_ge(sss.h, ss0 + b + 1)
                            va = bufA[:, b2, :].rearrange("p (g q) -> p g q", q=128)
                            vb = bufB[:, b2, :].rearrange("p (g q) -> p g q", q=128)
                            v.tensor_mul(vb, va, maskbd.unsqueeze(1).broadcast_to((128, 8, 128)))
                            v.drain()
                            v.reduce_sum(zrow[:, b2, :], vb, axis=AX.X)
                            v.drain()
                            v.reciprocal(zrec[:, b2, :], zrow[:, b2, :])
                            v.drain()
                            ssv.inc(v.tensor_mul(va, vb, zrec[:, b2, :].unsqueeze(2)
                                                 .broadcast_to((128, 8, 128))))
                            v.wait_ge(smm.h, mm0 + 4 * b + 2)
                            ssv.inc(v.tensor_copy(ptb[:, b2, :], ps_t[:, 0:1024]))
                            v.wait_ge(smm.h, mm0 + 4 * b + 3)
                            ssv.inc(v.tensor_copy(
                                vgb[:, b2, :, :],
                                ps_t[:, 1024:1536].rearrange("p (j dd) -> p j dd", dd=64)))
                            v.wait_ge(smm.h, mm0 + 4 * b + 4)
                            bp = (b % 2) * 64
                            ssv.inc(v.tensor_copy(
                                odmj[bp:bp + 64, b // 2, :], ps_g[0:64, 0:1024]))

                    @blk.scalar
                    def _(s):
                        for b in range(NB):
                            b2 = b % 2
                            s.wait_ge(ssv.h, sv0 + 5 * b + 1)
                            if b >= 2:
                                s.wait_ge(smm.h, mm0 + 4 * (b - 2) + 2)
                            sss.inc(s.activation(bufA[:, b2, :], submr[:, b2, :], AF.Exp))

                att_sv = ssv.v

                with nc.Block() as blk:
                    mm0, sv0 = smm.v, ssv.v

                    @blk.vector
                    def _(v):
                        v.wait_ge(ssv.h, att_sv)
                        for h in range(16):
                            off = 64 * (h % 2) + (h // 2) * 8
                            hb = (h % 2) * 64
                            for bp in range(2):
                                src_ap = odmj[bp * 64:bp * 64 + 64, :, :].rearrange(
                                    "p bb (j r) -> p bb j r", r=128)[:, :, :, off:off + 8]
                                dst_ap = o_fm[hb:hb + 64, h // 2, :].rearrange(
                                    "p (bb q) -> p bb q", bb=2)[:, :, 64 * bp:64 * bp + 64].rearrange(
                                    "p bb (j t) -> p bb j t", t=8)
                                ssv.inc(v.tensor_copy(dst_ap, src_ap))
                        for ts_ in range(2):
                            v.wait_ge(smm.h, mm0 + ts_ + 1)
                            tci = sl * 2 + ts_
                            pg = ps_g[:, 0:1024] if ts_ == 0 else ps_f[:, 0:1024]
                            ssv.inc(v.tensor_add(XS[:, tci, :], XS[:, tci, :], pg))

                    @blk.tensor
                    def _(t):
                        t.wait_ge(ssv.h, sv0 + 32)
                        for ts_ in range(2):
                            pg = ps_g if ts_ == 0 else ps_f
                            for nn in range(2):
                                pb = pg[:, nn * 512:(nn + 1) * 512]
                                for ic in range(DC):
                                    t.matmul(pb, o_fm[:, ic, ts_ * 128:(ts_ + 1) * 128],
                                             wout_sb[:, ic, nn * 512:(nn + 1) * 512],
                                             start=(ic == 0), stop=False)
                                ins = t.matmul(pb, onesrow[0:1, 0:128],
                                               boutr[0:1, nn * 512:(nn + 1) * 512],
                                               start=False, stop=True)
                            smm.inc(ins)

            DBG["ab"] = dict(ab.live)
            DBG["af"] = dict(af.live)
            with nc.Block() as blk:
                @blk.sync
                def _(sy):
                    w = [(ssv, ssv.v), (smm, smm.v), (sss, sss.v)]
                    tap("p1_qfm", ab.view("q_fm"), (128, DC, 256), sy, waits=w)
                    tap("p1_kvfm", ab.view("kv_fm"), (128, 16, 256), sy, waits=w)
                    tap("p1_qgrp", ab.view("q_grp"), (128, 32, 64), sy, waits=w)
                    tap("p1_kg2", ab.view("kg2"), (128, 32, 128), sy, waits=w)
                    tap("p1_vgrpb", ab.view("vgrpb"), (128, 32, 128), sy, waits=w)
                    tap("p1_ofm", ab.view("o_fm"), (128, DC, 256), sy, waits=w)
                    tap("p1_odmj", ab.view("odmj"), (128, 2, 1024), sy, waits=w)
                    tap("p1_bufA", ab.view("bufA"), (128, 2, 1024), sy, waits=w)
            for nm in ("xtok_sl", "xlnT_sl", "q_fm", "kv_fm", "lat_fm", "rt1", "rt2",
                       "bufA", "bufB", "ptb", "vgb", "odmj", "o_fm", "q_grp",
                       "wq_sb", "wdn_sb", "wup_sb", "wout_sb", "boutr"):
                ab.freea(nm)
            ab.live.pop("kg2", None); ab.live.pop("vgrpb", None)
            for nm in ("bq_f", "bq8_f", "bdn_f", "bup_f", "submr", "mrow", "zrow", "zrec"):
                af.freea(nm)

        def emit_l0_ffn(stats_dep):
            xtok2 = ab.alloc("xtok2", (NTC, D))
            phi_sb = ab.alloc("phi_sb", (DC, E))
            phib_sb = ab.alloc("phib_sb", (E,))
            ebuf = ab.alloc("ebuf", (2, E))
            cstage = ab.alloc("cstage", (2, E))
            xlnT_tc = ab.alloc("xlnT_tc", (2, DC, 128))
            neg10 = af.alloc("neg10", (1,))
            sdrec = af.alloc("sdrec", (2, 64))
            screc = af.alloc("screc", (2, 32))

            with nc.Block() as blk:
                @blk.gpsimd
                def _(g):
                    sld.inc(g.dma_start(out=phi_sb, in_=w_phi.ap().rearrange("(i p) o -> p i o", p=128)), 16)
                    sld.inc(g.dma_start(out=phib_sb[0:1, :], in_=b_phi[:, :]), 16)

                @blk.vector
                def _(v):
                    v.wait_ge(ssv.h, stats_dep)
                    v.memset(neg10, -10.0)
                    for tc in range(NTC):
                        ssv.inc(v.tensor_scalar(xtok2[:, tc, :], XS[:, tc, :],
                                                stats[:, 7, tc:tc + 1], stats[:, 9, tc:tc + 1],
                                                ALU.mult, ALU.add))
            wld3 = sld.v
            xln2_done = ssv.v

            with nc.Block() as blk:
                mm0, sv0, ss0, sd0, sg0 = smm.v, ssv.v, sss.v, sl2.v, sld.v

                @blk.tensor
                def _(t):
                    t.wait_ge(sld.h, wld3)
                    t.wait_ge(ssv.h, xln2_done)
                    for tc in range(NTC):
                        tp = tc % 2
                        if tc >= 2:
                            t.wait_ge(ssv.h, sv0 + 3 * (tc - 2) + 1)
                        for fc in range(DC):
                            ins = t.transpose(ps_t[:, tp * 1024 + fc * 128: tp * 1024 + (fc + 1) * 128],
                                              xtok2[:, tc, fc * 128:(fc + 1) * 128], ident)
                        smm.inc(ins)
                        t.wait_ge(ssv.h, sv0 + 3 * tc + 1)
                        if tc >= 1:
                            t.wait_ge(sss.h, ss0 + tc)
                        for nn in range(4):
                            pb = ps_f[:, nn * 512:(nn + 1) * 512]
                            for ic in range(DC):
                                t.matmul(pb, xlnT_tc[:, tp, ic, :],
                                         phi_sb[:, ic, nn * 512:(nn + 1) * 512],
                                         start=(ic == 0), stop=False)
                            ins = t.matmul(pb, onesrow[0:1, 0:128],
                                           phib_sb[0:1, nn * 512:(nn + 1) * 512],
                                           start=False, stop=True)
                        smm.inc(ins)

                @blk.scalar
                def _(s):
                    for tc in range(NTC):
                        tp = tc % 2
                        s.wait_ge(smm.h, mm0 + 2 * tc + 2)
                        if tc >= 2:
                            s.wait_ge(sld.h, sg0 + 16 * (tc - 1))
                        sss.inc(s.activation(ebuf[:, tp, :], ps_f[:, 0:2048], AF.Exp,
                                             bias=neg10))

                @blk.vector
                def _(v):
                    for tc in range(NTC):
                        tp = tc % 2
                        v.wait_ge(smm.h, mm0 + 2 * tc + 1)
                        ssv.inc(v.tensor_copy(
                            xlnT_tc[:, tp, :, :],
                            ps_t[:, tp * 1024:tp * 1024 + 1024].rearrange("p (f q) -> p f q", q=128)))
                        v.wait_ge(sss.h, ss0 + tc + 1)
                        ev = ebuf[:, tp, :].rearrange("p (n q) -> p n q", q=32)
                        ev2 = ebuf[:, tp, :].rearrange("p (n q) -> p q n", q=32)
                        v.reduce_sum(sdrec[:, tp, :], ev, axis=AX.X)
                        v.reduce_sum(screc[:, tp, :], ev2, axis=AX.X)
                        v.drain()
                        v.reciprocal(sdrec[:, tp, :], sdrec[:, tp, :])
                        v.reciprocal(screc[:, tp, :], screc[:, tp, :])
                        if tc >= 2:
                            v.wait_ge(sl2.h, sd0 + 16 * (tc - 1))
                        v.drain()
                        ssv.inc(v.tensor_mul(
                            cstage[:, tp, :].rearrange("p (n q) -> p n q", q=32),
                            ev, screc[:, tp, :].unsqueeze(1).broadcast_to((128, 64, 32))))
                        v.drain()
                        ssv.inc(v.tensor_mul(
                            ev, ev, sdrec[:, tp, :].unsqueeze(2).broadcast_to((128, 64, 32))))

                @blk.sync
                def _(sy):
                    for tc in range(NTC):
                        tp = tc % 2
                        sy.wait_ge(ssv.h, sv0 + 3 * tc + 2)
                        sy.wait_ge(sl2.h, sl2.v)
                        sl2.inc(sy.dma_start(
                            out=c_dram.ap().rearrange("(c p) e -> p c e", p=128)[:, tc, :],
                            in_=cstage[:, tp, :]), 16)

                @blk.gpsimd
                def _(g):
                    for tc in range(NTC):
                        tp = tc % 2
                        g.wait_ge(ssv.h, sv0 + 3 * tc + 3)
                        g.wait_ge(sld.h, sld.v)
                        sld.inc(g.dma_start(
                            out=d_dram.ap().rearrange("(c p) e -> p c e", p=128)[:, tc, :],
                            in_=ebuf[:, tp, :]), 16)

            d_dmas = (sl2.v, sld.v)
            for nm in ("phi_sb", "phib_sb", "ebuf", "cstage", "xlnT_tc"):
                ab.freea(nm)

            dtile = ab.alloc("dtile", (4, 256))
            xstg = ab.alloc("xstg", (2, 2048))
            ab.freea("xstg")
            xstg = af.alloc("xstg", (2, 1024))
            with nc.Block() as blk:
                mm0, sv0, sd0, st0 = smm.v, ssv.v, sl2.v, sl3.v

                @blk.sync
                def _(sy):
                    sy.wait_ge(sld.h, d_dmas[1])
                    for ps in range(8):
                        for tc in range(NTC):
                            idx = ps * NTC + tc
                            if idx >= 4:
                                sy.wait_ge(smm.h, mm0 + (idx - 3))
                            sdt[idx % 4].inc(sy.dma_start(
                                out=dtile[:, idx % 4, :],
                                in_=d_dram[tc * 128:(tc + 1) * 128, ps * 256:(ps + 1) * 256]), 16)

                @blk.tensor
                def _(t):
                    for ps in range(8):
                        if ps >= 1:
                            t.wait_ge(ssv.h, sv0 + 2 * ps)
                        for tc in range(NTC):
                            idx = ps * NTC + tc
                            t.wait_ge(sdt[idx % 4].h, 16 * (idx // 4 + 1))
                            for ei in range(2):
                                for nn in range(2):
                                    ins = t.matmul(
                                        ps_f[:, (ei * 2 + nn) * 512:(ei * 2 + nn + 1) * 512],
                                        dtile[:, idx % 4, ei * 128:(ei + 1) * 128],
                                        xtok2[:, tc, nn * 512:(nn + 1) * 512],
                                        start=(tc == 0), stop=(tc == NTC - 1))
                            smm.inc(ins)

                @blk.vector
                def _(v):
                    for ps in range(8):
                        v.wait_ge(smm.h, mm0 + NTC * (ps + 1))
                        for ei in range(2):
                            if ps >= 1:
                                v.wait_ge(sl3.h, st0 + 16 * (2 * (ps - 1) + ei + 1))
                            ssv.inc(v.tensor_copy(xstg[:, ei, :],
                                                  ps_f[:, ei * 1024:(ei + 1) * 1024]))


                @blk.scalar
                def _(s):
                    for ps in range(8):
                        for ei in range(2):
                            s.wait_ge(ssv.h, sv0 + 2 * ps + ei + 1)
                            s.wait_ge(sl3.h, sl3.v)
                            sl3.inc(s.dma_start(
                                out=xpart_dram[(2 * ps + ei) * 128:(2 * ps + ei + 1) * 128, :],
                                in_=xstg[:, ei, :]), 16)

            xpart_done = sl3.v
            ab.freea("dtile"); ab.freea("xtok2")
            af.freea("xstg"); af.freea("neg10"); af.freea("sdrec"); af.freea("screc")

            xn = ab.alloc("xn", (2, D))
            xnT = ab.alloc("xnT", (DC, 256))
            expw = ab.alloc("expw", (2, DC, D))
            expb_sb = ab.alloc("expb_sb", (8 * D,))
            ystg = ab.alloc("ystg", (2, D))
            xow = af.alloc("xow", (2, D))
            xst = af.alloc("xst", (12, 2))

            with nc.Block() as blk:
                sg0, ld0 = ssg.v, sld.v

                @blk.gpsimd
                def _(g):
                    g.wait_ge(sl3.h, xpart_done)
                    ssg.inc(g.collective_compute(
                        "ReduceScatter", ALU.add, replica_groups=[list(range(NCORE))],
                        ins=[xpart_dram.ap().opt()], outs=[xown_dram.ap().opt()]))
                    g.wait_ge(ssg.h, sg0 + 1)
                    sld.inc(g.dma_start(out=xow,
                                        in_=xown_dram.ap().rearrange("(c p) f -> p c f", p=128)), 16)
                    sld.inc(g.dma_start(out=expb_sb[0:1, :],
                                        in_=w_expb.ap().rearrange("n h -> (n h)").unsqueeze(0)), 16)
                    g.wait_ge(sld.h, sld.v)
                    sld.inc(g.dma_start(out=expw[:, 0, :, :],
                                        in_=w_expw[0, :, :].rearrange("(i p) o -> p i o", p=128)), 16)
                    g.wait_ge(sld.h, sld.v)
                    sld.inc(g.dma_start(out=expw[:, 1, :, :],
                                        in_=w_expw[1, :, :].rearrange("(i p) o -> p i o", p=128)), 16)
            rs_ld = sld.v

            with nc.Block() as blk:
                sv0, ss0, mm0 = ssv.v, sss.v, smm.v

                @blk.vector
                def _(v):
                    v.wait_ge(sld.h, rs_ld)
                    v.tensor_scalar(xow, xow, 65000.0, -33000.0, ALU.min, ALU.max)
                    v.drain()
                    for i in range(2):
                        v.tensor_scalar(fscr, xow[:, i, :], 1.0, 0.0, ALU.mult,
                                        ALU.add, accum_out=xst[:, 0, i:i + 1])
                        v.drain()
                        v.scalar_tensor_tensor(fscr, xow[:, i, :], 1.0, xow[:, i, :],
                                               ALU.mult, ALU.mult,
                                               accum_out=xst[:, 1, i:i + 1])
                        v.drain()
                    v.tensor_scalar_mul(xst[:, 2, :], xst[:, 0, :], 1.0 / D)
                    v.drain()
                    v.scalar_tensor_tensor(xst[:, 3, :], xst[:, 2, :], 1.0, xst[:, 2, :],
                                           ALU.mult, ALU.mult)
                    v.tensor_scalar_mul(xst[:, 4, :], xst[:, 1, :], 1.0 / D)
                    v.drain()
                    v.tensor_sub(xst[:, 5, :], xst[:, 4, :], xst[:, 3, :])
                    v.drain()
                    v.tensor_scalar_add(xst[:, 5, :], xst[:, 5, :], EPS)
                    v.drain()
                    ssv.inc(v.reciprocal(xst[:, 6, :], xst[:, 5, :]))
                    v.wait_ge(sss.h, ss0 + 1)
                    v.tensor_mul(xst[:, 8, :], xst[:, 2, :], xst[:, 7, :])
                    v.drain()
                    v.tensor_scalar_mul(xst[:, 9, :], xst[:, 8, :], -1.0)
                    v.drain()
                    for i in range(2):
                        ssv.inc(v.tensor_scalar(xn[:, i, :], xow[:, i, :],
                                                xst[:, 7, i:i + 1], xst[:, 9, i:i + 1],
                                                ALU.mult, ALU.add))
                    for i in range(2):
                        v.wait_ge(smm.h, mm0 + i + 1)
                        ssv.inc(v.tensor_copy(
                            xnT[:, :, i * 128:(i + 1) * 128],
                            ps_t[:, (i % 2) * 1024:(i % 2) * 1024 + 1024]
                            .rearrange("p (f q) -> p f q", q=128)))

                @blk.scalar
                def _(s):
                    s.wait_ge(ssv.h, sv0 + 1)
                    sss.inc(s.activation(xst[:, 7, :], xst[:, 6, :], AF.Sqrt))
                    for i in range(2):
                        s.wait_ge(ssv.h, sv0 + 1 + i + 1)
                        sss.inc(s.activation(xn[:, i, :], xn[:, i, :], AF.Tanh))

                @blk.tensor
                def _(t):
                    for i in range(2):
                        t.wait_ge(sss.h, ss0 + 1 + i + 1)
                        for fc in range(DC):
                            ins = t.transpose(
                                ps_t[:, (i % 2) * 1024 + fc * 128:(i % 2) * 1024 + (fc + 1) * 128],
                                xn[:, i, fc * 128:(fc + 1) * 128], ident)
                        smm.inc(ins)
            xnt_done = ssv.v

            with nc.Block() as blk:
                mm0, sv0, st0, ld0 = smm.v, ssv.v, sl3.v, sld.v

                @blk.gpsimd
                def _(g):
                    for n in range(2, 8):
                        g.wait_ge(smm.h, mm0 + (n - 2) + 1)
                        g.wait_ge(sld.h, sld.v)
                        sld.inc(g.dma_start(out=expw[:, n % 2, :, :],
                                            in_=w_expw[n, :, :].rearrange("(i p) o -> p i o", p=128)), 16)

                @blk.tensor
                def _(t):
                    t.wait_ge(ssv.h, xnt_done)
                    for n in range(8):
                        t.wait_ge(sld.h, ld0 + 16 * max(0, n - 1) if n >= 2 else ld0)
                        if n >= 1:
                            t.wait_ge(ssv.h, sv0 + n)
                        for nn in range(2):
                            pb = ps_g[0:32, nn * 512:(nn + 1) * 512]
                            for ic in range(DC):
                                t.matmul(pb, xnT[:, ic, n * 32:(n + 1) * 32],
                                         expw[:, n % 2, ic, nn * 512:(nn + 1) * 512],
                                         start=(ic == 0), stop=False)
                            ins = t.matmul(pb, onesrow[0:1, 0:32],
                                           expb_sb[0:1, n * D + nn * 512: n * D + (nn + 1) * 512],
                                           start=False, stop=True)
                        smm.inc(ins)

                @blk.vector
                def _(v):
                    for n in range(8):
                        v.wait_ge(smm.h, mm0 + n + 1)
                        if n >= 2:
                            v.wait_ge(sl3.h, st0 + 16 * (n - 1))
                        ssv.inc(v.tensor_copy(ystg[0:32, n % 2, :], ps_g[0:32, 0:1024]))

                @blk.scalar
                def _(s):
                    for n in range(8):
                        s.wait_ge(ssv.h, sv0 + n + 1)
                        s.wait_ge(sl3.h, sl3.v)
                        sl3.inc(s.dma_start(out=yown_dram[n * 32:(n + 1) * 32, :],
                                            in_=ystg[0:32, n % 2, :]), 16)
            yown_done = sl3.v
            for nm in ("xn", "xnT", "expw", "expb_sb", "ystg"):
                ab.freea(nm)
            af.freea("xow"); af.freea("xst")

            ytil = ab.alloc("ytil", (NTC, D))
            ctt = ab.alloc("ctt", (NTC, E))
            with nc.Block() as blk:
                sg0, ld0, sd0 = ssg.v, sld.v, sl2.v

                @blk.gpsimd
                def _(g):
                    g.wait_ge(sl3.h, yown_done)
                    ssg.inc(g.collective_compute(
                        "AllGather", ALU.bypass, replica_groups=[list(range(NCORE))],
                        ins=[yown_dram.ap().opt()], outs=[ytil_dram.ap().opt()]))
                    g.wait_ge(ssg.h, sg0 + 1)
                    sld.inc(g.dma_start(out=ytil,
                                        in_=ytil_dram.ap().rearrange("(c p) f -> p c f", p=128)), 16)

                @blk.sync
                def _(sy):
                    sy.wait_ge(sl2.h, d_dmas[0])
                    sy.wait_ge(sl3.h, yown_done)
                    for ec in range(16):
                        sl2.inc(sy.dma_start_transpose(
                            out=ctt[:, ec, :], in_=c_dram[:, ec * 128:(ec + 1) * 128]), 16)
            yc_ld = (sld.v, sl2.v)

            with nc.Block() as blk:
                mm0, sv0 = smm.v, ssv.v

                @blk.tensor
                def _(t):
                    t.wait_ge(sld.h, yc_ld[0])
                    t.wait_ge(sl2.h, yc_ld[1])
                    for tc in range(NTC):
                        tp = tc % 2
                        if tc >= 2:
                            t.wait_ge(ssv.h, sv0 + (tc - 2) + 1)
                        for nn in range(2):
                            pb = ps_f[:, (tp * 2 + nn) * 512:(tp * 2 + nn + 1) * 512]
                            for ec in range(16):
                                ins = t.matmul(pb, ctt[:, ec, tc * 128:(tc + 1) * 128],
                                               ytil[:, ec, nn * 512:(nn + 1) * 512],
                                               start=(ec == 0), stop=(ec == 15))
                        smm.inc(ins)

                @blk.vector
                def _(v):
                    for tc in range(NTC):
                        tp = tc % 2
                        v.wait_ge(smm.h, mm0 + tc + 1)
                        ssv.inc(v.tensor_add(XS[:, tc, :], XS[:, tc, :],
                                             ps_f[:, tp * 1024:(tp + 1) * 1024]))

            ab.freea("ytil"); ab.freea("ctt")
            DBG["ab3"] = dict(ab.live)
            with nc.Block() as blk:
                @blk.sync
                def _(sy):
                    tap("stream_p3", XS, (128, NTC, D), sy, waits=[(ssv, ssv.v)])

        if upto >= 0:
            st_dep = emit_ln_stats()
        if upto >= 2:
            with nc.named_scope("p1"):
                emit_l0_attn(st_dep)
            with nc.Block() as blk:
                @blk.sync
                def _(sy):
                    tap("stream_p1", XS, (128, NTC, D), sy, waits=[(ssv, ssv.v)])
        if upto >= 3:
            with nc.named_scope("p3"):
                st_dep2 = emit_ln_stats(xdep=(ssv, ssv.v))
                emit_l0_ffn(st_dep2)



        def emit_l1_attn(pair):
            xlnT3 = ab.alloc("xlnT3", (DC, TL))
            xtok_s = ab.alloc("xtok_s", (2, D))
            st3 = emit_ln_stats(xdep=(ssv, ssv.v))
            for sl in range(8):
                emit_xln_slice(sl * 2, 2, ab.view("xtok_s"), ab.view("xlnT3"), st3,
                               fm_off=sl * 256)
            ab.freea("xtok_s")
            lnT3 = ssv.v

            with nc.named_scope("qproj"):
                wq1 = ab.alloc("wq1", (DC, D))
                q1 = ab.alloc("q1", (DC, TL))
                bq1 = af.alloc("bq1", (8,))
                bq81 = af.alloc("bq81", (8,))
                with nc.Block() as blk:
                    @blk.gpsimd
                    def _(g):
                        g.wait_ge(sld.h, sld.v)
                        sld.inc(g.dma_start(out=wq1, in_=w_l1q.ap().rearrange("(i p) o -> p i o", p=128)), 16)

                    @blk.sync
                    def _(sy):
                        sy.wait_ge(sl2.h, sl2.v)
                        sl2.inc(sy.dma_start(out=bq1, in_=b_l1q[:, :]), 16)
                wq_ld, bq_ld = sld.v, sl2.v
                with nc.Block() as blk:
                    mm0, sv0 = smm.v, ssv.v

                    @blk.tensor
                    def _(t):
                        t.wait_ge(sld.h, wq_ld)
                        t.wait_ge(ssv.h, lnT3)
                        for oc in range(DC):
                            for qt in range(4):
                                idx = oc * 4 + qt
                                if idx >= 4:
                                    t.wait_ge(ssv.h, sv0 + idx - 3)
                                pb = ps_f[:, (idx % 4) * 512:(idx % 4) * 512 + 512]
                                for ic in range(DC):
                                    ins = t.matmul(pb, wq1[:, ic, oc * 128:(oc + 1) * 128],
                                                   xlnT3[:, ic, qt * 512:(qt + 1) * 512],
                                                   start=(ic == 0), stop=(ic == DC - 1))
                                smm.inc(ins)

                    @blk.vector
                    def _(v):
                        v.wait_ge(sl2.h, bq_ld)
                        v.tensor_scalar_mul(bq81, bq1, 0.125)
                        v.drain()
                        for oc in range(DC):
                            for qt in range(4):
                                idx = oc * 4 + qt
                                v.wait_ge(smm.h, mm0 + idx + 1)
                                pb = ps_f[:, (idx % 4) * 512:(idx % 4) * 512 + 512]
                                ssv.inc(v.tensor_scalar(q1[:, oc, qt * 512:(qt + 1) * 512], pb,
                                                        0.125, bq81[:, oc:oc + 1],
                                                        ALU.mult, ALU.add))
                ab.freea("wq1")

            with nc.named_scope("kvproj"):
                wk1 = ab.alloc("wk1", (DC, 256))
                wv1 = ab.alloc("wv1", (DC, 256))
                k1 = ab.alloc("k1", (2, TL))
                vtok = ab.alloc("vtok", (NTC, 256))
                bvr = ab.alloc("bvr", (256,))
                bk1 = af.alloc("bk1", (2,))
                with nc.Block() as blk:
                    @blk.gpsimd
                    def _(g):
                        g.wait_ge(sld.h, sld.v)
                        sld.inc(g.dma_start(out=wk1, in_=w_l1k.ap().rearrange("(i p) o -> p i o", p=128)), 16)
                        g.wait_ge(sld.h, sld.v)
                        sld.inc(g.dma_start(out=wv1, in_=w_l1v.ap().rearrange("(i p) o -> p i o", p=128)), 16)
                        g.wait_ge(sld.h, sld.v)
                        sld.inc(g.dma_start(out=bvr[0:1, :], in_=b_l1v[:, :]), 16)

                    @blk.sync
                    def _(sy):
                        sy.wait_ge(sl2.h, sl2.v)
                        sl2.inc(sy.dma_start(out=bk1, in_=b_l1k[:, :]), 16)
                wkv_ld, bk_ld = sld.v, sl2.v
                with nc.Block() as blk:
                    mm0, sv0 = smm.v, ssv.v

                    @blk.tensor
                    def _(t):
                        t.wait_ge(sld.h, wkv_ld)
                        for oc in range(2):
                            for qt in range(4):
                                idx = oc * 4 + qt
                                if idx >= 4:
                                    t.wait_ge(ssv.h, sv0 + idx - 3)
                                pb = ps_f[:, (idx % 4) * 512:(idx % 4) * 512 + 512]
                                for ic in range(DC):
                                    ins = t.matmul(pb, wk1[:, ic, oc * 128:(oc + 1) * 128],
                                                   xlnT3[:, ic, qt * 512:(qt + 1) * 512],
                                                   start=(ic == 0), stop=(ic == DC - 1))
                                smm.inc(ins)
                        t.wait_ge(ssv.h, sv0 + 8)
                        for tc in range(NTC):
                            if tc >= 2:
                                t.wait_ge(ssv.h, sv0 + 8 + tc - 1)
                            pb = ps_g[:, (tc % 2) * 512:(tc % 2) * 512 + 256]
                            for ic in range(DC):
                                t.matmul(pb, xlnT3[:, ic, tc * 128:(tc + 1) * 128],
                                         wv1[:, ic, :], start=(ic == 0), stop=False)
                            ins = t.matmul(pb, onesrow[0:1, 0:128], bvr[0:1, :],
                                           start=False, stop=True)
                            smm.inc(ins)

                    @blk.vector
                    def _(v):
                        v.wait_ge(sl2.h, bk_ld)
                        for oc in range(2):
                            for qt in range(4):
                                idx = oc * 4 + qt
                                v.wait_ge(smm.h, mm0 + idx + 1)
                                pb = ps_f[:, (idx % 4) * 512:(idx % 4) * 512 + 512]
                                ssv.inc(v.tensor_scalar(k1[:, oc, qt * 512:(qt + 1) * 512], pb,
                                                        bk1[:, oc:oc + 1], 0.0, ALU.add, ALU.add))
                        for tc in range(NTC):
                            v.wait_ge(smm.h, mm0 + 8 + tc + 1)
                            pb = ps_g[:, (tc % 2) * 512:(tc % 2) * 512 + 256]
                            ssv.inc(v.tensor_copy(vtok[:, tc, :], pb))
                ab.freea("wk1"); ab.freea("wv1"); ab.freea("xlnT3"); ab.freea("bvr")

            with nc.named_scope("rope"):
                pmt = ab.alloc("pmt", (128,))
                rtmp = ab.alloc("rtmp", (512,))
                with nc.Block() as blk:
                    @blk.gpsimd
                    def _(g):
                        g.wait_ge(sld.h, sld.v)
                        sld.inc(g.dma_start(out=pmt, in_=pmat_d[:, :]), 16)
                pm_ld = sld.v
                rp_slices = [("q", oc, qt) for oc in range(DC) for qt in range(4)] + \
                            [("k", oc, qt) for oc in range(2) for qt in range(4)]
                with nc.Block() as blk:
                    mm0, sv0 = smm.v, ssv.v

                    @blk.tensor
                    def _(t):
                        t.wait_ge(sld.h, pm_ld)
                        t.wait_ge(ssv.h, sv0)
                        for idx, (which, oc, qt) in enumerate(rp_slices):
                            if idx >= 4:
                                t.wait_ge(ssv.h, sv0 + idx - 3)
                            src_ = (q1 if which == "q" else k1)[:, oc, qt * 512:(qt + 1) * 512]
                            ins = t.matmul(ps_f[:, (idx % 4) * 512:(idx % 4) * 512 + 512],
                                           pmt, src_, start=True, stop=True)
                            smm.inc(ins)

                    @blk.vector
                    def _(v):
                        for idx, (which, oc, qt) in enumerate(rp_slices):
                            v.wait_ge(smm.h, mm0 + idx + 1)
                            sl_ = (q1 if which == "q" else k1)[:, oc, qt * 512:(qt + 1) * 512]
                            pb = ps_f[:, (idx % 4) * 512:(idx % 4) * 512 + 512]
                            v.tensor_mul(sl_, sl_, rcos[:, qt * 512:(qt + 1) * 512])
                            v.tensor_mul(rtmp, pb, rsin[:, qt * 512:(qt + 1) * 512])
                            v.drain()
                            ssv.inc(v.tensor_add(sl_, sl_, rtmp))
                            v.drain()
                rope_k = ssv.v
                ab.freea("pmt"); ab.freea("rtmp")
                ab.freea("rcos"); ab.freea("rsin")

            with nc.named_scope("bias8"):
                qbar = ab.alloc("qbar", (DC, 64))
                kbar0 = ab.alloc("kbar0", (2, 64))
                kbarg = ab.alloc("kbarg", (4, 64))
                m8b = ab.alloc("m8b", (8,))
                mg = ab.alloc("mg", (1024,))
                mr8 = ab.alloc("mr8", (8,))
                m8f = af.alloc("m8f", (8,))
                mr8f = af.alloc("mr8f", (8,))
                bias8 = af.alloc("bias8", (8,))
                with nc.Block() as blk:
                    mm0, sv0, sd0 = smm.v, ssv.v, sl2.v

                    @blk.vector
                    def _(v):
                        v.wait_ge(ssv.h, rope_k)
                        with nc.allow_low_precision(reason="coarse softmax-shift estimate only"):
                            v.reduce_sum(qbar, q1.rearrange("p h (bb t) -> p h bb t", t=32),
                                         axis=AX.X)
                            ssv.inc(v.reduce_sum(kbar0,
                                                 k1.rearrange("p h (bb t) -> p h bb t", t=32),
                                                 axis=AX.X))
                        v.wait_ge(smm.h, mm0 + 1)
                        v.reduce_max(m8f, ps_g[:, 0:512].rearrange("p (h q) -> p h q", q=64),
                                     axis=AX.X)
                        v.drain()
                        ssv.inc(v.tensor_copy(m8b, m8f))
                        v.wait_ge(sl2.h, sd0 + 9 * 16)
                        v.reduce_max(mr8f[0:1, :], mg.rearrange("p (q h) -> p h q", h=8)[0:1, :, :],
                                     axis=AX.X)
                        v.drain()
                        ssv.inc(v.tensor_copy(mr8[0:1, :], mr8f[0:1, :]))
                        v.wait_ge(smm.h, mm0 + 2)
                        ssv.inc(v.tensor_scalar(bias8, ps_g[:, 512:520], -1.0 / 1024.0, -8.0,
                                                ALU.mult, ALU.add))

                    @blk.sync
                    def _(sy):
                        sy.wait_ge(ssv.h, sv0 + 1)
                        for g_ in range(4):
                            for half in range(2):
                                sy.wait_ge(sl2.h, sl2.v)
                                sl2.inc(sy.dma_start(
                                    out=kbarg[half * 64:half * 64 + 64, g_, :],
                                    in_=kbar0[(g_ % 2) * 64:(g_ % 2) * 64 + 64, g_ // 2, :]), 16)
                        sy.wait_ge(ssv.h, sv0 + 2)
                        sy.wait_ge(sl2.h, sl2.v)
                        sl2.inc(sy.dma_start(out=mg[0:1, :], in_=m8b[:, :]), 16)

                    @blk.tensor
                    def _(t):
                        t.wait_ge(sl2.h, sd0 + 8 * 16)
                        for hp in range(8):
                            g_ = hp // 2
                            t.matmul(ps_g[0:64, hp * 64:(hp + 1) * 64],
                                     qbar[0:64, hp, :], kbarg[0:64, g_, :],
                                     start=True, stop=True, tile_position=(0, 0))
                            ins = t.matmul(ps_g[64:128, hp * 64:(hp + 1) * 64],
                                           qbar[64:128, hp, :], kbarg[64:128, g_, :],
                                           start=True, stop=True, tile_position=(64, 64))
                        smm.inc(ins)
                        t.wait_ge(ssv.h, sv0 + 3)
                        ins = t.matmul(ps_g[:, 512:520], onesrow[0:1, 0:128], mr8[0:1, :],
                                       start=True, stop=True)
                        smm.inc(ins)
                bias_done = ssv.v

            with nc.named_scope("kvag"):
                with nc.Block() as blk:
                    sg0, sd0 = ssg.v, sl2.v

                    @blk.sync
                    def _(sy):
                        sy.wait_ge(ssv.h, rope_k)
                        sy.wait_ge(sl2.h, sl2.v)
                        sl2.inc(sy.dma_start(out=k_ag_in.ap().rearrange("(c p) s -> p c s", p=128),
                                             in_=k1), 16)
                        sy.wait_ge(sl2.h, sl2.v)
                        sl2.inc(sy.dma_start(out=v_ag_in.ap().rearrange("(c p) f -> p c f", p=128),
                                             in_=vtok), 16)

                    @blk.gpsimd
                    def _(g):
                        g.wait_ge(sl2.h, sd0 + 32)
                        ssg.inc(g.collective_compute(
                            "AllGather", ALU.bypass, replica_groups=pair,
                            ins=[k_ag_in.ap().opt()], outs=[k_ag_out.ap().opt()]))
                        ssg.inc(g.collective_compute(
                            "AllGather", ALU.bypass, replica_groups=pair,
                            ins=[v_ag_in.ap().opt()], outs=[v_ag_out.ap().opt()]))
                ag_done = ssg.v
                ab.freea("vtok"); ab.freea("k1")
                for nm in ("qbar", "kbar0", "kbarg", "m8b", "mg", "mr8"):
                    ab.freea(nm)

            pst_stack.close()
            psx_stack = ExitStack()
            ps_x = psx_stack.enter_context(nc.psum_tensor("PSX", [128, 1024], F32))

            with nc.named_scope("mloop"):
                kdup = ab.alloc("kdup", (2, 4096))
                vaug = ab.alloc("vaug", (2, 32, 65))
                bufP = ab.alloc("bufP", (2, 2, 1024))
                obuf = ab.alloc("obuf", (DC, TL))
                zfm = af.alloc("zfm", (2048,))
                g_ld = []

                with nc.Block() as blk:
                    ld0 = sld.v

                    @blk.gpsimd
                    def _(g):
                        g.wait_ge(ssg.h, ag_done)
                        for g_ in range(2):
                            for half in range(2):
                                for bh in range(2):
                                    g.wait_ge(sld.h, sld.v)
                                    sld.inc(g.dma_start(
                                        out=kdup[half * 64:half * 64 + 64, g_ % 2,
                                                 bh * 2048:(bh + 1) * 2048],
                                        in_=k_ag_out[bh * 256 + g_ * 64: bh * 256 + (g_ + 1) * 64, :]), 16)
                            g.wait_ge(sld.h, sld.v)
                            sld.inc(g.dma_start(
                                out=vaug[:, g_ % 2, :, 0:64],
                                in_=v_ag_out.ap().rearrange("(sc p) f -> p sc f", p=128)
                                [:, :, g_ * 64:(g_ + 1) * 64]), 16)
                            g_ld.append(sld.v)

                    @blk.vector
                    def _(v):
                        v.memset(vaug[:, 0, :, 64:65], 1.0)
                        ssv.inc(v.memset(vaug[:, 1, :, 64:65], 1.0))
                ones_set = ssv.v

                with nc.Block() as blk:
                    mm0, sv0, ss0, zd0 = smm.v, ssv.v, sss.v, sl2.v
                    gmark = [mm0 + (gg + 1) * 4 * 96 for gg in range(4)]

                    @blk.gpsimd
                    def _(g):
                        for g_ in range(2, 4):
                            g.wait_ge(smm.h, gmark[g_ - 2])
                            for half in range(2):
                                for bh in range(2):
                                    g.wait_ge(sld.h, sld.v)
                                    sld.inc(g.dma_start(
                                        out=kdup[half * 64:half * 64 + 64, g_ % 2,
                                                 bh * 2048:(bh + 1) * 2048],
                                        in_=k_ag_out[bh * 256 + g_ * 64: bh * 256 + (g_ + 1) * 64, :]), 16)
                            g.wait_ge(sld.h, sld.v)
                            sld.inc(g.dma_start(
                                out=vaug[:, g_ % 2, :, 0:64],
                                in_=v_ag_out.ap().rearrange("(sc p) f -> p sc f", p=128)
                                [:, :, g_ * 64:(g_ + 1) * 64]), 16)
                            g_ld.append(sld.v)

                    @blk.tensor
                    def _(t):
                        t.wait_ge(ssv.h, ones_set)
                        t.wait_ge(ssv.h, bias_done)
                        for hp in range(8):
                            g_ = hp // 2
                            gp = g_ % 2
                            t.wait_ge(sld.h, g_ld[g_])
                            for qh in range(2):
                                it = hp * 2 + qh
                                base_t = mm0 + it * 96
                                base_s = ss0 + it * 64
                                for sc in range(32):
                                    for hx in range(2):
                                        if sc >= 1:
                                            t.wait_ge(sss.h, base_s + 2 * (sc - 1) + hx + 1)
                                        elif it >= 1 and hx == 0:
                                            t.wait_ge(sss.h, base_s)
                                        q_sl = q1[hx * 64:hx * 64 + 64, hp,
                                                  qh * 1024:(qh + 1) * 1024]
                                        k_sl = kdup[hx * 64:hx * 64 + 64, gp,
                                                    sc * 128:(sc + 1) * 128]
                                        for q2 in range(2):
                                            ins = t.matmul(
                                                ps_f[:, (hx * 2 + q2) * 512:(hx * 2 + q2 + 1) * 512],
                                                k_sl, q_sl[:, q2 * 512:(q2 + 1) * 512],
                                                start=True, stop=True,
                                                tile_position=(hx * 64, 0))
                                        smm.inc(ins)
                                    if sc >= 1:
                                        if sc == 1 and it >= 1:
                                            t.wait_ge(ssv.h, sv0 + it * 4)
                                        t.wait_ge(sss.h, base_s + 2 * sc)
                                        for hx in range(2):
                                            for q2 in range(2):
                                                pv = (ps_g if hx == 0 else ps_x)
                                                ins = t.matmul(
                                                    pv[0:65, q2 * 512:(q2 + 1) * 512],
                                                    vaug[:, gp, sc - 1, :],
                                                    bufP[:, (sc - 1) % 2, hx, q2 * 512:(q2 + 1) * 512],
                                                    start=(sc - 1 == 0), stop=False)
                                        smm.inc(ins)
                                t.wait_ge(sss.h, base_s + 64)
                                for hx in range(2):
                                    for q2 in range(2):
                                        pv = (ps_g if hx == 0 else ps_x)
                                        ins = t.matmul(pv[0:65, q2 * 512:(q2 + 1) * 512],
                                                       vaug[:, gp, 31, :],
                                                       bufP[:, 31 % 2, hx, q2 * 512:(q2 + 1) * 512],
                                                       start=False, stop=True)
                                smm.inc(ins)

                    @blk.scalar
                    def _(s):
                        for hp in range(8):
                            for qh in range(2):
                                it = hp * 2 + qh
                                base_t = mm0 + it * 96
                                base_s = ss0 + it * 64
                                for sc in range(32):
                                    for hx in range(2):
                                        sw = (base_t + hx + 1 if sc == 0
                                              else base_t + 2 + 3 * (sc - 1) + hx + 1)
                                        s.wait_ge(smm.h, sw)
                                        sss.inc(s.activation(
                                            bufP[:, sc % 2, hx, :],
                                            ps_f[:, hx * 1024:(hx + 1) * 1024], AF.Exp,
                                            bias=bias8[:, hp:hp + 1]))

                    @blk.vector
                    def _(v):
                        for hp in range(8):
                            for qh in range(2):
                                it = hp * 2 + qh
                                base_t = mm0 + it * 96
                                v.wait_ge(smm.h, base_t + 96)
                                if it >= 1:
                                    v.wait_ge(sl2.h, zd0 + it * 32)
                                v.tensor_copy(zfm[64:65, 0:1024], ps_g[64:65, 0:1024])
                                v.tensor_copy(zfm[64:65, 1024:2048], ps_x[64:65, 0:1024])
                                for hx in range(2):
                                    pv = (ps_g if hx == 0 else ps_x)
                                    for q2 in range(2):
                                        col = qh * 1024 + q2 * 512
                                        ssv.inc(v.tensor_copy(
                                            obuf[hx * 64:hx * 64 + 64, hp, col:col + 512],
                                            pv[0:64, q2 * 512:(q2 + 1) * 512]))

                    @blk.sync
                    def _(sy):
                        for hp in range(8):
                            for qh in range(2):
                                it = hp * 2 + qh
                                sy.wait_ge(ssv.h, sv0 + it * 4 + 4)
                                for hx in range(2):
                                    h = 2 * hp + hx
                                    sy.wait_ge(sl2.h, sl2.v)
                                    sl2.inc(sy.dma_start(
                                        out=zfm[h:h + 1, qh * 1024:(qh + 1) * 1024],
                                        in_=zfm[64:65, hx * 1024:(hx + 1) * 1024]), 16)
                att5 = ssv.v
                zfm_done = sl2.v
                ab.freea("kdup"); ab.freea("vaug"); ab.freea("bufP")

            with nc.named_scope("norm"):
                wo1 = ab.alloc("wo1", (DC, D))
                bor = ab.alloc("bor", (D,))
                zrb = ab.alloc("zrb", (2048,))
                zr0 = ab.alloc("zr0", (2048,))
                with nc.Block() as blk:
                    ld0 = sld.v

                    @blk.gpsimd
                    def _(g):
                        g.wait_ge(sld.h, sld.v)
                        sld.inc(g.dma_start(out=wo1, in_=w_l1o.ap().rearrange("(i p) o -> p i o", p=128)), 16)
                        g.wait_ge(sld.h, sld.v)
                        sld.inc(g.dma_start(out=bor[0:1, :], in_=b_l1o[:, :]), 16)
                wo_ld = sld.v

                with nc.Block() as blk:
                    mm0, sv0, sd0 = smm.v, ssv.v, sl2.v

                    @blk.vector
                    def _(v):
                        v.wait_ge(ssv.h, att5)
                        v.wait_ge(sl2.h, zfm_done)
                        v.reciprocal(zfm[0:16, :], zfm[0:16, :])
                        v.drain()
                        ssv.inc(v.tensor_copy(zrb[0:16, :], zfm[0:16, :]))
                        for h in range(16):
                            v.wait_ge(smm.h, mm0 + h + 1)
                            hb = (h % 2) * 64
                            for qt in range(4):
                                pv = ps_g if qt < 2 else ps_x
                                ins = v.tensor_mul(
                                    obuf[hb:hb + 64, h // 2, qt * 512:(qt + 1) * 512],
                                    obuf[hb:hb + 64, h // 2, qt * 512:(qt + 1) * 512],
                                    pv[hb:hb + 64, (qt % 2) * 512:(qt % 2) * 512 + 512])
                            ssv.inc(ins)

                    @blk.sync
                    def _(sy):
                        for h in range(16):
                            sy.wait_ge(ssv.h, sv0 + 1)
                            sy.wait_ge(sl2.h, sl2.v)
                            if h >= 1:
                                sy.wait_ge(smm.h, mm0 + h)
                            sl2.inc(sy.dma_start(out=zr0[0:1, :], in_=zrb[h:h + 1, :]), 16)

                    @blk.tensor
                    def _(t):
                        t.wait_ge(sld.h, wo_ld)
                        for h in range(16):
                            t.wait_ge(sl2.h, sd0 + 16 * (h + 1))
                            if h >= 1:
                                t.wait_ge(ssv.h, sv0 + h + 1)
                            hb = (h % 2) * 64
                            for qt in range(4):
                                pv = ps_g if qt < 2 else ps_x
                                ins = t.matmul(pv[hb:hb + 64, (qt % 2) * 512:(qt % 2) * 512 + 512],
                                               onesrow[0:1, 0:64],
                                               zr0[0:1, qt * 512:(qt + 1) * 512],
                                               start=True, stop=True,
                                               tile_position=(0, hb))
                            smm.inc(ins)
                zdone = ssv.v
                ab.freea("zrb")

                with nc.Block() as blk:
                    mm0, sv0 = smm.v, ssv.v

                    @blk.tensor
                    def _(t):
                        t.wait_ge(ssv.h, zdone)
                        for tc in range(NTC):
                            tp = tc % 2
                            if tc >= 2:
                                t.wait_ge(ssv.h, sv0 + tc - 1)
                            for nn in range(2):
                                pb = ps_f[:, (tp * 2 + nn) * 512:(tp * 2 + nn + 1) * 512]
                                for ic in range(DC):
                                    t.matmul(pb, obuf[:, ic, tc * 128:(tc + 1) * 128],
                                             wo1[:, ic, nn * 512:(nn + 1) * 512],
                                             start=(ic == 0), stop=False)
                                ins = t.matmul(pb, onesrow[0:1, 0:128],
                                               bor[0:1, nn * 512:(nn + 1) * 512],
                                               start=False, stop=True)
                            smm.inc(ins)

                    @blk.vector
                    def _(v):
                        for tc in range(NTC):
                            tp = tc % 2
                            v.wait_ge(smm.h, mm0 + tc + 1)
                            ssv.inc(v.tensor_add(XS[:, tc, :], XS[:, tc, :],
                                                 ps_f[:, tp * 1024:(tp + 1) * 1024]))

                for nm in ("q1", "obuf", "wo1", "bor", "zr0"):
                    ab.freea(nm)
                for nm in ("bq1", "bq81", "bk1", "m8f", "mr8f", "bias8", "zfm"):
                    af.freea(nm)
                psx_stack.close()
            with nc.Block() as blk:
                @blk.sync
                def _(sy):
                    tap("stream_p5", XS, (128, NTC, D), sy, waits=[(ssv, ssv.v)])


        def emit_l1_ffn(pair):
            pst2 = stack.enter_context(nc.psum_tensor("PST2", [128, 2048], BF16))
            cn1 = ab.alloc("cn1", (DC, D))
            cn2 = ab.alloc("cn2", (DC, 512))
            cn3 = ab.alloc("cn3", (4, 256))
            cn4 = ab.alloc("cn4", (2,))
            gc1 = ab.alloc("gc1", (DC, 512))
            gc2 = ab.alloc("gc2", (4, 256))
            gc3 = ab.alloc("gc3", (2,))
            onescol = ab.alloc("onescol", (1,))
            gxr = ab.alloc("gxr", (1024,))
            gfm = ab.alloc("gfm", (8,))
            g1b = ab.alloc("g1b", (4,))
            g2b = ab.alloc("g2b", (2,))
            abrow = ab.alloc("abrow", (3,))
            xg = ab.alloc("xg", (2, D))
            cb1 = af.alloc("cb1", (8,)); cb2 = af.alloc("cb2", (4,)); cb3 = af.alloc("cb3", (2,))
            b4r = af.alloc("b4r", (1,)); gb1 = af.alloc("gb1", (4,)); gb2 = af.alloc("gb2", (2,))
            gb3 = af.alloc("gb3", (1,)); arr = af.alloc("arr", (1,))
            tiny = af.alloc("tiny", (8,))
            acbc = af.alloc("acbc", (3,))
            garow = af.alloc("garow", (1024,)); gbrow = af.alloc("gbrow", (1024,))

            with nc.Block() as blk:
                @blk.gpsimd
                def _(g):
                    for dst, srcd in ((cn1, w_cn1), (cn2, w_cn2), (cn3, w_cn3),
                                      (gc1, w_gc1), (gc2, w_gc2)):
                        g.wait_ge(sld.h, sld.v)
                        sld.inc(g.dma_start(out=dst,
                                            in_=srcd.ap().rearrange("(i p) o -> p i o", p=128)), 16)
                    for dst, srcd in ((cn4, w_cn4), (gc3, w_gc3)):
                        g.wait_ge(sld.h, sld.v)
                        sld.inc(g.dma_start(out=dst, in_=srcd[:, :]), 16)

                @blk.sync
                def _(sy):
                    for dst, srcd in ((cb1, b_cn1), (cb2, b_cn2), (cb3, b_cn3),
                                      (gb1, b_gc1), (gb2, b_gc2)):
                        sy.wait_ge(sl2.h, sl2.v)
                        sl2.inc(sy.dma_start(out=dst, in_=srcd[:, :]), 16)
                    for dst, srcd in ((b4r, b_cn4), (gb3, b_gc3), (arr, a_rng)):
                        sy.wait_ge(sl2.h, sl2.v)
                        sl2.inc(sy.dma_start(out=dst[0:1, :], in_=srcd[:, :]), 16)

                @blk.vector
                def _(v):
                    ssv.inc(v.memset(onescol, 1.0))
                    v.wait_ge(sld.h, sld.v)
                    v.tensor_scalar_mul(cn4, cn4, 0.5)
                    ssv.inc(v.tensor_scalar_mul(gc3, gc3, 0.5))
            w7_ld, b7_ld = sld.v, sl2.v
            st4 = emit_ln_stats(xdep=(ssv, ssv.v))

            with nc.Block() as blk:
                mm0, sv0, sd0 = smm.v, ssv.v, sl2.v

                @blk.vector
                def _(v):
                    v.wait_ge(ssv.h, st4)
                    for tc in range(NTC):
                        tp = tc % 2
                        if tc >= 2:
                            v.wait_ge(smm.h, mm0 + tc - 1)
                        ssv.inc(v.tensor_scalar(xg[:, tp, :], XS[:, tc, :],
                                                stats[:, 7, tc:tc + 1], stats[:, 9, tc:tc + 1],
                                                ALU.mult, ALU.add))
                    v.wait_ge(smm.h, mm0 + NTC)
                    ssv.inc(v.tensor_copy(garow[0:1, :], ps_g[0:1, 0:1024]))

                @blk.tensor
                def _(t):
                    for tc in range(NTC):
                        tp = tc % 2
                        t.wait_ge(ssv.h, sv0 + tc + 1)
                        for nn in range(2):
                            ins = t.matmul(ps_g[0:1, nn * 512:(nn + 1) * 512],
                                           onescol[:, 0:1], xg[:, tp, nn * 512:(nn + 1) * 512],
                                           start=(tc == 0), stop=(tc == NTC - 1))
                        smm.inc(ins)

                @blk.sync
                def _(sy):
                    sy.wait_ge(ssv.h, sv0 + NTC + 1)
                    sy.wait_ge(sl2.h, sl2.v)
                    sl2.inc(sy.dma_start(out=gm_ag_in[:, :], in_=garow[0:1, :]), 16)

                @blk.gpsimd
                def _(g):
                    g.wait_ge(sl2.h, sd0 + 16)
                    ssg.inc(g.collective_compute(
                        "AllGather", ALU.bypass, replica_groups=pair,
                        ins=[gm_ag_in.ap().opt()], outs=[gm_ag_out.ap().opt()]))
                    g.wait_ge(ssg.h, ssg.v)
                    g.wait_ge(sld.h, sld.v)
                    sld.inc(g.dma_start(out=garow[0:1, :], in_=gm_ag_out[0:1, :]), 16)
                    g.wait_ge(sld.h, sld.v)
                    sld.inc(g.dma_start(out=gbrow[0:1, :], in_=gm_ag_out[1:2, :]), 16)
            gm_ld = sld.v
            if P7SUB < 1:
                return

            with nc.Block() as blk:
                mm0, sv0, ss0, sd0 = smm.v, ssv.v, sss.v, sl2.v

                @blk.vector
                def _(v):
                    v.wait_ge(sld.h, gm_ld)
                    v.wait_ge(sl2.h, b7_ld)
                    v.tensor_add(garow[0:1, :], garow[0:1, :], gbrow[0:1, :])
                    v.drain()
                    ssv.inc(v.tensor_scalar(gxr[0:1, :], garow[0:1, :], 1.0 / 4096.0, 0.0,
                                            ALU.mult, ALU.add))
                    v.tensor_scalar(tiny[0:1, 6:7], b4r[0:1, 0:1], 0.5, 0.0, ALU.mult, ALU.add)
                    ssv.inc(v.tensor_scalar(tiny[0:1, 7:8], gb3[0:1, 0:1], 0.5, 0.0,
                                            ALU.mult, ALU.add))
                    v.wait_ge(smm.h, mm0 + 1)
                    ssv.inc(v.tensor_copy(gfm, ps_g[:, 700:708]))
                    v.wait_ge(sss.h, ss0 + 7)
                    v.tensor_mul(tiny[0:1, 0:1], arr[0:1, 0:1], tiny[0:1, 2:3])
                    v.drain()
                    v.tensor_scalar(tiny[0:1, 4:5], tiny[0:1, 0:1], 0.7, 0.1, ALU.mult, ALU.add)
                    v.tensor_scalar(tiny[0:1, 5:6], arr[0:1, 0:1], 0.3, 0.0, ALU.mult, ALU.add)
                    v.drain()
                    ssv.inc(v.tensor_copy(abrow[0:1, :], tiny[0:1, 4:7]))
                    v.wait_ge(smm.h, mm0 + 5)
                    ssv.inc(v.tensor_copy(acbc, ps_g[:, 600:603]))

                @blk.tensor
                def _(t):
                    t.wait_ge(sld.h, w7_ld)
                    t.wait_ge(ssv.h, sv0 + 1)
                    for j in range(8):
                        ins = t.matmul(ps_g[:, 700 + j:701 + j],
                                       gxr[0:1, j * 128:(j + 1) * 128],
                                       onesrow[0:1, 0:1], start=True, stop=True)
                    smm.inc(ins)
                    t.wait_ge(ssv.h, sv0 + 3)
                    for oc in range(4):
                        for ic in range(DC):
                            ins = t.matmul(ps_f[:, oc * 512:oc * 512 + 1],
                                           gc1[:, ic, oc * 128:(oc + 1) * 128],
                                           gfm[:, ic:ic + 1],
                                           start=(ic == 0), stop=(ic == DC - 1))
                    smm.inc(ins)
                    t.wait_ge(sss.h, ss0 + 4)
                    for oc in range(2):
                        for ic in range(4):
                            ins = t.matmul(ps_g[:, oc * 512:oc * 512 + 1],
                                           gc2[:, ic, oc * 128:(oc + 1) * 128],
                                           g1b[:, ic:ic + 1], start=(ic == 0), stop=(ic == 3))
                    smm.inc(ins)
                    t.wait_ge(sss.h, ss0 + 6)
                    for ic in range(2):
                        ins = t.matmul(ps_f[0:1, 1024:1025], gc3[:, ic:ic + 1],
                                       g2b[:, ic:ic + 1], start=(ic == 0), stop=(ic == 1))
                    smm.inc(ins)
                    t.wait_ge(ssv.h, sv0 + 4)
                    ins = t.matmul(ps_g[:, 600:603], onesrow[0:1, 0:128], abrow[0:1, :],
                                   start=True, stop=True)
                    smm.inc(ins)

                @blk.scalar
                def _(s):
                    s.wait_ge(smm.h, mm0 + 2)
                    for oc in range(4):
                        sss.inc(s.activation(g1b[:, oc:oc + 1], ps_f[:, oc * 512:oc * 512 + 1],
                                             AF.Gelu, bias=gb1[:, oc:oc + 1]))
                    s.wait_ge(smm.h, mm0 + 3)
                    for oc in range(2):
                        sss.inc(s.activation(g2b[:, oc:oc + 1], ps_g[:, oc * 512:oc * 512 + 1],
                                             AF.Gelu, bias=gb2[:, oc:oc + 1]))
                    s.wait_ge(smm.h, mm0 + 4)
                    s.wait_ge(ssv.h, sv0 + 2)
                    sss.inc(s.activation(tiny[0:1, 2:3], ps_f[0:1, 1024:1025], AF.Tanh,
                                         bias=tiny[0:1, 7:8]))
            ab_done = ssv.v
            if P7SUB < 2:
                return

            af.freea("fscr"); af.freea("garow"); af.freea("gbrow")
            osf = af.alloc("osf", (2, 1024))
            otok = ab.alloc("otok", (2, D))
            outT = ab.alloc("outT", (DC, 256))
            h1b = ab.alloc("h1b", (DC, 256))
            h2b = ab.alloc("h2b", (4, 256))
            h3b = ab.alloc("h3b", (2, 256))
            ucol = af.alloc("ucol", (2,))
            ccol = af.alloc("ccol", (2,))
            ncol = af.alloc("ncol", (2,))
            twq = af.alloc("twq", (512,))
            DBG["ab7"] = dict(ab.live); DBG["af7"] = dict(af.live)

            for sl in range(min(8, P7SUB - 1)):
                tc0 = sl * 2
                with nc.Block() as blk:
                    mm0, sv0, ss0 = smm.v, ssv.v, sss.v

                    @blk.vector
                    def _(v):
                        v.wait_ge(ssv.h, ab_done)
                        for i in range(2):
                            ssv.inc(v.tensor_scalar(osf[:, i, :], XS[:, tc0 + i, :],
                                                    stats[:, 7, tc0 + i:tc0 + i + 1],
                                                    stats[:, 9, tc0 + i:tc0 + i + 1],
                                                    ALU.mult, ALU.add))
                        v.drain()
                        for cyc in range(3):
                            cm = mm0 + cyc * 18
                            cs = ss0 + cyc * 16
                            for i in range(2):
                                ssv.inc(v.tensor_copy(otok[:, i, :], osf[:, i, :]))
                            for i in range(2):
                                v.wait_ge(smm.h, cm + i + 1)
                                ssv.inc(v.tensor_copy(
                                    outT[:, :, i * 128:(i + 1) * 128],
                                    pst2[:, (i % 2) * 1024:(i % 2) * 1024 + 1024]
                                    .rearrange("p (f q) -> p f q", q=128)))
                            for i in range(2):
                                v.wait_ge(sss.h, cs + 14 + i + 1)
                                v.tensor_scalar(ccol[:, i:i + 1], ucol[:, i:i + 1],
                                                acbc[:, 1:2], acbc[:, 0:1],
                                                ALU.mult, ALU.add)
                                v.drain()
                                v.tensor_scalar_mul(ncol[:, i:i + 1], ccol[:, i:i + 1], -1.0)
                                os_i = osf[:, i, :]
                                v.tensor_copy(twq, os_i[:, 0:512])
                                v.drain()
                                v.scalar_tensor_tensor(os_i[:, 0:256], os_i[:, 512:768],
                                                       ccol[:, i:i + 1], os_i[:, 0:256],
                                                       ALU.mult, ALU.add)
                                v.scalar_tensor_tensor(os_i[:, 256:512], os_i[:, 768:1024],
                                                       ncol[:, i:i + 1], os_i[:, 256:512],
                                                       ALU.mult, ALU.add)
                                v.drain()
                                v.scalar_tensor_tensor(os_i[:, 512:768], twq[:, 0:256],
                                                       ncol[:, i:i + 1], os_i[:, 512:768],
                                                       ALU.mult, ALU.add)
                                ssv.inc(v.scalar_tensor_tensor(
                                    os_i[:, 768:1024], twq[:, 256:512],
                                    ccol[:, i:i + 1], os_i[:, 768:1024],
                                    ALU.mult, ALU.add))
                                v.drain()
                        for i in range(2):
                            ssv.inc(v.tensor_add(XS[:, tc0 + i, :], XS[:, tc0 + i, :],
                                                 osf[:, i, :]))

                    @blk.tensor
                    def _(t):
                        t.wait_ge(sld.h, w7_ld)
                        for cyc in range(3):
                            cb = sv0 + 2 + cyc * 6
                            cm = mm0 + cyc * 18
                            cs = ss0 + cyc * 16
                            for i in range(2):
                                t.wait_ge(ssv.h, cb + i + 1)
                                for fc in range(DC):
                                    ins = t.transpose(
                                        pst2[:, (i % 2) * 1024 + fc * 128:
                                             (i % 2) * 1024 + (fc + 1) * 128],
                                        otok[:, i, fc * 128:(fc + 1) * 128], ident)
                                smm.inc(ins)
                            t.wait_ge(ssv.h, cb + 4)
                            for oc in range(DC):
                                if oc >= 4:
                                    t.wait_ge(sss.h, cs + oc - 4 + 1)
                                pb = ps_f[:, (oc % 4) * 512:(oc % 4) * 512 + 256]
                                for ic in range(DC):
                                    ins = t.matmul(pb, cn1[:, ic, oc * 128:(oc + 1) * 128],
                                                   outT[:, ic, :], start=(ic == 0),
                                                   stop=(ic == DC - 1))
                                smm.inc(ins)
                            t.wait_ge(sss.h, cs + 8)
                            for oc in range(4):
                                if oc >= 2:
                                    t.wait_ge(sss.h, cs + 8 + oc - 2 + 1)
                                pb = ps_g[:, (oc % 2) * 512:(oc % 2) * 512 + 256]
                                for ic in range(DC):
                                    ins = t.matmul(pb, cn2[:, ic, oc * 128:(oc + 1) * 128],
                                                   h1b[:, ic, :], start=(ic == 0),
                                                   stop=(ic == DC - 1))
                                smm.inc(ins)
                            t.wait_ge(sss.h, cs + 12)
                            for oc in range(2):
                                pb = ps_f[:, (oc % 2) * 512:(oc % 2) * 512 + 256]
                                for ic in range(4):
                                    ins = t.matmul(pb, cn3[:, ic, oc * 128:(oc + 1) * 128],
                                                   h2b[:, ic, :], start=(ic == 0),
                                                   stop=(ic == 3))
                                smm.inc(ins)
                            t.wait_ge(sss.h, cs + 14)
                            for i in range(2):
                                for ic in range(2):
                                    ins = t.matmul(ps_g[:, 300 + i * 600:301 + i * 600],
                                                   h3b[:, ic, i * 128:(i + 1) * 128],
                                                   cn4[:, ic:ic + 1],
                                                   start=(ic == 0), stop=(ic == 1))
                                smm.inc(ins)

                    @blk.scalar
                    def _(s):
                        for cyc in range(3):
                            cm = mm0 + cyc * 18
                            for oc in range(DC):
                                s.wait_ge(smm.h, cm + 2 + oc + 1)
                                sss.inc(s.activation(h1b[:, oc, :],
                                                     ps_f[:, (oc % 4) * 512:(oc % 4) * 512 + 256],
                                                     AF.Gelu, bias=cb1[:, oc:oc + 1]))
                            for oc in range(4):
                                s.wait_ge(smm.h, cm + 10 + oc + 1)
                                sss.inc(s.activation(h2b[:, oc, :],
                                                     ps_g[:, (oc % 2) * 512:(oc % 2) * 512 + 256],
                                                     AF.Gelu, bias=cb2[:, oc:oc + 1]))
                            for oc in range(2):
                                s.wait_ge(smm.h, cm + 14 + oc + 1)
                                sss.inc(s.activation(h3b[:, oc, :],
                                                     ps_f[:, (oc % 2) * 512:(oc % 2) * 512 + 256],
                                                     AF.Gelu, bias=cb3[:, oc:oc + 1]))
                            for i in range(2):
                                s.wait_ge(smm.h, cm + 16 + i + 1)
                                sss.inc(s.activation(ucol[:, i:i + 1],
                                                     ps_g[:, 300 + i * 600:301 + i * 600],
                                                     AF.Tanh, bias=acbc[:, 2:3]))

            for nm in ("cn1", "cn2", "cn3", "cn4", "gc1", "gc2", "gc3", "onescol",
                       "gxr", "gfm", "g1b", "g2b", "abrow", "xg", "otok",
                       "outT", "h1b", "h2b", "h3b"):
                ab.freea(nm)
            for nm in ("cb1", "cb2", "cb3", "b4r", "gb1", "gb2", "gb3", "arr",
                       "tiny", "acbc", "osf", "ucol", "ccol", "ncol", "twq"):
                af.freea(nm)
            with nc.Block() as blk:
                @blk.sync
                def _(sy):
                    tap("stream_p7", XS, (128, NTC, D), sy, waits=[(ssv, ssv.v)])

        pg = [[0, 1], [2, 3], [4, 5], [6, 7]]
        if upto >= 4:
            with nc.named_scope("p5"):
                emit_l1_attn(pg)
        if upto >= 5:
            with nc.named_scope("p7"):
                emit_l1_ffn(pg)

        with nc.Block() as blk:
            @blk.sync
            def _(sy):
                sy.wait_ge(ssv.h, ssv.v)
                sy.wait_ge(smm.h, smm.v)
                sy.wait_ge(sss.h, sss.v)
                sy.wait_ge(sld.h, sld.v)
                sy.wait_ge(sl2.h, sl2.v)
                sst.inc(sy.dma_start(out=out_d.ap().rearrange("(c p) f -> p c f", p=128),
                                     in_=XS), 16)
                sy.wait_ge(sst.h, sst.v)

    return nc, tap_outs


def kernel(**inputs):
    maps = _host_prep(inputs)
    nc, _ = build()
    res = run_bass_kernel_spmd(nc, maps, core_ids=list(range(NCORE)))
    out = np.concatenate([r["out"] for r in res.results], axis=0)
    return out.reshape(4, 4096, D).astype(np.float32)


def dev_run(inputs, taps=(), upto=99):
    maps = _host_prep(inputs)
    nc, tap_outs = build(taps=taps, upto=upto)
    res = run_bass_kernel_spmd(nc, maps, core_ids=list(range(NCORE)))
    return res.results


# version 05
# speedup vs baseline: 1.3301x; 1.0289x over previous
import math
from contextlib import ExitStack

import numpy as np
import ml_dtypes

import concourse.bass as bass
import concourse.mybir as mybir
from concourse.bass_utils import run_bass_kernel_spmd

F32 = mybir.dt.float32
BF16 = mybir.dt.bfloat16
AF = mybir.ActivationFunctionType
ALU = mybir.AluOpType
AX = mybir.AxisListType
BF = ml_dtypes.bfloat16

NCORE = 8
DBG = {}
P7SUB = 99
TL = 2048
NTC = 16
D = 1024
DC = 8
E = 2048
EPS = 1e-5


def _pp(b):
    b = np.asarray(b, np.float32)
    return np.ascontiguousarray(b.reshape(b.size // 128, 128).T)


def _row(b):
    return np.ascontiguousarray(np.asarray(b, np.float32).reshape(1, -1))


def _rope_tables(core):
    tau = (core % 2) * 2048 + np.arange(2048, dtype=np.float64)
    inv = 1.0 / 10000.0 ** (np.arange(0, 64, 2, dtype=np.float64) / 64.0)
    fr = tau[:, None] * inv[None, :]
    sin, cos = np.sin(fr), np.cos(fr)
    p = np.arange(128)
    j = (p % 64) % 32
    sgn = np.where((p % 64) < 32, -1.0, 1.0)
    cos_t = cos[:, j].T
    sin_t = sin[:, j].T * sgn[:, None]
    return {"rope_cos": np.ascontiguousarray(cos_t.astype(BF)),
            "rope_sin": np.ascontiguousarray(sin_t.astype(BF))}


def _host_prep(I):
    x = np.ascontiguousarray(np.asarray(I["L0_attn_x"], np.float32).reshape(16384, D))
    shared = {
        "l0_wq": np.asarray(I["L0_attn_Wq"], np.float32),
        "l0_bq": _pp(I["L0_attn_bq"]),
        "l0_wdown": np.asarray(I["L0_attn_Wdown"], np.float32),
        "l0_bdown": _pp(I["L0_attn_bdown"]),
        "l0_wup": np.asarray(I["L0_attn_Wup"], np.float32),
        "l0_bup": _pp(I["L0_attn_bup"]),
        "l0_wout": np.asarray(I["L0_attn_Wout"], np.float32),
        "l0_bout": _row(I["L0_attn_bout"]),
        "phi_wT": np.ascontiguousarray(np.asarray(I["L0_ffn_phi_w"], np.float32).T),
        "phi_b": _row(I["L0_ffn_phi_b"]),
        "l1_wq": np.ascontiguousarray(np.asarray(I["L1_attn_Wq"], np.float32).T),
        "l1_bq": _pp(I["L1_attn_bq"]),
        "l1_wk": np.ascontiguousarray(np.asarray(I["L1_attn_Wk"], np.float32).T),
        "l1_bk": _pp(I["L1_attn_bk"]),
        "l1_wv": np.ascontiguousarray(np.asarray(I["L1_attn_Wv"], np.float32).T),
        "l1_bv": _row(I["L1_attn_bv"]),
        "l1_wo": np.ascontiguousarray(np.asarray(I["L1_attn_Wo"], np.float32).T),
        "l1_bo": _row(I["L1_attn_bo"]),
        "cn_w1": np.asarray(I["L1_ffn_cn_w1"], np.float32), "cn_b1": _pp(I["L1_ffn_cn_b1"]),
        "cn_w2": np.asarray(I["L1_ffn_cn_w2"], np.float32), "cn_b2": _pp(I["L1_ffn_cn_b2"]),
        "cn_w3": np.asarray(I["L1_ffn_cn_w3"], np.float32), "cn_b3": _pp(I["L1_ffn_cn_b3"]),
        "cn_w4": _pp(I["L1_ffn_cn_w4"]), "cn_b4": _row(I["L1_ffn_cn_b4"]),
        "gc_w1": np.asarray(I["L1_ffn_gc_w1"], np.float32), "gc_b1": _pp(I["L1_ffn_gc_b1"]),
        "gc_w2": np.asarray(I["L1_ffn_gc_w2"], np.float32), "gc_b2": _pp(I["L1_ffn_gc_b2"]),
        "gc_w3": _pp(I["L1_ffn_gc_w3"]), "gc_b3": _row(I["L1_ffn_gc_b3"]),
        "a_range": _row(I["L1_ffn_adaptive_range"]),
    }
    exp_w = np.asarray(I["L0_ffn_expert_w"], np.float32)
    exp_b = np.asarray(I["L0_ffn_expert_b"], np.float32)
    maps = []
    for c in range(NCORE):
        m = dict(shared)
        m["x"] = np.ascontiguousarray(x[TL * c: TL * (c + 1)])
        m["exp_w"] = np.ascontiguousarray(exp_w[8 * c: 8 * c + 8])
        m["exp_b"] = np.ascontiguousarray(exp_b[8 * c: 8 * c + 8])
        m.update(_rope_tables(c))
        maps.append(m)
    return maps



class Arena:
    def __init__(self, nc, stack, name, words, dtype):
        self.t = stack.enter_context(nc.sbuf_tensor(name, [128, words], dtype))
        self.free_list = [(0, words)]
        self.live = {}
        self.hiwater = 0

    def alloc(self, name, shape):
        n = int(np.prod(shape))
        for i, (off, sz) in enumerate(self.free_list):
            if sz >= n:
                if sz == n:
                    self.free_list.pop(i)
                else:
                    self.free_list[i] = (off + n, sz - n)
                self.live[name] = (off, tuple(shape))
                self.hiwater = max(self.hiwater, off + n)
                return self.view(name)
        raise MemoryError(f"arena {self.t.name}: no {n} for {name} free={self.free_list} "
                          f"live={self.live}")

    def view(self, name):
        off, shape = self.live[name]
        n = int(np.prod(shape))
        ap = self.t[:, off:off + n]
        if len(shape) > 1:
            spec = "p (" + " ".join(f"d{i}" for i in range(len(shape))) + ") -> p " + \
                   " ".join(f"d{i}" for i in range(len(shape)))
            ap = ap.rearrange(spec, **{f"d{i}": s for i, s in enumerate(shape)})
        return ap

    def alias(self, src_name, new_name, shape):
        off, _ = self.live[src_name]
        assert int(np.prod(shape)) <= int(np.prod(self.live[src_name][1]))
        self.live[new_name] = (off, tuple(shape))
        return self.view(new_name)

    def freea(self, name):
        off, shape = self.live.pop(name)
        self.free_list.append((off, int(np.prod(shape))))
        self.free_list.sort()
        merged = []
        for o, s in self.free_list:
            if merged and merged[-1][0] + merged[-1][1] == o:
                merged[-1] = (merged[-1][0], merged[-1][1] + s)
            else:
                merged.append((o, s))
        self.free_list = [tuple(m) for m in merged]


class Sem:
    def __init__(self, nc, stack, name):
        self.h = stack.enter_context(nc.semaphore(name))
        self.v = 0

    def inc(self, instr, n=1):
        instr.then_inc(self.h, n)
        self.v += n
        return self.v


def build(taps=(), upto=99, races=True):
    nc = bass.Bass("TRN2", target_bir_lowering=False, detect_race_conditions=races)
    taps = set(taps)

    x_d = nc.dram_tensor("x", [TL, D], F32, kind="ExternalInput")
    out_d = nc.dram_tensor("out", [TL, D], F32, kind="ExternalOutput")

    def din(name, shape):
        return nc.dram_tensor(name, list(shape), F32, kind="ExternalInput")

    w_l0q = din("l0_wq", (D, D)); b_l0q = din("l0_bq", (128, 8))
    w_l0d = din("l0_wdown", (D, 256)); b_l0d = din("l0_bdown", (128, 2))
    w_l0u = din("l0_wup", (256, E)); b_l0u = din("l0_bup", (128, 16))
    w_l0o = din("l0_wout", (D, D)); b_l0o = din("l0_bout", (1, D))
    w_phi = din("phi_wT", (D, E)); b_phi = din("phi_b", (1, E))
    w_expw = din("exp_w", (8, D, D)); w_expb = din("exp_b", (8, D))
    w_l1q = din("l1_wq", (D, D)); b_l1q = din("l1_bq", (128, 8))
    w_l1k = din("l1_wk", (D, 256)); b_l1k = din("l1_bk", (128, 2))
    w_l1v = din("l1_wv", (D, 256)); b_l1v = din("l1_bv", (1, 256))
    w_l1o = din("l1_wo", (D, D)); b_l1o = din("l1_bo", (1, D))
    w_cn1 = din("cn_w1", (D, D)); b_cn1 = din("cn_b1", (128, 8))
    w_cn2 = din("cn_w2", (D, 512)); b_cn2 = din("cn_b2", (128, 4))
    w_cn3 = din("cn_w3", (512, 256)); b_cn3 = din("cn_b3", (128, 2))
    w_cn4 = din("cn_w4", (128, 2)); b_cn4 = din("cn_b4", (1, 1))
    w_gc1 = din("gc_w1", (D, 512)); b_gc1 = din("gc_b1", (128, 4))
    w_gc2 = din("gc_w2", (512, 256)); b_gc2 = din("gc_b2", (128, 2))
    w_gc3 = din("gc_w3", (128, 2)); b_gc3 = din("gc_b3", (1, 1))
    a_rng = din("a_range", (1, 1))
    rcos_d = nc.dram_tensor("rope_cos", [128, TL], BF16, kind="ExternalInput")
    rsin_d = nc.dram_tensor("rope_sin", [128, TL], BF16, kind="ExternalInput")

    ident_d = nc.inline_tensor(np.eye(128, dtype=BF), "ident")
    r_t = np.arange(128) % 8
    bd = (r_t[:, None] == r_t[None, :]).astype(BF)
    mask_d = nc.inline_tensor(np.ascontiguousarray(bd), "bdmask")
    ones_row_d = nc.inline_tensor(np.ones((1, 128), dtype=BF), "onesrow")
    _pj = np.arange(128)
    _j64 = _pj % 64
    _h64 = (_pj // 64) * 64
    _src = np.where(_j64 < 32, _h64 + 2 * _j64 + 1, _h64 + 2 * (_j64 - 32))
    _pm = np.zeros((128, 128), dtype=BF)
    _pm[_src, _pj] = 1.0
    pmat_d = nc.inline_tensor(np.ascontiguousarray(_pm), "permmat")

    c_dram = nc.dram_tensor("c_scr", [TL, E], BF16)
    d_dram = nc.dram_tensor("d_scr", [TL, E], BF16)
    xpart_dram = nc.dram_tensor("xpart_scr", [E, D], F32)
    xown_dram = nc.dram_tensor("xown_scr", [E // 8, D], F32)
    yown_dram = nc.dram_tensor("yown_scr", [E // 8, D], BF16)
    ytil_dram = nc.dram_tensor("ytil_scr", [E, D], BF16)
    k_ag_in = nc.dram_tensor("k_ag_in", [256, 2048], BF16)
    k_ag_out = nc.dram_tensor("k_ag_out", [512, 2048], BF16)
    v_ag_in = nc.dram_tensor("v_ag_in", [2048, 256], BF16)
    v_ag_out = nc.dram_tensor("v_ag_out", [4096, 256], BF16)
    gm_ag_in = nc.dram_tensor("gm_ag_in", [1, D], F32)
    gm_ag_out = nc.dram_tensor("gm_ag_out", [2, D], F32)

    tap_outs = {}
    stack = ExitStack()
    with stack:
        ab = Arena(nc, stack, "AB", 62 * 1024, BF16)
        af = Arena(nc, stack, "AF", 4 * 1024, F32)
        xs_t = stack.enter_context(nc.sbuf_tensor("XS", [128, NTC, D], F32))
        XS = xs_t[:, :, :]

        ps_f = stack.enter_context(nc.psum_tensor("PSF", [128, 2048], F32))
        ps_g = stack.enter_context(nc.psum_tensor("PSG", [128, 1024], F32))
        pst_stack = ExitStack()
        ps_t = pst_stack.enter_context(nc.psum_tensor("PST", [128, 2048], BF16))

        sld = Sem(nc, stack, "sld")
        sl2 = Sem(nc, stack, "sl2")
        sl3 = Sem(nc, stack, "sl3")
        smm = Sem(nc, stack, "smm")
        ssv = Sem(nc, stack, "ssv")
        sss = Sem(nc, stack, "sss")
        ssg = Sem(nc, stack, "ssg")
        sst = Sem(nc, stack, "sst")
        sdt = [Sem(nc, stack, f"sdt{i}") for i in range(4)]

        ident = ab.alloc("ident", (128,))
        maskbd = ab.alloc("maskbd", (128,))
        onesrow = ab.alloc("onesrow", (128,))
        rcos = ab.alloc("rcos", (TL,))
        rsin = ab.alloc("rsin", (TL,))

        stats = af.alloc("stats", (16, 16))
        fscr = af.alloc("fscr", (1024,))

        def tap(name, ap, shape, engine, waits=()):
            if name not in taps:
                return
            t_d = nc.dram_tensor("tap_" + name, list(shape), ap.dtype, kind="ExternalOutput")
            tap_outs[name] = t_d
            for s, v in waits:
                engine.wait_ge(s.h, v)
            sst.inc(engine.dma_start(out=t_d.ap(), in_=ap), 16)

        with nc.Block() as blk:
            @blk.sync
            def _(sy):
                sl2.inc(sy.dma_start(out=XS, in_=x_d.ap().rearrange("(c p) f -> p c f", p=128)), 16)

            @blk.gpsimd
            def _(g):
                for dst, src in ((ident, ident_d[:, :]), (maskbd, mask_d[:, :]),
                                 (onesrow[0:1, :], ones_row_d[:, :]),
                                 (rcos, rcos_d[:, :]), (rsin, rsin_d[:, :])):
                    sld.inc(g.dma_start(out=dst, in_=src), 16)
        consts_ld = sld.v

        def emit_ln_stats(xdep=None):
            with nc.Block() as blk:
                sv0, ss0 = ssv.v, sss.v

                @blk.vector
                def _(v):
                    v.wait_ge(sl2.h, 16)
                    if xdep is not None:
                        v.wait_ge(xdep[0].h, xdep[1])
                    st = stats
                    for tc in range(NTC):
                        v.tensor_scalar(fscr, XS[:, tc, :], 1.0, 0.0, ALU.mult,
                                        ALU.add, accum_out=st[:, 0, tc:tc + 1])
                        v.drain()
                        v.scalar_tensor_tensor(fscr, XS[:, tc, :], 1.0, XS[:, tc, :],
                                               ALU.mult, ALU.mult,
                                               accum_out=st[:, 1, tc:tc + 1])
                        v.drain()
                    v.tensor_scalar_mul(st[:, 2, :], st[:, 0, :], 1.0 / D)
                    v.drain()
                    v.scalar_tensor_tensor(st[:, 3, :], st[:, 2, :], 1.0, st[:, 2, :],
                                           ALU.mult, ALU.mult)
                    v.tensor_scalar_mul(st[:, 4, :], st[:, 1, :], 1.0 / D)
                    v.drain()
                    v.tensor_sub(st[:, 5, :], st[:, 4, :], st[:, 3, :])
                    v.drain()
                    v.tensor_scalar_add(st[:, 5, :], st[:, 5, :], EPS)
                    v.drain()
                    ssv.inc(v.reciprocal(st[:, 6, :], st[:, 5, :]))
                    v.wait_ge(sss.h, ss0 + 1)
                    v.tensor_mul(st[:, 8, :], st[:, 2, :], st[:, 7, :])
                    v.drain()
                    ssv.inc(v.tensor_scalar_mul(st[:, 9, :], st[:, 8, :], -1.0))

                @blk.scalar
                def _(s):
                    s.wait_ge(ssv.h, sv0 + 1)
                    sss.inc(s.activation(stats[:, 7, :], stats[:, 6, :], AF.Sqrt))
            return ssv.v

        def emit_xln_slice(tc0, ntc, tokbuf, fmbuf, stats_dep, fm_off=0):
            with nc.Block() as blk:
                sv0, mm0 = ssv.v, smm.v

                @blk.vector
                def _(v):
                    v.wait_ge(ssv.h, stats_dep)
                    for i in range(ntc):
                        tc = tc0 + i
                        ssv.inc(v.tensor_scalar(tokbuf[:, i, :], XS[:, tc, :],
                                                stats[:, 7, tc:tc + 1], stats[:, 9, tc:tc + 1],
                                                ALU.mult, ALU.add))
                    for i in range(ntc):
                        v.wait_ge(smm.h, mm0 + i + 1)
                        bank = (i % 2) * 1024
                        ssv.inc(v.tensor_copy(
                            fmbuf[:, :, fm_off + i * 128:fm_off + (i + 1) * 128],
                            ps_t[:, bank:bank + DC * 128].rearrange("p (f q) -> p f q", q=128)))

                @blk.tensor
                def _(t):
                    t.wait_ge(sld.h, sld.v)
                    t.wait_ge(ssv.h, sv0)
                    for i in range(ntc):
                        t.wait_ge(ssv.h, sv0 + i + 1)
                        if i >= 2:
                            t.wait_ge(ssv.h, sv0 + ntc + i - 1)
                        bank = (i % 2) * 1024
                        for fc in range(DC):
                            ins = t.transpose(ps_t[:, bank + fc * 128: bank + (fc + 1) * 128],
                                              tokbuf[:, i, fc * 128:(fc + 1) * 128], ident)
                        smm.inc(ins)
            return ssv.v

        def emit_l0_attn(stats_dep):
            TS = 256
            NSL = TL // TS
            wq_sb = ab.alloc("wq_sb", (DC, D))
            wdn_sb = ab.alloc("wdn_sb", (DC, 256))
            wup_sb = ab.alloc("wup_sb", (2, E))
            wout_sb = ab.alloc("wout_sb", (DC, D))
            boutr = ab.alloc("boutr", (D,))
            bq_f = af.alloc("bq_f", (8,))
            bq8_f = af.alloc("bq8_f", (8,))
            bdn_f = af.alloc("bdn_f", (2,))
            bup_f = af.alloc("bup_f", (16,))
            with nc.Block() as blk:
                sd0 = sl2.v

                @blk.gpsimd
                def _(g):
                    sld.inc(g.dma_start(out=wq_sb, in_=w_l0q.ap().rearrange("(i p) o -> p i o", p=128)), 16)
                    sld.inc(g.dma_start(out=wdn_sb, in_=w_l0d.ap().rearrange("(i p) o -> p i o", p=128)), 16)
                    sld.inc(g.dma_start(out=wup_sb, in_=w_l0u.ap().rearrange("(i p) o -> p i o", p=128)), 16)
                    sld.inc(g.dma_start(out=wout_sb, in_=w_l0o.ap().rearrange("(i p) o -> p i o", p=128)), 16)
                    sld.inc(g.dma_start(out=boutr[0:1, :], in_=b_l0o[:, :]), 16)

                @blk.sync
                def _(sy):
                    sl2.inc(sy.dma_start(out=bq_f, in_=b_l0q[:, :]), 16)
                    sl2.inc(sy.dma_start(out=bdn_f, in_=b_l0d[:, :]), 16)
                    sl2.inc(sy.dma_start(out=bup_f, in_=b_l0u[:, :]), 16)

                @blk.vector
                def _(v):
                    v.wait_ge(sl2.h, sd0 + 48)
                    ssv.inc(v.tensor_scalar_mul(bq8_f, bq_f, 0.125))
            wld, bld, b8 = sld.v, sl2.v, ssv.v

            xtok_sl = ab.alloc("xtok_sl", (2, D))
            xlnT_sl = ab.alloc("xlnT_sl", (DC, TS))
            q_fm = ab.alloc("q_fm", (DC, TS))
            kv_fm = ab.alloc("kv_fm", (16, TS))
            lat_fm = ab.alloc("lat_fm", (2, TS))
            rt1 = ab.alloc("rt1", (16, TS))
            rt2 = ab.alloc("rt2", (16, TS))
            bufA = ab.alloc("bufA", (2, 1024))
            bufB = ab.alloc("bufB", (2, 1024))
            ptb = ab.alloc("ptb", (2, 1024))
            vgb = ab.alloc("vgb", (2, 8, 64))
            odmj = ab.alloc("odmj", (2, 1024))
            o_fm = ab.alloc("o_fm", (DC, TS))
            submr = af.alloc("submr", (2, 1024))
            mrow = af.alloc("mrow", (2, 8))
            zrow = af.alloc("zrow", (2, 8))
            zrec = af.alloc("zrec", (2, 8))
            q_grp = ab.alloc("q_grp", (32, 64))
            kg2 = ab.alias("rt1", "kg2", (32, 128))
            vgrpb = ab.alias("rt2", "vgrpb", (32, 128))

            for sl in range(NSL):
                t0 = sl * TS
                emit_xln_slice(sl * 2, 2, ab.view("xtok_sl"), ab.view("xlnT_sl"), stats_dep)
                xlnT_v = ab.view("xlnT_sl")

                with nc.Block() as blk:
                    mm0, sv0 = smm.v, ssv.v

                    @blk.tensor
                    def _(t):
                        t.wait_ge(sld.h, wld)
                        t.wait_ge(ssv.h, sv0)
                        for oc in range(DC):
                            if oc >= 4:
                                t.wait_ge(ssv.h, sv0 + (oc - 4) + 1)
                            pb = ps_f[:, (oc % 4) * 512:(oc % 4) * 512 + TS]
                            for ic in range(DC):
                                ins = t.matmul(pb, wq_sb[:, ic, oc * 128:(oc + 1) * 128],
                                               xlnT_v[:, ic, :],
                                               start=(ic == 0), stop=(ic == DC - 1))
                            smm.inc(ins)
                        for oc in range(2):
                            pb = ps_g[:, oc * 512: oc * 512 + TS]
                            for ic in range(DC):
                                ins = t.matmul(pb, wdn_sb[:, ic, oc * 128:(oc + 1) * 128],
                                               xlnT_v[:, ic, :],
                                               start=(ic == 0), stop=(ic == DC - 1))
                            smm.inc(ins)
                        t.wait_ge(ssv.h, sv0 + 10)
                        for oc in range(16):
                            if oc >= 4:
                                t.wait_ge(ssv.h, sv0 + 10 + (oc - 4) + 1)
                            pb = ps_f[:, (oc % 4) * 512:(oc % 4) * 512 + TS]
                            for ic in range(2):
                                ins = t.matmul(pb, wup_sb[:, ic, oc * 128:(oc + 1) * 128],
                                               lat_fm[:, ic, :],
                                               start=(ic == 0), stop=(ic == 1))
                            smm.inc(ins)

                    @blk.vector
                    def _(v):
                        v.wait_ge(sl2.h, bld)
                        v.wait_ge(ssv.h, b8)
                        for oc in range(DC):
                            v.wait_ge(smm.h, mm0 + oc + 1)
                            pb = ps_f[:, (oc % 4) * 512:(oc % 4) * 512 + TS]
                            ssv.inc(v.tensor_scalar(q_fm[:, oc, :], pb, 0.125,
                                                    bq8_f[:, oc:oc + 1], ALU.mult, ALU.add))
                        for oc in range(2):
                            v.wait_ge(smm.h, mm0 + 8 + oc + 1)
                            pb = ps_g[:, oc * 512: oc * 512 + TS]
                            ssv.inc(v.tensor_scalar(lat_fm[:, oc, :], pb,
                                                    bdn_f[:, oc:oc + 1], 0.0, ALU.add, ALU.add))
                        for oc in range(16):
                            v.wait_ge(smm.h, mm0 + 10 + oc + 1)
                            pb = ps_f[:, (oc % 4) * 512:(oc % 4) * 512 + TS]
                            ssv.inc(v.tensor_scalar(kv_fm[:, oc, :], pb,
                                                    bup_f[:, oc:oc + 1], 0.0, ALU.add, ALU.add))

                with nc.Block() as blk:
                    sv0, sd0 = ssv.v, sl2.v

                    @blk.sync
                    def _(sy):
                        sy.wait_ge(ssv.h, sv0)
                        for dst0, src0 in ((0, 32), (32, 0), (64, 96), (96, 64)):
                            sl2.inc(sy.dma_start(out=rt1[dst0:dst0 + 32, 0:8, :],
                                                 in_=q_fm[src0:src0 + 32, :, :]), 16)
                        for dst0, src0 in ((0, 32), (32, 0)):
                            sl2.inc(sy.dma_start(out=rt2[dst0:dst0 + 32, :, :],
                                                 in_=kv_fm[src0:src0 + 32, :, :]), 16)

                    @blk.vector
                    def _(v):
                        cq = rcos[:, t0:t0 + TS].unsqueeze(1)
                        sq = rsin[:, t0:t0 + TS].unsqueeze(1)
                        ck = rcos[0:64, t0:t0 + TS].unsqueeze(1)
                        sk = rsin[0:64, t0:t0 + TS].unsqueeze(1)
                        v.wait_ge(sl2.h, sd0 + 96)
                        v.tensor_mul(rt1[:, 8:16, :], rt1[:, 0:8, :],
                                     sq.broadcast_to((128, 8, TS)))
                        v.drain()
                        v.tensor_mul(rt1[:, 0:8, :], q_fm, cq.broadcast_to((128, 8, TS)))
                        v.drain()
                        ssv.inc(v.tensor_add(q_fm, rt1[:, 0:8, :], rt1[:, 8:16, :]))
                        v.drain()
                        v.tensor_mul(rt1[0:64, :, :], rt2[0:64, :, :],
                                     sk.broadcast_to((64, 16, TS)))
                        v.drain()
                        v.tensor_mul(rt2[0:64, :, :], kv_fm[0:64, :, :],
                                     ck.broadcast_to((64, 16, TS)))
                        v.drain()
                        ssv.inc(v.tensor_add(kv_fm[0:64, :, :], rt2[0:64, :, :],
                                             rt1[0:64, :, :]))
                        v.drain()
                        v.tensor_copy(q_grp[0:64, :, :].rearrange("p g (c t) -> p g c t", t=8),
                                      q_fm[0:64, :, :].rearrange("p c (g t) -> p g c t", t=8))
                        v.tensor_copy(q_grp[64:128, :, :].rearrange("p g (c t) -> p g c t", t=8),
                                      q_fm[64:128, :, :].rearrange("p c (g t) -> p g c t", t=8))
                        ssv.inc(v.tensor_copy(
                            kg2[0:64, :, :].rearrange("p g (gg t) -> p g gg t", t=8),
                            kv_fm[0:64, :, :].rearrange("p gg (g t) -> p g gg t", t=8)))
                        ssv.inc(v.tensor_copy(
                            vgrpb[0:64, :, :].rearrange("p g (gg t) -> p g gg t", t=8),
                            kv_fm[64:128, :, :].rearrange("p gg (g t) -> p g gg t", t=8)))

                    @blk.gpsimd
                    def _(g):
                        g.wait_ge(ssv.h, sv0 + 3)
                        sld.inc(g.dma_start(out=kg2[64:128, :, :], in_=kg2[0:64, :, :]), 16)

                rope_sv, rope_ld = ssv.v, sld.v

                NB = 4
                with nc.Block() as blk:
                    mm0, sv0, ss0 = smm.v, ssv.v, sss.v

                    @blk.tensor
                    def _(t):
                        t.wait_ge(ssv.h, rope_sv)
                        t.wait_ge(sld.h, rope_ld)
                        for b in range(NB):
                            b2 = b % 2
                            if b >= 2:
                                t.wait_ge(ssv.h, sv0 + 5 * (b - 2) + 1)
                            for j in range(8):
                                grp = b * 8 + j
                                pb = ps_f[:, b2 * 1024 + j * 128: b2 * 1024 + (j + 1) * 128]
                                t.matmul(pb[0:64, :], q_grp[0:64, grp, :],
                                         kg2[0:64, grp, :],
                                         start=True, stop=True, tile_position=(0, 0))
                                ins = t.matmul(pb[64:128, :], q_grp[64:128, grp, :],
                                               kg2[64:128, grp, :],
                                               start=True, stop=True, tile_position=(64, 64))
                            smm.inc(ins)
                            t.wait_ge(ssv.h, sv0 + 5 * b + 2)
                            if b >= 1:
                                t.wait_ge(ssv.h, sv0 + 5 * (b - 1) + 3)
                            for j in range(8):
                                ins = t.transpose(ps_t[:, j * 128:(j + 1) * 128],
                                                  bufA[:, b2, j * 128:(j + 1) * 128], ident)
                            smm.inc(ins)
                            if b >= 1:
                                t.wait_ge(ssv.h, sv0 + 5 * (b - 1) + 4)
                            for j in range(8):
                                grp = b * 8 + j
                                ins = t.transpose(ps_t[:, 1024 + j * 64: 1024 + (j + 1) * 64],
                                                  vgrpb[0:64, grp, :],
                                                  ident[0:64, 0:64])
                            smm.inc(ins)
                            t.wait_ge(ssv.h, sv0 + 5 * b + 4)
                            if b >= 1:
                                t.wait_ge(ssv.h, sv0 + 5 * (b - 1) + 5)
                            for j in range(8):
                                ins = t.matmul(ps_g[0:64, j * 128:(j + 1) * 128],
                                               vgb[:, b2, j, :],
                                               ptb[:, b2, j * 128:(j + 1) * 128],
                                               start=True, stop=True)
                            smm.inc(ins)

                    @blk.vector
                    def _(v):
                        v.wait_ge(sld.h, rope_ld)
                        for b in range(NB):
                            b2 = b % 2
                            if b >= 2:
                                v.wait_ge(sss.h, ss0 + (b - 2) + 1)
                            v.wait_ge(smm.h, mm0 + 4 * b + 1)
                            for hh in range(2):
                                pbv = ps_f[:, b2 * 1024 + hh * 512: b2 * 1024 + (hh + 1) * 512]\
                                    .rearrange("p (g q) -> p g q", q=128)
                                v.reduce_max(mrow[:, b2, hh * 4:(hh + 1) * 4], pbv, axis=AX.X,
                                             negate=True)
                                v.drain()
                                ins = v.tensor_add(
                                    submr[:, b2, hh * 512:(hh + 1) * 512]
                                    .rearrange("p (g q) -> p g q", q=128), pbv,
                                    mrow[:, b2, hh * 4:(hh + 1) * 4].unsqueeze(2)
                                    .broadcast_to((128, 4, 128)))
                            ssv.inc(ins)
                            v.wait_ge(sss.h, ss0 + b + 1)
                            va = bufA[:, b2, :].rearrange("p (g q) -> p g q", q=128)
                            vb = bufB[:, b2, :].rearrange("p (g q) -> p g q", q=128)
                            v.tensor_mul(vb, va, maskbd.unsqueeze(1).broadcast_to((128, 8, 128)))
                            v.drain()
                            v.reduce_sum(zrow[:, b2, :], vb, axis=AX.X)
                            v.drain()
                            v.reciprocal(zrec[:, b2, :], zrow[:, b2, :])
                            v.drain()
                            ssv.inc(v.tensor_mul(va, vb, zrec[:, b2, :].unsqueeze(2)
                                                 .broadcast_to((128, 8, 128))))
                            v.wait_ge(smm.h, mm0 + 4 * b + 2)
                            ssv.inc(v.tensor_copy(ptb[:, b2, :], ps_t[:, 0:1024]))
                            v.wait_ge(smm.h, mm0 + 4 * b + 3)
                            ssv.inc(v.tensor_copy(
                                vgb[:, b2, :, :],
                                ps_t[:, 1024:1536].rearrange("p (j dd) -> p j dd", dd=64)))
                            v.wait_ge(smm.h, mm0 + 4 * b + 4)
                            bp = (b % 2) * 64
                            ssv.inc(v.tensor_copy(
                                odmj[bp:bp + 64, b // 2, :], ps_g[0:64, 0:1024]))

                    @blk.scalar
                    def _(s):
                        for b in range(NB):
                            b2 = b % 2
                            s.wait_ge(ssv.h, sv0 + 5 * b + 1)
                            if b >= 2:
                                s.wait_ge(smm.h, mm0 + 4 * (b - 2) + 2)
                            sss.inc(s.activation(bufA[:, b2, :], submr[:, b2, :], AF.Exp))

                att_sv = ssv.v

                with nc.Block() as blk:
                    mm0, sv0 = smm.v, ssv.v

                    @blk.vector
                    def _(v):
                        v.wait_ge(ssv.h, att_sv)
                        for h in range(16):
                            off = 64 * (h % 2) + (h // 2) * 8
                            hb = (h % 2) * 64
                            for bp in range(2):
                                src_ap = odmj[bp * 64:bp * 64 + 64, :, :].rearrange(
                                    "p bb (j r) -> p bb j r", r=128)[:, :, :, off:off + 8]
                                dst_ap = o_fm[hb:hb + 64, h // 2, :].rearrange(
                                    "p (bb q) -> p bb q", bb=2)[:, :, 64 * bp:64 * bp + 64].rearrange(
                                    "p bb (j t) -> p bb j t", t=8)
                                ssv.inc(v.tensor_copy(dst_ap, src_ap))
                        for ts_ in range(2):
                            v.wait_ge(smm.h, mm0 + ts_ + 1)
                            tci = sl * 2 + ts_
                            pg = ps_g[:, 0:1024] if ts_ == 0 else ps_f[:, 0:1024]
                            ssv.inc(v.tensor_add(XS[:, tci, :], XS[:, tci, :], pg))

                    @blk.tensor
                    def _(t):
                        t.wait_ge(ssv.h, sv0 + 32)
                        for ts_ in range(2):
                            pg = ps_g if ts_ == 0 else ps_f
                            for nn in range(2):
                                pb = pg[:, nn * 512:(nn + 1) * 512]
                                for ic in range(DC):
                                    t.matmul(pb, o_fm[:, ic, ts_ * 128:(ts_ + 1) * 128],
                                             wout_sb[:, ic, nn * 512:(nn + 1) * 512],
                                             start=(ic == 0), stop=False)
                                ins = t.matmul(pb, onesrow[0:1, 0:128],
                                               boutr[0:1, nn * 512:(nn + 1) * 512],
                                               start=False, stop=True)
                            smm.inc(ins)

            DBG["ab"] = dict(ab.live)
            DBG["af"] = dict(af.live)
            with nc.Block() as blk:
                @blk.sync
                def _(sy):
                    w = [(ssv, ssv.v), (smm, smm.v), (sss, sss.v)]
                    tap("p1_qfm", ab.view("q_fm"), (128, DC, 256), sy, waits=w)
                    tap("p1_kvfm", ab.view("kv_fm"), (128, 16, 256), sy, waits=w)
                    tap("p1_qgrp", ab.view("q_grp"), (128, 32, 64), sy, waits=w)
                    tap("p1_kg2", ab.view("kg2"), (128, 32, 128), sy, waits=w)
                    tap("p1_vgrpb", ab.view("vgrpb"), (128, 32, 128), sy, waits=w)
                    tap("p1_ofm", ab.view("o_fm"), (128, DC, 256), sy, waits=w)
                    tap("p1_odmj", ab.view("odmj"), (128, 2, 1024), sy, waits=w)
                    tap("p1_bufA", ab.view("bufA"), (128, 2, 1024), sy, waits=w)
            for nm in ("xtok_sl", "xlnT_sl", "q_fm", "kv_fm", "lat_fm", "rt1", "rt2",
                       "bufA", "bufB", "ptb", "vgb", "odmj", "o_fm", "q_grp",
                       "wq_sb", "wdn_sb", "wup_sb", "wout_sb", "boutr"):
                ab.freea(nm)
            ab.live.pop("kg2", None); ab.live.pop("vgrpb", None)
            for nm in ("bq_f", "bq8_f", "bdn_f", "bup_f", "submr", "mrow", "zrow", "zrec"):
                af.freea(nm)

        def emit_l0_ffn(stats_dep):
            xtok2 = ab.alloc("xtok2", (NTC, D))
            phi_sb = ab.alloc("phi_sb", (DC, E))
            phib_sb = ab.alloc("phib_sb", (E,))
            ebuf = ab.alloc("ebuf", (2, E))
            cstage = ab.alloc("cstage", (2, E))
            xlnT_tc = ab.alloc("xlnT_tc", (2, DC, 128))
            neg10 = af.alloc("neg10", (1,))
            sdrec = af.alloc("sdrec", (2, 64))
            screc = af.alloc("screc", (2, 32))

            with nc.Block() as blk:
                @blk.gpsimd
                def _(g):
                    sld.inc(g.dma_start(out=phi_sb, in_=w_phi.ap().rearrange("(i p) o -> p i o", p=128)), 16)
                    sld.inc(g.dma_start(out=phib_sb[0:1, :], in_=b_phi[:, :]), 16)

                @blk.vector
                def _(v):
                    v.wait_ge(ssv.h, stats_dep)
                    v.memset(neg10, -10.0)
                    for tc in range(NTC):
                        ssv.inc(v.tensor_scalar(xtok2[:, tc, :], XS[:, tc, :],
                                                stats[:, 7, tc:tc + 1], stats[:, 9, tc:tc + 1],
                                                ALU.mult, ALU.add))
            wld3 = sld.v
            xln2_done = ssv.v

            with nc.named_scope("logits"):
                with nc.Block() as blk:
                    mm0, sv0, ss0, sd0, sg0 = smm.v, ssv.v, sss.v, sl2.v, sld.v

                    @blk.tensor
                    def _(t):
                        t.wait_ge(sld.h, wld3)
                        t.wait_ge(ssv.h, xln2_done)
                        for tc in range(NTC):
                            tp = tc % 2
                            if tc >= 2:
                                t.wait_ge(ssv.h, sv0 + 3 * (tc - 2) + 1)
                            for fc in range(DC):
                                ins = t.transpose(ps_t[:, tp * 1024 + fc * 128: tp * 1024 + (fc + 1) * 128],
                                                  xtok2[:, tc, fc * 128:(fc + 1) * 128], ident)
                            smm.inc(ins)
                            t.wait_ge(ssv.h, sv0 + 3 * tc + 1)
                            if tc >= 1:
                                t.wait_ge(sss.h, ss0 + tc)
                            for nn in range(4):
                                pb = ps_f[:, nn * 512:(nn + 1) * 512]
                                for ic in range(DC):
                                    t.matmul(pb, xlnT_tc[:, tp, ic, :],
                                             phi_sb[:, ic, nn * 512:(nn + 1) * 512],
                                             start=(ic == 0), stop=False)
                                ins = t.matmul(pb, onesrow[0:1, 0:128],
                                               phib_sb[0:1, nn * 512:(nn + 1) * 512],
                                               start=False, stop=True)
                            smm.inc(ins)

                    @blk.scalar
                    def _(s):
                        for tc in range(NTC):
                            tp = tc % 2
                            s.wait_ge(smm.h, mm0 + 2 * tc + 2)
                            if tc >= 2:
                                s.wait_ge(sld.h, sg0 + 16 * (tc - 1))
                            sss.inc(s.activation(ebuf[:, tp, :], ps_f[:, 0:2048], AF.Exp,
                                                 bias=neg10))

                    @blk.vector
                    def _(v):
                        for tc in range(NTC):
                            tp = tc % 2
                            v.wait_ge(smm.h, mm0 + 2 * tc + 1)
                            ssv.inc(v.tensor_copy(
                                xlnT_tc[:, tp, :, :],
                                ps_t[:, tp * 1024:tp * 1024 + 1024].rearrange("p (f q) -> p f q", q=128)))
                            v.wait_ge(sss.h, ss0 + tc + 1)
                            ev = ebuf[:, tp, :].rearrange("p (n q) -> p n q", q=32)
                            ev2 = ebuf[:, tp, :].rearrange("p (n q) -> p q n", q=32)
                            v.reduce_sum(sdrec[:, tp, :], ev, axis=AX.X)
                            v.reduce_sum(screc[:, tp, :], ev2, axis=AX.X)
                            v.drain()
                            v.reciprocal(sdrec[:, tp, :], sdrec[:, tp, :])
                            v.reciprocal(screc[:, tp, :], screc[:, tp, :])
                            if tc >= 2:
                                v.wait_ge(sl2.h, sd0 + 16 * (tc - 1))
                            v.drain()
                            ssv.inc(v.tensor_mul(
                                cstage[:, tp, :].rearrange("p (n q) -> p n q", q=32),
                                ev, screc[:, tp, :].unsqueeze(1).broadcast_to((128, 64, 32))))
                            v.drain()
                            ssv.inc(v.tensor_mul(
                                ev, ev, sdrec[:, tp, :].unsqueeze(2).broadcast_to((128, 64, 32))))

                    @blk.sync
                    def _(sy):
                        for tc in range(NTC):
                            tp = tc % 2
                            sy.wait_ge(ssv.h, sv0 + 3 * tc + 2)
                            sy.wait_ge(sl2.h, sl2.v)
                            sl2.inc(sy.dma_start(
                                out=c_dram.ap().rearrange("(c p) e -> p c e", p=128)[:, tc, :],
                                in_=cstage[:, tp, :]), 16)

                    @blk.gpsimd
                    def _(g):
                        for tc in range(NTC):
                            tp = tc % 2
                            g.wait_ge(ssv.h, sv0 + 3 * tc + 3)
                            g.wait_ge(sld.h, sld.v)
                            sld.inc(g.dma_start(
                                out=d_dram.ap().rearrange("(c p) e -> p c e", p=128)[:, tc, :],
                                in_=ebuf[:, tp, :]), 16)

                d_dmas = (sl2.v, sld.v)
                for nm in ("phi_sb", "phib_sb", "ebuf", "cstage", "xlnT_tc"):
                    ab.freea(nm)

            with nc.named_scope("xpart"):
                dtile = ab.alloc("dtile", (4, 256))
                xstg = ab.alloc("xstg", (2, 2048))
                ab.freea("xstg")
                xstg = af.alloc("xstg", (2, 1024))
                with nc.Block() as blk:
                    mm0, sv0, sd0, st0 = smm.v, ssv.v, sl2.v, sl3.v

                    @blk.sync
                    def _(sy):
                        sy.wait_ge(sld.h, d_dmas[1])
                        for ps in range(8):
                            for tc in range(NTC):
                                idx = ps * NTC + tc
                                if idx >= 4:
                                    sy.wait_ge(smm.h, mm0 + (idx - 3))
                                sdt[idx % 4].inc(sy.dma_start(
                                    out=dtile[:, idx % 4, :],
                                    in_=d_dram[tc * 128:(tc + 1) * 128, ps * 256:(ps + 1) * 256]), 16)

                    @blk.tensor
                    def _(t):
                        for ps in range(8):
                            if ps >= 1:
                                t.wait_ge(ssv.h, sv0 + 2 * ps)
                            for tc in range(NTC):
                                idx = ps * NTC + tc
                                t.wait_ge(sdt[idx % 4].h, 16 * (idx // 4 + 1))
                                for ei in range(2):
                                    for nn in range(2):
                                        ins = t.matmul(
                                            ps_f[:, (ei * 2 + nn) * 512:(ei * 2 + nn + 1) * 512],
                                            dtile[:, idx % 4, ei * 128:(ei + 1) * 128],
                                            xtok2[:, tc, nn * 512:(nn + 1) * 512],
                                            start=(tc == 0), stop=(tc == NTC - 1))
                                smm.inc(ins)

                    @blk.vector
                    def _(v):
                        for ps in range(8):
                            v.wait_ge(smm.h, mm0 + NTC * (ps + 1))
                            for ei in range(2):
                                if ps >= 1:
                                    v.wait_ge(sl3.h, st0 + 16 * (2 * (ps - 1) + ei + 1))
                                ssv.inc(v.tensor_copy(xstg[:, ei, :],
                                                      ps_f[:, ei * 1024:(ei + 1) * 1024]))


                    @blk.scalar
                    def _(s):
                        for ps in range(8):
                            for ei in range(2):
                                s.wait_ge(ssv.h, sv0 + 2 * ps + ei + 1)
                                s.wait_ge(sl3.h, sl3.v)
                                sl3.inc(s.dma_start(
                                    out=xpart_dram[(2 * ps + ei) * 128:(2 * ps + ei + 1) * 128, :],
                                    in_=xstg[:, ei, :]), 16)

                xpart_done = sl3.v
                ab.freea("dtile"); ab.freea("xtok2")
                af.freea("xstg"); af.freea("neg10"); af.freea("sdrec"); af.freea("screc")

            with nc.named_scope("experts"):
                xn = ab.alloc("xn", (2, D))
                xnT = ab.alloc("xnT", (DC, 256))
                expw = ab.alloc("expw", (2, DC, D))
                expb_sb = ab.alloc("expb_sb", (8 * D,))
                ystg = ab.alloc("ystg", (2, D))
                xow = af.alloc("xow", (2, D))
                xst = af.alloc("xst", (12, 2))

                with nc.Block() as blk:
                    sg0, ld0 = ssg.v, sld.v

                    @blk.gpsimd
                    def _(g):
                        g.wait_ge(sl3.h, xpart_done)
                        ssg.inc(g.collective_compute(
                            "ReduceScatter", ALU.add, replica_groups=[list(range(NCORE))],
                            ins=[xpart_dram.ap().opt()], outs=[xown_dram.ap().opt()]))
                        g.wait_ge(ssg.h, sg0 + 1)
                        sld.inc(g.dma_start(out=xow,
                                            in_=xown_dram.ap().rearrange("(c p) f -> p c f", p=128)), 16)
                        sld.inc(g.dma_start(out=expb_sb[0:1, :],
                                            in_=w_expb.ap().rearrange("n h -> (n h)").unsqueeze(0)), 16)
                        g.wait_ge(sld.h, sld.v)
                        sld.inc(g.dma_start(out=expw[:, 0, :, :],
                                            in_=w_expw[0, :, :].rearrange("(i p) o -> p i o", p=128)), 16)
                        g.wait_ge(sld.h, sld.v)
                        sld.inc(g.dma_start(out=expw[:, 1, :, :],
                                            in_=w_expw[1, :, :].rearrange("(i p) o -> p i o", p=128)), 16)
                rs_ld = sld.v

                with nc.Block() as blk:
                    sv0, ss0, mm0 = ssv.v, sss.v, smm.v

                    @blk.vector
                    def _(v):
                        v.wait_ge(sld.h, rs_ld)
                        v.tensor_scalar(xow, xow, 65000.0, -33000.0, ALU.min, ALU.max)
                        v.drain()
                        for i in range(2):
                            v.tensor_scalar(fscr, xow[:, i, :], 1.0, 0.0, ALU.mult,
                                            ALU.add, accum_out=xst[:, 0, i:i + 1])
                            v.drain()
                            v.scalar_tensor_tensor(fscr, xow[:, i, :], 1.0, xow[:, i, :],
                                                   ALU.mult, ALU.mult,
                                                   accum_out=xst[:, 1, i:i + 1])
                            v.drain()
                        v.tensor_scalar_mul(xst[:, 2, :], xst[:, 0, :], 1.0 / D)
                        v.drain()
                        v.scalar_tensor_tensor(xst[:, 3, :], xst[:, 2, :], 1.0, xst[:, 2, :],
                                               ALU.mult, ALU.mult)
                        v.tensor_scalar_mul(xst[:, 4, :], xst[:, 1, :], 1.0 / D)
                        v.drain()
                        v.tensor_sub(xst[:, 5, :], xst[:, 4, :], xst[:, 3, :])
                        v.drain()
                        v.tensor_scalar_add(xst[:, 5, :], xst[:, 5, :], EPS)
                        v.drain()
                        ssv.inc(v.reciprocal(xst[:, 6, :], xst[:, 5, :]))
                        v.wait_ge(sss.h, ss0 + 1)
                        v.tensor_mul(xst[:, 8, :], xst[:, 2, :], xst[:, 7, :])
                        v.drain()
                        v.tensor_scalar_mul(xst[:, 9, :], xst[:, 8, :], -1.0)
                        v.drain()
                        for i in range(2):
                            ssv.inc(v.tensor_scalar(xn[:, i, :], xow[:, i, :],
                                                    xst[:, 7, i:i + 1], xst[:, 9, i:i + 1],
                                                    ALU.mult, ALU.add))
                        for i in range(2):
                            v.wait_ge(smm.h, mm0 + i + 1)
                            ssv.inc(v.tensor_copy(
                                xnT[:, :, i * 128:(i + 1) * 128],
                                ps_t[:, (i % 2) * 1024:(i % 2) * 1024 + 1024]
                                .rearrange("p (f q) -> p f q", q=128)))

                    @blk.scalar
                    def _(s):
                        s.wait_ge(ssv.h, sv0 + 1)
                        sss.inc(s.activation(xst[:, 7, :], xst[:, 6, :], AF.Sqrt))
                        for i in range(2):
                            s.wait_ge(ssv.h, sv0 + 1 + i + 1)
                            sss.inc(s.activation(xn[:, i, :], xn[:, i, :], AF.Tanh))

                    @blk.tensor
                    def _(t):
                        for i in range(2):
                            t.wait_ge(sss.h, ss0 + 1 + i + 1)
                            for fc in range(DC):
                                ins = t.transpose(
                                    ps_t[:, (i % 2) * 1024 + fc * 128:(i % 2) * 1024 + (fc + 1) * 128],
                                    xn[:, i, fc * 128:(fc + 1) * 128], ident)
                            smm.inc(ins)
                xnt_done = ssv.v

                with nc.Block() as blk:
                    mm0, sv0, st0, ld0 = smm.v, ssv.v, sl3.v, sld.v

                    @blk.gpsimd
                    def _(g):
                        for n in range(2, 8):
                            g.wait_ge(smm.h, mm0 + (n - 2) + 1)
                            g.wait_ge(sld.h, sld.v)
                            sld.inc(g.dma_start(out=expw[:, n % 2, :, :],
                                                in_=w_expw[n, :, :].rearrange("(i p) o -> p i o", p=128)), 16)

                    @blk.tensor
                    def _(t):
                        t.wait_ge(ssv.h, xnt_done)
                        for n in range(8):
                            t.wait_ge(sld.h, ld0 + 16 * max(0, n - 1) if n >= 2 else ld0)
                            if n >= 1:
                                t.wait_ge(ssv.h, sv0 + n)
                            for nn in range(2):
                                pb = ps_g[0:32, nn * 512:(nn + 1) * 512]
                                for ic in range(DC):
                                    t.matmul(pb, xnT[:, ic, n * 32:(n + 1) * 32],
                                             expw[:, n % 2, ic, nn * 512:(nn + 1) * 512],
                                             start=(ic == 0), stop=False)
                                ins = t.matmul(pb, onesrow[0:1, 0:32],
                                               expb_sb[0:1, n * D + nn * 512: n * D + (nn + 1) * 512],
                                               start=False, stop=True)
                            smm.inc(ins)

                    @blk.vector
                    def _(v):
                        for n in range(8):
                            v.wait_ge(smm.h, mm0 + n + 1)
                            if n >= 2:
                                v.wait_ge(sl3.h, st0 + 16 * (n - 1))
                            ssv.inc(v.tensor_copy(ystg[0:32, n % 2, :], ps_g[0:32, 0:1024]))

                    @blk.scalar
                    def _(s):
                        for n in range(8):
                            s.wait_ge(ssv.h, sv0 + n + 1)
                            s.wait_ge(sl3.h, sl3.v)
                            sl3.inc(s.dma_start(out=yown_dram[n * 32:(n + 1) * 32, :],
                                                in_=ystg[0:32, n % 2, :]), 16)
                yown_done = sl3.v
                for nm in ("xn", "xnT", "expw", "expb_sb", "ystg"):
                    ab.freea(nm)
                af.freea("xow"); af.freea("xst")

            with nc.named_scope("ymm"):
                ytil = ab.alloc("ytil", (NTC, D))
                ctt = ab.alloc("ctt", (NTC, E))
                with nc.Block() as blk:
                    sg0, ld0, sd0 = ssg.v, sld.v, sl2.v

                    @blk.gpsimd
                    def _(g):
                        g.wait_ge(sl3.h, yown_done)
                        ssg.inc(g.collective_compute(
                            "AllGather", ALU.bypass, replica_groups=[list(range(NCORE))],
                            ins=[yown_dram.ap().opt()], outs=[ytil_dram.ap().opt()]))
                        g.wait_ge(ssg.h, sg0 + 1)
                        sld.inc(g.dma_start(out=ytil,
                                            in_=ytil_dram.ap().rearrange("(c p) f -> p c f", p=128)), 16)

                    @blk.sync
                    def _(sy):
                        sy.wait_ge(sl2.h, d_dmas[0])
                        sy.wait_ge(sl3.h, yown_done)
                        for ec in range(16):
                            sl2.inc(sy.dma_start_transpose(
                                out=ctt[:, ec, :], in_=c_dram[:, ec * 128:(ec + 1) * 128]), 16)
                yc_ld = (sld.v, sl2.v)

                with nc.Block() as blk:
                    mm0, sv0 = smm.v, ssv.v

                    @blk.tensor
                    def _(t):
                        t.wait_ge(sld.h, yc_ld[0])
                        t.wait_ge(sl2.h, yc_ld[1])
                        for tc in range(NTC):
                            tp = tc % 2
                            if tc >= 2:
                                t.wait_ge(ssv.h, sv0 + (tc - 2) + 1)
                            for nn in range(2):
                                pb = ps_f[:, (tp * 2 + nn) * 512:(tp * 2 + nn + 1) * 512]
                                for ec in range(16):
                                    ins = t.matmul(pb, ctt[:, ec, tc * 128:(tc + 1) * 128],
                                                   ytil[:, ec, nn * 512:(nn + 1) * 512],
                                                   start=(ec == 0), stop=(ec == 15))
                            smm.inc(ins)

                    @blk.vector
                    def _(v):
                        for tc in range(NTC):
                            tp = tc % 2
                            v.wait_ge(smm.h, mm0 + tc + 1)
                            ssv.inc(v.tensor_add(XS[:, tc, :], XS[:, tc, :],
                                                 ps_f[:, tp * 1024:(tp + 1) * 1024]))

            ab.freea("ytil"); ab.freea("ctt")
            DBG["ab3"] = dict(ab.live)
            with nc.Block() as blk:
                @blk.sync
                def _(sy):
                    tap("stream_p3", XS, (128, NTC, D), sy, waits=[(ssv, ssv.v)])

        if upto >= 0:
            st_dep = emit_ln_stats()
        if upto >= 2:
            with nc.named_scope("p1"):
                emit_l0_attn(st_dep)
            with nc.Block() as blk:
                @blk.sync
                def _(sy):
                    tap("stream_p1", XS, (128, NTC, D), sy, waits=[(ssv, ssv.v)])
        if upto >= 3:
            with nc.named_scope("p3"):
                st_dep2 = emit_ln_stats(xdep=(ssv, ssv.v))
                emit_l0_ffn(st_dep2)



        def emit_l1_attn(pair):
            xlnT3 = ab.alloc("xlnT3", (DC, TL))
            xtok_s = ab.alloc("xtok_s", (2, D))
            st3 = emit_ln_stats(xdep=(ssv, ssv.v))
            for sl in range(8):
                emit_xln_slice(sl * 2, 2, ab.view("xtok_s"), ab.view("xlnT3"), st3,
                               fm_off=sl * 256)
            ab.freea("xtok_s")
            lnT3 = ssv.v

            with nc.named_scope("qproj"):
                wq1 = ab.alloc("wq1", (DC, D))
                q1 = ab.alloc("q1", (DC, TL))
                bq1 = af.alloc("bq1", (8,))
                bq81 = af.alloc("bq81", (8,))
                with nc.Block() as blk:
                    @blk.gpsimd
                    def _(g):
                        g.wait_ge(sld.h, sld.v)
                        sld.inc(g.dma_start(out=wq1, in_=w_l1q.ap().rearrange("(i p) o -> p i o", p=128)), 16)

                    @blk.sync
                    def _(sy):
                        sy.wait_ge(sl2.h, sl2.v)
                        sl2.inc(sy.dma_start(out=bq1, in_=b_l1q[:, :]), 16)
                wq_ld, bq_ld = sld.v, sl2.v
                with nc.Block() as blk:
                    mm0, sv0 = smm.v, ssv.v

                    @blk.tensor
                    def _(t):
                        t.wait_ge(sld.h, wq_ld)
                        t.wait_ge(ssv.h, lnT3)
                        for oc in range(DC):
                            for qt in range(4):
                                idx = oc * 4 + qt
                                if idx >= 4:
                                    t.wait_ge(ssv.h, sv0 + idx - 3)
                                pb = ps_f[:, (idx % 4) * 512:(idx % 4) * 512 + 512]
                                for ic in range(DC):
                                    ins = t.matmul(pb, wq1[:, ic, oc * 128:(oc + 1) * 128],
                                                   xlnT3[:, ic, qt * 512:(qt + 1) * 512],
                                                   start=(ic == 0), stop=(ic == DC - 1))
                                smm.inc(ins)

                    @blk.vector
                    def _(v):
                        v.wait_ge(sl2.h, bq_ld)
                        v.tensor_scalar_mul(bq81, bq1, 0.125)
                        v.drain()
                        for oc in range(DC):
                            for qt in range(4):
                                idx = oc * 4 + qt
                                v.wait_ge(smm.h, mm0 + idx + 1)
                                pb = ps_f[:, (idx % 4) * 512:(idx % 4) * 512 + 512]
                                ssv.inc(v.tensor_scalar(q1[:, oc, qt * 512:(qt + 1) * 512], pb,
                                                        0.125, bq81[:, oc:oc + 1],
                                                        ALU.mult, ALU.add))
                ab.freea("wq1")

            with nc.named_scope("kvproj"):
                wk1 = ab.alloc("wk1", (DC, 256))
                wv1 = ab.alloc("wv1", (DC, 256))
                k1 = ab.alloc("k1", (2, TL))
                vtok = ab.alloc("vtok", (NTC, 256))
                bvr = ab.alloc("bvr", (256,))
                bk1 = af.alloc("bk1", (2,))
                with nc.Block() as blk:
                    @blk.gpsimd
                    def _(g):
                        g.wait_ge(sld.h, sld.v)
                        sld.inc(g.dma_start(out=wk1, in_=w_l1k.ap().rearrange("(i p) o -> p i o", p=128)), 16)
                        g.wait_ge(sld.h, sld.v)
                        sld.inc(g.dma_start(out=wv1, in_=w_l1v.ap().rearrange("(i p) o -> p i o", p=128)), 16)
                        g.wait_ge(sld.h, sld.v)
                        sld.inc(g.dma_start(out=bvr[0:1, :], in_=b_l1v[:, :]), 16)

                    @blk.sync
                    def _(sy):
                        sy.wait_ge(sl2.h, sl2.v)
                        sl2.inc(sy.dma_start(out=bk1, in_=b_l1k[:, :]), 16)
                wkv_ld, bk_ld = sld.v, sl2.v
                with nc.Block() as blk:
                    mm0, sv0 = smm.v, ssv.v

                    @blk.tensor
                    def _(t):
                        t.wait_ge(sld.h, wkv_ld)
                        for oc in range(2):
                            for qt in range(4):
                                idx = oc * 4 + qt
                                if idx >= 4:
                                    t.wait_ge(ssv.h, sv0 + idx - 3)
                                pb = ps_f[:, (idx % 4) * 512:(idx % 4) * 512 + 512]
                                for ic in range(DC):
                                    ins = t.matmul(pb, wk1[:, ic, oc * 128:(oc + 1) * 128],
                                                   xlnT3[:, ic, qt * 512:(qt + 1) * 512],
                                                   start=(ic == 0), stop=(ic == DC - 1))
                                smm.inc(ins)
                        t.wait_ge(ssv.h, sv0 + 8)
                        for tc in range(NTC):
                            if tc >= 2:
                                t.wait_ge(ssv.h, sv0 + 8 + tc - 1)
                            pb = ps_g[:, (tc % 2) * 512:(tc % 2) * 512 + 256]
                            for ic in range(DC):
                                t.matmul(pb, xlnT3[:, ic, tc * 128:(tc + 1) * 128],
                                         wv1[:, ic, :], start=(ic == 0), stop=False)
                            ins = t.matmul(pb, onesrow[0:1, 0:128], bvr[0:1, :],
                                           start=False, stop=True)
                            smm.inc(ins)

                    @blk.vector
                    def _(v):
                        v.wait_ge(sl2.h, bk_ld)
                        for oc in range(2):
                            for qt in range(4):
                                idx = oc * 4 + qt
                                v.wait_ge(smm.h, mm0 + idx + 1)
                                pb = ps_f[:, (idx % 4) * 512:(idx % 4) * 512 + 512]
                                ssv.inc(v.tensor_scalar(k1[:, oc, qt * 512:(qt + 1) * 512], pb,
                                                        bk1[:, oc:oc + 1], 0.0, ALU.add, ALU.add))
                        for tc in range(NTC):
                            v.wait_ge(smm.h, mm0 + 8 + tc + 1)
                            pb = ps_g[:, (tc % 2) * 512:(tc % 2) * 512 + 256]
                            ssv.inc(v.tensor_copy(vtok[:, tc, :], pb))
                ab.freea("wk1"); ab.freea("wv1"); ab.freea("xlnT3"); ab.freea("bvr")

            with nc.named_scope("rope"):
                pmt = ab.alloc("pmt", (128,))
                rtmp = ab.alloc("rtmp", (512,))
                with nc.Block() as blk:
                    @blk.gpsimd
                    def _(g):
                        g.wait_ge(sld.h, sld.v)
                        sld.inc(g.dma_start(out=pmt, in_=pmat_d[:, :]), 16)
                pm_ld = sld.v
                rp_slices = [("q", oc, qt) for oc in range(DC) for qt in range(4)] + \
                            [("k", oc, qt) for oc in range(2) for qt in range(4)]
                with nc.Block() as blk:
                    mm0, sv0 = smm.v, ssv.v

                    @blk.tensor
                    def _(t):
                        t.wait_ge(sld.h, pm_ld)
                        t.wait_ge(ssv.h, sv0)
                        for idx, (which, oc, qt) in enumerate(rp_slices):
                            if idx >= 4:
                                t.wait_ge(ssv.h, sv0 + idx - 3)
                            src_ = (q1 if which == "q" else k1)[:, oc, qt * 512:(qt + 1) * 512]
                            ins = t.matmul(ps_f[:, (idx % 4) * 512:(idx % 4) * 512 + 512],
                                           pmt, src_, start=True, stop=True)
                            smm.inc(ins)

                    @blk.vector
                    def _(v):
                        for idx, (which, oc, qt) in enumerate(rp_slices):
                            v.wait_ge(smm.h, mm0 + idx + 1)
                            sl_ = (q1 if which == "q" else k1)[:, oc, qt * 512:(qt + 1) * 512]
                            pb = ps_f[:, (idx % 4) * 512:(idx % 4) * 512 + 512]
                            v.tensor_mul(sl_, sl_, rcos[:, qt * 512:(qt + 1) * 512])
                            v.tensor_mul(rtmp, pb, rsin[:, qt * 512:(qt + 1) * 512])
                            v.drain()
                            ssv.inc(v.tensor_add(sl_, sl_, rtmp))
                            v.drain()
                rope_k = ssv.v
                ab.freea("pmt"); ab.freea("rtmp")
                ab.freea("rcos"); ab.freea("rsin")

            with nc.named_scope("bias8"):
                qbar = ab.alloc("qbar", (DC, 64))
                kbar0 = ab.alloc("kbar0", (2, 64))
                kbarg = ab.alloc("kbarg", (4, 64))
                m8b = ab.alloc("m8b", (8,))
                mg = ab.alloc("mg", (1024,))
                mr8 = ab.alloc("mr8", (8,))
                m8f = af.alloc("m8f", (8,))
                mr8f = af.alloc("mr8f", (8,))
                bias8 = af.alloc("bias8", (8,))
                with nc.Block() as blk:
                    mm0, sv0, sd0 = smm.v, ssv.v, sl2.v

                    @blk.vector
                    def _(v):
                        v.wait_ge(ssv.h, rope_k)
                        with nc.allow_low_precision(reason="coarse softmax-shift estimate only"):
                            v.reduce_sum(qbar, q1.rearrange("p h (bb t) -> p h bb t", t=32),
                                         axis=AX.X)
                            ssv.inc(v.reduce_sum(kbar0,
                                                 k1.rearrange("p h (bb t) -> p h bb t", t=32),
                                                 axis=AX.X))
                        v.wait_ge(smm.h, mm0 + 1)
                        v.reduce_max(m8f, ps_g[:, 0:512].rearrange("p (h q) -> p h q", q=64),
                                     axis=AX.X)
                        v.drain()
                        ssv.inc(v.tensor_copy(m8b, m8f))
                        v.wait_ge(sl2.h, sd0 + 9 * 16)
                        v.reduce_max(mr8f[0:1, :], mg.rearrange("p (q h) -> p h q", h=8)[0:1, :, :],
                                     axis=AX.X)
                        v.drain()
                        ssv.inc(v.tensor_copy(mr8[0:1, :], mr8f[0:1, :]))
                        v.wait_ge(smm.h, mm0 + 2)
                        ssv.inc(v.tensor_scalar(bias8, ps_g[:, 512:520], -1.0 / 1024.0, -8.0,
                                                ALU.mult, ALU.add))

                    @blk.sync
                    def _(sy):
                        sy.wait_ge(ssv.h, sv0 + 1)
                        for g_ in range(4):
                            for half in range(2):
                                sy.wait_ge(sl2.h, sl2.v)
                                sl2.inc(sy.dma_start(
                                    out=kbarg[half * 64:half * 64 + 64, g_, :],
                                    in_=kbar0[(g_ % 2) * 64:(g_ % 2) * 64 + 64, g_ // 2, :]), 16)
                        sy.wait_ge(ssv.h, sv0 + 2)
                        sy.wait_ge(sl2.h, sl2.v)
                        sl2.inc(sy.dma_start(out=mg[0:1, :], in_=m8b[:, :]), 16)

                    @blk.tensor
                    def _(t):
                        t.wait_ge(sl2.h, sd0 + 8 * 16)
                        for hp in range(8):
                            g_ = hp // 2
                            t.matmul(ps_g[0:64, hp * 64:(hp + 1) * 64],
                                     qbar[0:64, hp, :], kbarg[0:64, g_, :],
                                     start=True, stop=True, tile_position=(0, 0))
                            ins = t.matmul(ps_g[64:128, hp * 64:(hp + 1) * 64],
                                           qbar[64:128, hp, :], kbarg[64:128, g_, :],
                                           start=True, stop=True, tile_position=(64, 64))
                        smm.inc(ins)
                        t.wait_ge(ssv.h, sv0 + 3)
                        ins = t.matmul(ps_g[:, 512:520], onesrow[0:1, 0:128], mr8[0:1, :],
                                       start=True, stop=True)
                        smm.inc(ins)
                bias_done = ssv.v

            with nc.named_scope("kvag"):
                with nc.Block() as blk:
                    sg0, sd0 = ssg.v, sl2.v

                    @blk.sync
                    def _(sy):
                        sy.wait_ge(ssv.h, rope_k)
                        sy.wait_ge(sl2.h, sl2.v)
                        sl2.inc(sy.dma_start(out=k_ag_in.ap().rearrange("(c p) s -> p c s", p=128),
                                             in_=k1), 16)
                        sy.wait_ge(sl2.h, sl2.v)
                        sl2.inc(sy.dma_start(out=v_ag_in.ap().rearrange("(c p) f -> p c f", p=128),
                                             in_=vtok), 16)

                    @blk.gpsimd
                    def _(g):
                        g.wait_ge(sl2.h, sd0 + 32)
                        ssg.inc(g.collective_compute(
                            "AllGather", ALU.bypass, replica_groups=pair,
                            ins=[k_ag_in.ap().opt()], outs=[k_ag_out.ap().opt()]))
                        ssg.inc(g.collective_compute(
                            "AllGather", ALU.bypass, replica_groups=pair,
                            ins=[v_ag_in.ap().opt()], outs=[v_ag_out.ap().opt()]))
                ag_done = ssg.v
                ab.freea("vtok"); ab.freea("k1")
                for nm in ("qbar", "kbar0", "kbarg", "m8b", "mg", "mr8"):
                    ab.freea(nm)

            pst_stack.close()
            psx_stack = ExitStack()
            ps_x = psx_stack.enter_context(nc.psum_tensor("PSX", [128, 1024], F32))

            with nc.named_scope("mloop"):
                kdup = ab.alloc("kdup", (2, 4096))
                vaug = ab.alloc("vaug", (2, 32, 65))
                bufP = ab.alloc("bufP", (2, 2, 1024))
                obuf = ab.alloc("obuf", (DC, TL))
                zfm = af.alloc("zfm", (2048,))
                g_ld = []

                with nc.Block() as blk:
                    ld0 = sld.v

                    @blk.gpsimd
                    def _(g):
                        g.wait_ge(ssg.h, ag_done)
                        for g_ in range(2):
                            for half in range(2):
                                for bh in range(2):
                                    g.wait_ge(sld.h, sld.v)
                                    sld.inc(g.dma_start(
                                        out=kdup[half * 64:half * 64 + 64, g_ % 2,
                                                 bh * 2048:(bh + 1) * 2048],
                                        in_=k_ag_out[bh * 256 + g_ * 64: bh * 256 + (g_ + 1) * 64, :]), 16)
                            g.wait_ge(sld.h, sld.v)
                            sld.inc(g.dma_start(
                                out=vaug[:, g_ % 2, :, 0:64],
                                in_=v_ag_out.ap().rearrange("(sc p) f -> p sc f", p=128)
                                [:, :, g_ * 64:(g_ + 1) * 64]), 16)
                            g_ld.append(sld.v)

                    @blk.vector
                    def _(v):
                        v.memset(vaug[:, 0, :, 64:65], 1.0)
                        ssv.inc(v.memset(vaug[:, 1, :, 64:65], 1.0))
                ones_set = ssv.v

                with nc.Block() as blk:
                    mm0, sv0, ss0, zd0 = smm.v, ssv.v, sss.v, sl2.v
                    gmark = [mm0 + (gg + 1) * 4 * 96 for gg in range(4)]

                    @blk.gpsimd
                    def _(g):
                        for g_ in range(2, 4):
                            g.wait_ge(smm.h, gmark[g_ - 2])
                            for half in range(2):
                                for bh in range(2):
                                    g.wait_ge(sld.h, sld.v)
                                    sld.inc(g.dma_start(
                                        out=kdup[half * 64:half * 64 + 64, g_ % 2,
                                                 bh * 2048:(bh + 1) * 2048],
                                        in_=k_ag_out[bh * 256 + g_ * 64: bh * 256 + (g_ + 1) * 64, :]), 16)
                            g.wait_ge(sld.h, sld.v)
                            sld.inc(g.dma_start(
                                out=vaug[:, g_ % 2, :, 0:64],
                                in_=v_ag_out.ap().rearrange("(sc p) f -> p sc f", p=128)
                                [:, :, g_ * 64:(g_ + 1) * 64]), 16)
                            g_ld.append(sld.v)

                    @blk.tensor
                    def _(t):
                        t.wait_ge(ssv.h, ones_set)
                        t.wait_ge(ssv.h, bias_done)
                        for hp in range(8):
                            g_ = hp // 2
                            gp = g_ % 2
                            t.wait_ge(sld.h, g_ld[g_])
                            for qh in range(2):
                                it = hp * 2 + qh
                                base_t = mm0 + it * 96
                                base_s = ss0 + it * 64
                                for sc in range(32):
                                    for hx in range(2):
                                        if sc >= 1:
                                            t.wait_ge(sss.h, base_s + 2 * (sc - 1) + hx + 1)
                                        elif it >= 1 and hx == 0:
                                            t.wait_ge(sss.h, base_s)
                                        q_sl = q1[hx * 64:hx * 64 + 64, hp,
                                                  qh * 1024:(qh + 1) * 1024]
                                        k_sl = kdup[hx * 64:hx * 64 + 64, gp,
                                                    sc * 128:(sc + 1) * 128]
                                        for q2 in range(2):
                                            ins = t.matmul(
                                                ps_f[:, (hx * 2 + q2) * 512:(hx * 2 + q2 + 1) * 512],
                                                k_sl, q_sl[:, q2 * 512:(q2 + 1) * 512],
                                                start=True, stop=True,
                                                tile_position=(hx * 64, 0))
                                        smm.inc(ins)
                                    if sc >= 1:
                                        if sc == 1 and it >= 1:
                                            t.wait_ge(ssv.h, sv0 + it * 4)
                                        t.wait_ge(sss.h, base_s + 2 * sc)
                                        for hx in range(2):
                                            for q2 in range(2):
                                                pv = (ps_g if hx == 0 else ps_x)
                                                ins = t.matmul(
                                                    pv[0:65, q2 * 512:(q2 + 1) * 512],
                                                    vaug[:, gp, sc - 1, :],
                                                    bufP[:, (sc - 1) % 2, hx, q2 * 512:(q2 + 1) * 512],
                                                    start=(sc - 1 == 0), stop=False)
                                        smm.inc(ins)
                                t.wait_ge(sss.h, base_s + 64)
                                for hx in range(2):
                                    for q2 in range(2):
                                        pv = (ps_g if hx == 0 else ps_x)
                                        ins = t.matmul(pv[0:65, q2 * 512:(q2 + 1) * 512],
                                                       vaug[:, gp, 31, :],
                                                       bufP[:, 31 % 2, hx, q2 * 512:(q2 + 1) * 512],
                                                       start=False, stop=True)
                                smm.inc(ins)

                    @blk.scalar
                    def _(s):
                        for hp in range(8):
                            for qh in range(2):
                                it = hp * 2 + qh
                                base_t = mm0 + it * 96
                                base_s = ss0 + it * 64
                                for sc in range(32):
                                    for hx in range(2):
                                        sw = (base_t + hx + 1 if sc == 0
                                              else base_t + 2 + 3 * (sc - 1) + hx + 1)
                                        s.wait_ge(smm.h, sw)
                                        sss.inc(s.activation(
                                            bufP[:, sc % 2, hx, :],
                                            ps_f[:, hx * 1024:(hx + 1) * 1024], AF.Exp,
                                            bias=bias8[:, hp:hp + 1]))

                    @blk.vector
                    def _(v):
                        for hp in range(8):
                            for qh in range(2):
                                it = hp * 2 + qh
                                base_t = mm0 + it * 96
                                v.wait_ge(smm.h, base_t + 96)
                                if it >= 1:
                                    v.wait_ge(sl2.h, zd0 + it * 32)
                                v.tensor_copy(zfm[64:65, 0:1024], ps_g[64:65, 0:1024])
                                v.tensor_copy(zfm[64:65, 1024:2048], ps_x[64:65, 0:1024])
                                for hx in range(2):
                                    pv = (ps_g if hx == 0 else ps_x)
                                    for q2 in range(2):
                                        col = qh * 1024 + q2 * 512
                                        ssv.inc(v.tensor_copy(
                                            obuf[hx * 64:hx * 64 + 64, hp, col:col + 512],
                                            pv[0:64, q2 * 512:(q2 + 1) * 512]))

                    @blk.sync
                    def _(sy):
                        for hp in range(8):
                            for qh in range(2):
                                it = hp * 2 + qh
                                sy.wait_ge(ssv.h, sv0 + it * 4 + 4)
                                for hx in range(2):
                                    h = 2 * hp + hx
                                    sy.wait_ge(sl2.h, sl2.v)
                                    sl2.inc(sy.dma_start(
                                        out=zfm[h:h + 1, qh * 1024:(qh + 1) * 1024],
                                        in_=zfm[64:65, hx * 1024:(hx + 1) * 1024]), 16)
                att5 = ssv.v
                zfm_done = sl2.v
                ab.freea("kdup"); ab.freea("vaug"); ab.freea("bufP")

            with nc.named_scope("norm"):
                wo1 = ab.alloc("wo1", (DC, D))
                bor = ab.alloc("bor", (D,))
                zrb = ab.alloc("zrb", (2048,))
                zr0 = ab.alloc("zr0", (2048,))
                with nc.Block() as blk:
                    ld0 = sld.v

                    @blk.gpsimd
                    def _(g):
                        g.wait_ge(sld.h, sld.v)
                        sld.inc(g.dma_start(out=wo1, in_=w_l1o.ap().rearrange("(i p) o -> p i o", p=128)), 16)
                        g.wait_ge(sld.h, sld.v)
                        sld.inc(g.dma_start(out=bor[0:1, :], in_=b_l1o[:, :]), 16)
                wo_ld = sld.v

                with nc.Block() as blk:
                    mm0, sv0, sd0 = smm.v, ssv.v, sl2.v

                    @blk.vector
                    def _(v):
                        v.wait_ge(ssv.h, att5)
                        v.wait_ge(sl2.h, zfm_done)
                        v.reciprocal(zfm[0:16, :], zfm[0:16, :])
                        v.drain()
                        ssv.inc(v.tensor_copy(zrb[0:16, :], zfm[0:16, :]))
                        for h in range(16):
                            v.wait_ge(smm.h, mm0 + h + 1)
                            hb = (h % 2) * 64
                            for qt in range(4):
                                pv = ps_g if qt < 2 else ps_x
                                ins = v.tensor_mul(
                                    obuf[hb:hb + 64, h // 2, qt * 512:(qt + 1) * 512],
                                    obuf[hb:hb + 64, h // 2, qt * 512:(qt + 1) * 512],
                                    pv[hb:hb + 64, (qt % 2) * 512:(qt % 2) * 512 + 512])
                            ssv.inc(ins)

                    @blk.sync
                    def _(sy):
                        for h in range(16):
                            sy.wait_ge(ssv.h, sv0 + 1)
                            sy.wait_ge(sl2.h, sl2.v)
                            if h >= 1:
                                sy.wait_ge(smm.h, mm0 + h)
                            sl2.inc(sy.dma_start(out=zr0[0:1, :], in_=zrb[h:h + 1, :]), 16)

                    @blk.tensor
                    def _(t):
                        t.wait_ge(sld.h, wo_ld)
                        for h in range(16):
                            t.wait_ge(sl2.h, sd0 + 16 * (h + 1))
                            if h >= 1:
                                t.wait_ge(ssv.h, sv0 + h + 1)
                            hb = (h % 2) * 64
                            for qt in range(4):
                                pv = ps_g if qt < 2 else ps_x
                                ins = t.matmul(pv[hb:hb + 64, (qt % 2) * 512:(qt % 2) * 512 + 512],
                                               onesrow[0:1, 0:64],
                                               zr0[0:1, qt * 512:(qt + 1) * 512],
                                               start=True, stop=True,
                                               tile_position=(0, hb))
                            smm.inc(ins)
                zdone = ssv.v
                ab.freea("zrb")

                with nc.Block() as blk:
                    mm0, sv0 = smm.v, ssv.v

                    @blk.tensor
                    def _(t):
                        t.wait_ge(ssv.h, zdone)
                        for tc in range(NTC):
                            tp = tc % 2
                            if tc >= 2:
                                t.wait_ge(ssv.h, sv0 + tc - 1)
                            for nn in range(2):
                                pb = ps_f[:, (tp * 2 + nn) * 512:(tp * 2 + nn + 1) * 512]
                                for ic in range(DC):
                                    t.matmul(pb, obuf[:, ic, tc * 128:(tc + 1) * 128],
                                             wo1[:, ic, nn * 512:(nn + 1) * 512],
                                             start=(ic == 0), stop=False)
                                ins = t.matmul(pb, onesrow[0:1, 0:128],
                                               bor[0:1, nn * 512:(nn + 1) * 512],
                                               start=False, stop=True)
                            smm.inc(ins)

                    @blk.vector
                    def _(v):
                        for tc in range(NTC):
                            tp = tc % 2
                            v.wait_ge(smm.h, mm0 + tc + 1)
                            ssv.inc(v.tensor_add(XS[:, tc, :], XS[:, tc, :],
                                                 ps_f[:, tp * 1024:(tp + 1) * 1024]))

                for nm in ("q1", "obuf", "wo1", "bor", "zr0"):
                    ab.freea(nm)
                for nm in ("bq1", "bq81", "bk1", "m8f", "mr8f", "bias8", "zfm"):
                    af.freea(nm)
                psx_stack.close()
            with nc.Block() as blk:
                @blk.sync
                def _(sy):
                    tap("stream_p5", XS, (128, NTC, D), sy, waits=[(ssv, ssv.v)])


        def emit_l1_ffn(pair):
            pst2 = stack.enter_context(nc.psum_tensor("PST2", [128, 2048], BF16))
            cn1 = ab.alloc("cn1", (DC, D))
            cn2 = ab.alloc("cn2", (DC, 512))
            cn3 = ab.alloc("cn3", (4, 256))
            cn4 = ab.alloc("cn4", (2,))
            gc1 = ab.alloc("gc1", (DC, 512))
            gc2 = ab.alloc("gc2", (4, 256))
            gc3 = ab.alloc("gc3", (2,))
            onescol = ab.alloc("onescol", (1,))
            gxr = ab.alloc("gxr", (1024,))
            gfm = ab.alloc("gfm", (8,))
            g1b = ab.alloc("g1b", (4,))
            g2b = ab.alloc("g2b", (2,))
            abrow = ab.alloc("abrow", (3,))
            xg = ab.alloc("xg", (2, D))
            cb1 = af.alloc("cb1", (8,)); cb2 = af.alloc("cb2", (4,)); cb3 = af.alloc("cb3", (2,))
            b4r = af.alloc("b4r", (1,)); gb1 = af.alloc("gb1", (4,)); gb2 = af.alloc("gb2", (2,))
            gb3 = af.alloc("gb3", (1,)); arr = af.alloc("arr", (1,))
            tiny = af.alloc("tiny", (8,))
            acbc = af.alloc("acbc", (3,))
            garow = af.alloc("garow", (1024,)); gbrow = af.alloc("gbrow", (1024,))

            with nc.Block() as blk:
                @blk.gpsimd
                def _(g):
                    for dst, srcd in ((cn1, w_cn1), (cn2, w_cn2), (cn3, w_cn3),
                                      (gc1, w_gc1), (gc2, w_gc2)):
                        g.wait_ge(sld.h, sld.v)
                        sld.inc(g.dma_start(out=dst,
                                            in_=srcd.ap().rearrange("(i p) o -> p i o", p=128)), 16)
                    for dst, srcd in ((cn4, w_cn4), (gc3, w_gc3)):
                        g.wait_ge(sld.h, sld.v)
                        sld.inc(g.dma_start(out=dst, in_=srcd[:, :]), 16)

                @blk.sync
                def _(sy):
                    for dst, srcd in ((cb1, b_cn1), (cb2, b_cn2), (cb3, b_cn3),
                                      (gb1, b_gc1), (gb2, b_gc2)):
                        sy.wait_ge(sl2.h, sl2.v)
                        sl2.inc(sy.dma_start(out=dst, in_=srcd[:, :]), 16)
                    for dst, srcd in ((b4r, b_cn4), (gb3, b_gc3), (arr, a_rng)):
                        sy.wait_ge(sl2.h, sl2.v)
                        sl2.inc(sy.dma_start(out=dst[0:1, :], in_=srcd[:, :]), 16)

                @blk.vector
                def _(v):
                    ssv.inc(v.memset(onescol, 1.0))
                    v.wait_ge(sld.h, sld.v)
                    v.tensor_scalar_mul(cn4, cn4, 0.5)
                    ssv.inc(v.tensor_scalar_mul(gc3, gc3, 0.5))
            w7_ld, b7_ld = sld.v, sl2.v
            st4 = emit_ln_stats(xdep=(ssv, ssv.v))

            with nc.Block() as blk:
                mm0, sv0, sd0 = smm.v, ssv.v, sl2.v

                @blk.vector
                def _(v):
                    v.wait_ge(ssv.h, st4)
                    for tc in range(NTC):
                        tp = tc % 2
                        if tc >= 2:
                            v.wait_ge(smm.h, mm0 + tc - 1)
                        ssv.inc(v.tensor_scalar(xg[:, tp, :], XS[:, tc, :],
                                                stats[:, 7, tc:tc + 1], stats[:, 9, tc:tc + 1],
                                                ALU.mult, ALU.add))
                    v.wait_ge(smm.h, mm0 + NTC)
                    ssv.inc(v.tensor_copy(garow[0:1, :], ps_g[0:1, 0:1024]))

                @blk.tensor
                def _(t):
                    for tc in range(NTC):
                        tp = tc % 2
                        t.wait_ge(ssv.h, sv0 + tc + 1)
                        for nn in range(2):
                            ins = t.matmul(ps_g[0:1, nn * 512:(nn + 1) * 512],
                                           onescol[:, 0:1], xg[:, tp, nn * 512:(nn + 1) * 512],
                                           start=(tc == 0), stop=(tc == NTC - 1))
                        smm.inc(ins)

                @blk.sync
                def _(sy):
                    sy.wait_ge(ssv.h, sv0 + NTC + 1)
                    sy.wait_ge(sl2.h, sl2.v)
                    sl2.inc(sy.dma_start(out=gm_ag_in[:, :], in_=garow[0:1, :]), 16)

                @blk.gpsimd
                def _(g):
                    g.wait_ge(sl2.h, sd0 + 16)
                    ssg.inc(g.collective_compute(
                        "AllGather", ALU.bypass, replica_groups=pair,
                        ins=[gm_ag_in.ap().opt()], outs=[gm_ag_out.ap().opt()]))
                    g.wait_ge(ssg.h, ssg.v)
                    g.wait_ge(sld.h, sld.v)
                    sld.inc(g.dma_start(out=garow[0:1, :], in_=gm_ag_out[0:1, :]), 16)
                    g.wait_ge(sld.h, sld.v)
                    sld.inc(g.dma_start(out=gbrow[0:1, :], in_=gm_ag_out[1:2, :]), 16)
            gm_ld = sld.v
            if P7SUB < 1:
                return

            with nc.Block() as blk:
                mm0, sv0, ss0, sd0 = smm.v, ssv.v, sss.v, sl2.v

                @blk.vector
                def _(v):
                    v.wait_ge(sld.h, gm_ld)
                    v.wait_ge(sl2.h, b7_ld)
                    v.tensor_add(garow[0:1, :], garow[0:1, :], gbrow[0:1, :])
                    v.drain()
                    ssv.inc(v.tensor_scalar(gxr[0:1, :], garow[0:1, :], 1.0 / 4096.0, 0.0,
                                            ALU.mult, ALU.add))
                    v.tensor_scalar(tiny[0:1, 6:7], b4r[0:1, 0:1], 0.5, 0.0, ALU.mult, ALU.add)
                    ssv.inc(v.tensor_scalar(tiny[0:1, 7:8], gb3[0:1, 0:1], 0.5, 0.0,
                                            ALU.mult, ALU.add))
                    v.wait_ge(smm.h, mm0 + 1)
                    ssv.inc(v.tensor_copy(gfm, ps_g[:, 700:708]))
                    v.wait_ge(sss.h, ss0 + 7)
                    v.tensor_mul(tiny[0:1, 0:1], arr[0:1, 0:1], tiny[0:1, 2:3])
                    v.drain()
                    v.tensor_scalar(tiny[0:1, 4:5], tiny[0:1, 0:1], 0.7, 0.1, ALU.mult, ALU.add)
                    v.tensor_scalar(tiny[0:1, 5:6], arr[0:1, 0:1], 0.3, 0.0, ALU.mult, ALU.add)
                    v.drain()
                    ssv.inc(v.tensor_copy(abrow[0:1, :], tiny[0:1, 4:7]))
                    v.wait_ge(smm.h, mm0 + 5)
                    ssv.inc(v.tensor_copy(acbc, ps_g[:, 600:603]))

                @blk.tensor
                def _(t):
                    t.wait_ge(sld.h, w7_ld)
                    t.wait_ge(ssv.h, sv0 + 1)
                    for j in range(8):
                        ins = t.matmul(ps_g[:, 700 + j:701 + j],
                                       gxr[0:1, j * 128:(j + 1) * 128],
                                       onesrow[0:1, 0:1], start=True, stop=True)
                    smm.inc(ins)
                    t.wait_ge(ssv.h, sv0 + 3)
                    for oc in range(4):
                        for ic in range(DC):
                            ins = t.matmul(ps_f[:, oc * 512:oc * 512 + 1],
                                           gc1[:, ic, oc * 128:(oc + 1) * 128],
                                           gfm[:, ic:ic + 1],
                                           start=(ic == 0), stop=(ic == DC - 1))
                    smm.inc(ins)
                    t.wait_ge(sss.h, ss0 + 4)
                    for oc in range(2):
                        for ic in range(4):
                            ins = t.matmul(ps_g[:, oc * 512:oc * 512 + 1],
                                           gc2[:, ic, oc * 128:(oc + 1) * 128],
                                           g1b[:, ic:ic + 1], start=(ic == 0), stop=(ic == 3))
                    smm.inc(ins)
                    t.wait_ge(sss.h, ss0 + 6)
                    for ic in range(2):
                        ins = t.matmul(ps_f[0:1, 1024:1025], gc3[:, ic:ic + 1],
                                       g2b[:, ic:ic + 1], start=(ic == 0), stop=(ic == 1))
                    smm.inc(ins)
                    t.wait_ge(ssv.h, sv0 + 4)
                    ins = t.matmul(ps_g[:, 600:603], onesrow[0:1, 0:128], abrow[0:1, :],
                                   start=True, stop=True)
                    smm.inc(ins)

                @blk.scalar
                def _(s):
                    s.wait_ge(smm.h, mm0 + 2)
                    for oc in range(4):
                        sss.inc(s.activation(g1b[:, oc:oc + 1], ps_f[:, oc * 512:oc * 512 + 1],
                                             AF.Gelu, bias=gb1[:, oc:oc + 1]))
                    s.wait_ge(smm.h, mm0 + 3)
                    for oc in range(2):
                        sss.inc(s.activation(g2b[:, oc:oc + 1], ps_g[:, oc * 512:oc * 512 + 1],
                                             AF.Gelu, bias=gb2[:, oc:oc + 1]))
                    s.wait_ge(smm.h, mm0 + 4)
                    s.wait_ge(ssv.h, sv0 + 2)
                    sss.inc(s.activation(tiny[0:1, 2:3], ps_f[0:1, 1024:1025], AF.Tanh,
                                         bias=tiny[0:1, 7:8]))
            ab_done = ssv.v
            if P7SUB < 2:
                return

            af.freea("fscr"); af.freea("garow"); af.freea("gbrow")
            osf0 = af.alloc("osf0", (2, 1024))
            osb1 = ab.alloc("osb1", (2, 2048))
            osf1 = ab.view("osb1").bitcast(F32)
            osfs = [osf0, osf1]
            otok = [ab.alloc(f"otok{s}", (2, D)) for s in range(2)]
            outT = [ab.alloc(f"outT{s}", (DC, 256)) for s in range(2)]
            h1b = [ab.alloc(f"h1b{s}", (DC, 256)) for s in range(2)]
            h2b = [ab.alloc(f"h2b{s}", (4, 256)) for s in range(2)]
            h3b = [ab.alloc(f"h3b{s}", (2, 256)) for s in range(2)]
            ucol = af.alloc("ucol", (4,))
            ccol = af.alloc("ccol", (2,))
            ncol = af.alloc("ncol", (2,))
            twq = af.alloc("twq", (512,))
            DBG["ab7"] = dict(ab.live); DBG["af7"] = dict(af.live)

            def h1reg(s, j):
                return ps_f[:, ((2 * j + s) % 4) * 512:((2 * j + s) % 4) * 512 + 256]

            def h2reg(s):
                return ps_g[:, s * 512:s * 512 + 256]

            def h3reg(s, j):
                return ps_f[:, ((2 * j + s) % 4) * 512:((2 * j + s) % 4) * 512 + 256]

            def h4reg(s, i):
                ps = ps_g if s == 0 else ps_f
                return ps[:, 300 + i * 512:301 + i * 512]

            for pj in range(min(4, max(0, P7SUB - 1))):
                tcA, tcB = pj * 4, pj * 4 + 2
                tc0s = [tcA, tcB]
                with nc.Block() as blk:
                    mm0, sv0, ss0 = smm.v, ssv.v, sss.v

                    @blk.vector
                    def _(v):
                        v.wait_ge(ssv.h, ab_done)
                        for s in range(2):
                            for i in range(2):
                                ssv.inc(v.tensor_scalar(
                                    osfs[s][:, i, :], XS[:, tc0s[s] + i, :],
                                    stats[:, 7, tc0s[s] + i:tc0s[s] + i + 1],
                                    stats[:, 9, tc0s[s] + i:tc0s[s] + i + 1],
                                    ALU.mult, ALU.add))
                        v.drain()
                        for cyc in range(3):
                            cm = mm0 + cyc * 36
                            cs = ss0 + cyc * 32
                            for s in range(2):
                                for i in range(2):
                                    ssv.inc(v.tensor_copy(otok[s][:, i, :], osfs[s][:, i, :]))
                            for i in range(2):
                                for s in range(2):
                                    v.wait_ge(smm.h, cm + 2 * i + s + 1)
                                    ssv.inc(v.tensor_copy(
                                        outT[s][:, :, i * 128:(i + 1) * 128],
                                        pst2[:, s * 1024:s * 1024 + 1024]
                                        .rearrange("p (f q) -> p f q", q=128)))
                            for s in range(2):
                                for i in range(2):
                                    v.wait_ge(sss.h, cs + 28 + 2 * s + i + 1)
                                    v.tensor_scalar(ccol[:, i:i + 1], ucol[:, 2 * s + i:2 * s + i + 1],
                                                    acbc[:, 1:2], acbc[:, 0:1],
                                                    ALU.mult, ALU.add)
                                    v.drain()
                                    v.tensor_scalar_mul(ncol[:, i:i + 1], ccol[:, i:i + 1], -1.0)
                                    os_i = osfs[s][:, i, :]
                                    v.tensor_copy(twq, os_i[:, 0:512])
                                    v.drain()
                                    v.scalar_tensor_tensor(os_i[:, 0:256], os_i[:, 512:768],
                                                           ccol[:, i:i + 1], os_i[:, 0:256],
                                                           ALU.mult, ALU.add)
                                    v.scalar_tensor_tensor(os_i[:, 256:512], os_i[:, 768:1024],
                                                           ncol[:, i:i + 1], os_i[:, 256:512],
                                                           ALU.mult, ALU.add)
                                    v.drain()
                                    v.scalar_tensor_tensor(os_i[:, 512:768], twq[:, 0:256],
                                                           ncol[:, i:i + 1], os_i[:, 512:768],
                                                           ALU.mult, ALU.add)
                                    ssv.inc(v.scalar_tensor_tensor(
                                        os_i[:, 768:1024], twq[:, 256:512],
                                        ccol[:, i:i + 1], os_i[:, 768:1024],
                                        ALU.mult, ALU.add))
                                    v.drain()
                        for s in range(2):
                            for i in range(2):
                                ssv.inc(v.tensor_add(XS[:, tc0s[s] + i, :],
                                                     XS[:, tc0s[s] + i, :], osfs[s][:, i, :]))

                    @blk.tensor
                    def _(t):
                        t.wait_ge(sld.h, w7_ld)
                        for cyc in range(3):
                            cm = mm0 + cyc * 36
                            cs = ss0 + cyc * 32
                            cb0 = sv0 + 4 + cyc * 12
                            for i in range(2):
                                for s in range(2):
                                    if i == 0:
                                        t.wait_ge(ssv.h, cb0 + 2 * s + 1)
                                    else:
                                        t.wait_ge(ssv.h, cb0 + 4 + s + 1)
                                    for fc in range(DC):
                                        ins = t.transpose(
                                            pst2[:, s * 1024 + fc * 128:s * 1024 + (fc + 1) * 128],
                                            otok[s][:, i, fc * 128:(fc + 1) * 128], ident)
                                    smm.inc(ins)
                            for j in range(DC):
                                for s in range(2):
                                    if j == 0:
                                        t.wait_ge(ssv.h, cb0 + 6 + s + 1)
                                    if j >= 2:
                                        t.wait_ge(sss.h, cs + 2 * (j - 2) + s + 1)
                                    for ic in range(DC):
                                        ins = t.matmul(h1reg(s, j),
                                                       cn1[:, ic, j * 128:(j + 1) * 128],
                                                       outT[s][:, ic, :], start=(ic == 0),
                                                       stop=(ic == DC - 1))
                                    smm.inc(ins)
                            for j in range(4):
                                for s in range(2):
                                    if j == 0:
                                        t.wait_ge(sss.h, cs + 14 + s + 1)
                                    else:
                                        t.wait_ge(sss.h, cs + 16 + 2 * (j - 1) + s + 1)
                                    for ic in range(DC):
                                        ins = t.matmul(h2reg(s),
                                                       cn2[:, ic, j * 128:(j + 1) * 128],
                                                       h1b[s][:, ic, :], start=(ic == 0),
                                                       stop=(ic == DC - 1))
                                    smm.inc(ins)
                            for j in range(2):
                                for s in range(2):
                                    t.wait_ge(sss.h, cs + 22 + s + 1)
                                    for ic in range(4):
                                        ins = t.matmul(h3reg(s, j),
                                                       cn3[:, ic, j * 128:(j + 1) * 128],
                                                       h2b[s][:, ic, :], start=(ic == 0),
                                                       stop=(ic == 3))
                                    smm.inc(ins)
                            for s in range(2):
                                t.wait_ge(sss.h, cs + 26 + s + 1)
                                for i in range(2):
                                    for ic in range(2):
                                        ins = t.matmul(h4reg(s, i),
                                                       h3b[s][:, ic, i * 128:(i + 1) * 128],
                                                       cn4[:, ic:ic + 1],
                                                       start=(ic == 0), stop=(ic == 1))
                                    smm.inc(ins)

                    @blk.scalar
                    def _(s_):
                        for cyc in range(3):
                            cm = mm0 + cyc * 36
                            for j in range(DC):
                                for s in range(2):
                                    s_.wait_ge(smm.h, cm + 4 + 2 * j + s + 1)
                                    sss.inc(s_.activation(h1b[s][:, j, :], h1reg(s, j),
                                                          AF.Gelu, bias=cb1[:, j:j + 1]))
                            for j in range(4):
                                for s in range(2):
                                    s_.wait_ge(smm.h, cm + 20 + 2 * j + s + 1)
                                    sss.inc(s_.activation(h2b[s][:, j, :], h2reg(s),
                                                          AF.Gelu, bias=cb2[:, j:j + 1]))
                            for j in range(2):
                                for s in range(2):
                                    s_.wait_ge(smm.h, cm + 28 + 2 * j + s + 1)
                                    sss.inc(s_.activation(h3b[s][:, j, :], h3reg(s, j),
                                                          AF.Gelu, bias=cb3[:, j:j + 1]))
                            for s in range(2):
                                for i in range(2):
                                    s_.wait_ge(smm.h, cm + 32 + 2 * s + i + 1)
                                    sss.inc(s_.activation(ucol[:, 2 * s + i:2 * s + i + 1],
                                                          h4reg(s, i),
                                                          AF.Tanh, bias=acbc[:, 2:3]))

            for nm in ("cn1", "cn2", "cn3", "cn4", "gc1", "gc2", "gc3", "onescol",
                       "gxr", "gfm", "g1b", "g2b", "abrow", "xg", "osb1",
                       "otok0", "otok1", "outT0", "outT1", "h1b0", "h1b1",
                       "h2b0", "h2b1", "h3b0", "h3b1"):
                ab.freea(nm)
            for nm in ("cb1", "cb2", "cb3", "b4r", "gb1", "gb2", "gb3", "arr",
                       "tiny", "acbc", "osf0", "ucol", "ccol", "ncol", "twq"):
                af.freea(nm)
            with nc.Block() as blk:
                @blk.sync
                def _(sy):
                    tap("stream_p7", XS, (128, NTC, D), sy, waits=[(ssv, ssv.v)])

        pg = [[0, 1], [2, 3], [4, 5], [6, 7]]
        if upto >= 4:
            with nc.named_scope("p5"):
                emit_l1_attn(pg)
        if upto >= 5:
            with nc.named_scope("p7"):
                emit_l1_ffn(pg)

        with nc.Block() as blk:
            @blk.sync
            def _(sy):
                sy.wait_ge(ssv.h, ssv.v)
                sy.wait_ge(smm.h, smm.v)
                sy.wait_ge(sss.h, sss.v)
                sy.wait_ge(sld.h, sld.v)
                sy.wait_ge(sl2.h, sl2.v)
                sst.inc(sy.dma_start(out=out_d.ap().rearrange("(c p) f -> p c f", p=128),
                                     in_=XS), 16)
                sy.wait_ge(sst.h, sst.v)

    return nc, tap_outs


def kernel(**inputs):
    maps = _host_prep(inputs)
    nc, _ = build()
    res = run_bass_kernel_spmd(nc, maps, core_ids=list(range(NCORE)))
    out = np.concatenate([r["out"] for r in res.results], axis=0)
    return out.reshape(4, 4096, D).astype(np.float32)


def dev_run(inputs, taps=(), upto=99):
    maps = _host_prep(inputs)
    nc, tap_outs = build(taps=taps, upto=upto)
    res = run_bass_kernel_spmd(nc, maps, core_ids=list(range(NCORE)))
    return res.results


# version 06
# speedup vs baseline: 1.3425x; 1.0094x over previous
import math
from contextlib import ExitStack

import numpy as np
import ml_dtypes

import concourse.bass as bass
import concourse.mybir as mybir
from concourse.bass_utils import run_bass_kernel_spmd

F32 = mybir.dt.float32
BF16 = mybir.dt.bfloat16
AF = mybir.ActivationFunctionType
ALU = mybir.AluOpType
AX = mybir.AxisListType
BF = ml_dtypes.bfloat16

NCORE = 8
DBG = {}
P7SUB = 99
TL = 2048
NTC = 16
D = 1024
DC = 8
E = 2048
EPS = 1e-5


def _pp(b):
    b = np.asarray(b, np.float32)
    return np.ascontiguousarray(b.reshape(b.size // 128, 128).T)


def _row(b):
    return np.ascontiguousarray(np.asarray(b, np.float32).reshape(1, -1))


def _rope_tables(core):
    tau = (core % 2) * 2048 + np.arange(2048, dtype=np.float64)
    inv = 1.0 / 10000.0 ** (np.arange(0, 64, 2, dtype=np.float64) / 64.0)
    fr = tau[:, None] * inv[None, :]
    sin, cos = np.sin(fr), np.cos(fr)
    p = np.arange(128)
    j = (p % 64) % 32
    sgn = np.where((p % 64) < 32, -1.0, 1.0)
    cos_t = cos[:, j].T
    sin_t = sin[:, j].T * sgn[:, None]
    return {"rope_cos": np.ascontiguousarray(cos_t.astype(BF)),
            "rope_sin": np.ascontiguousarray(sin_t.astype(BF))}


def _host_prep(I):
    x = np.ascontiguousarray(np.asarray(I["L0_attn_x"], np.float32).reshape(16384, D))
    shared = {
        "l0_wq": np.asarray(I["L0_attn_Wq"], np.float32),
        "l0_bq": _pp(I["L0_attn_bq"]),
        "l0_wdown": np.asarray(I["L0_attn_Wdown"], np.float32),
        "l0_bdown": _pp(I["L0_attn_bdown"]),
        "l0_wup": np.asarray(I["L0_attn_Wup"], np.float32),
        "l0_bup": _pp(I["L0_attn_bup"]),
        "l0_wout": np.asarray(I["L0_attn_Wout"], np.float32),
        "l0_bout": _row(I["L0_attn_bout"]),
        "phi_wT": np.ascontiguousarray(np.asarray(I["L0_ffn_phi_w"], np.float32).T),
        "phi_b": _row(I["L0_ffn_phi_b"]),
        "l1_wq": np.ascontiguousarray(np.asarray(I["L1_attn_Wq"], np.float32).T),
        "l1_bq": _pp(I["L1_attn_bq"]),
        "l1_wk": np.ascontiguousarray(np.asarray(I["L1_attn_Wk"], np.float32).T),
        "l1_bk": _pp(I["L1_attn_bk"]),
        "l1_wv": np.ascontiguousarray(np.asarray(I["L1_attn_Wv"], np.float32).T),
        "l1_bv": _row(I["L1_attn_bv"]),
        "l1_wo": np.ascontiguousarray(np.asarray(I["L1_attn_Wo"], np.float32).T),
        "l1_bo": _row(I["L1_attn_bo"]),
        "cn_w1": np.asarray(I["L1_ffn_cn_w1"], np.float32), "cn_b1": _pp(I["L1_ffn_cn_b1"]),
        "cn_w2": np.asarray(I["L1_ffn_cn_w2"], np.float32), "cn_b2": _pp(I["L1_ffn_cn_b2"]),
        "cn_w3": np.asarray(I["L1_ffn_cn_w3"], np.float32), "cn_b3": _pp(I["L1_ffn_cn_b3"]),
        "cn_w4": _pp(I["L1_ffn_cn_w4"]), "cn_b4": _row(I["L1_ffn_cn_b4"]),
        "gc_w1": np.asarray(I["L1_ffn_gc_w1"], np.float32), "gc_b1": _pp(I["L1_ffn_gc_b1"]),
        "gc_w2": np.asarray(I["L1_ffn_gc_w2"], np.float32), "gc_b2": _pp(I["L1_ffn_gc_b2"]),
        "gc_w3": _pp(I["L1_ffn_gc_w3"]), "gc_b3": _row(I["L1_ffn_gc_b3"]),
        "a_range": _row(I["L1_ffn_adaptive_range"]),
    }
    exp_w = np.asarray(I["L0_ffn_expert_w"], np.float32)
    exp_b = np.asarray(I["L0_ffn_expert_b"], np.float32)
    maps = []
    for c in range(NCORE):
        m = dict(shared)
        m["x"] = np.ascontiguousarray(x[TL * c: TL * (c + 1)])
        m["exp_w"] = np.ascontiguousarray(exp_w[8 * c: 8 * c + 8])
        m["exp_b"] = np.ascontiguousarray(exp_b[8 * c: 8 * c + 8])
        m.update(_rope_tables(c))
        maps.append(m)
    return maps



class Arena:
    def __init__(self, nc, stack, name, words, dtype):
        self.t = stack.enter_context(nc.sbuf_tensor(name, [128, words], dtype))
        self.free_list = [(0, words)]
        self.live = {}
        self.hiwater = 0

    def alloc(self, name, shape):
        n = int(np.prod(shape))
        for i, (off, sz) in enumerate(self.free_list):
            if sz >= n:
                if sz == n:
                    self.free_list.pop(i)
                else:
                    self.free_list[i] = (off + n, sz - n)
                self.live[name] = (off, tuple(shape))
                self.hiwater = max(self.hiwater, off + n)
                return self.view(name)
        raise MemoryError(f"arena {self.t.name}: no {n} for {name} free={self.free_list} "
                          f"live={self.live}")

    def view(self, name):
        off, shape = self.live[name]
        n = int(np.prod(shape))
        ap = self.t[:, off:off + n]
        if len(shape) > 1:
            spec = "p (" + " ".join(f"d{i}" for i in range(len(shape))) + ") -> p " + \
                   " ".join(f"d{i}" for i in range(len(shape)))
            ap = ap.rearrange(spec, **{f"d{i}": s for i, s in enumerate(shape)})
        return ap

    def alias(self, src_name, new_name, shape):
        off, _ = self.live[src_name]
        assert int(np.prod(shape)) <= int(np.prod(self.live[src_name][1]))
        self.live[new_name] = (off, tuple(shape))
        return self.view(new_name)

    def freea(self, name):
        off, shape = self.live.pop(name)
        self.free_list.append((off, int(np.prod(shape))))
        self.free_list.sort()
        merged = []
        for o, s in self.free_list:
            if merged and merged[-1][0] + merged[-1][1] == o:
                merged[-1] = (merged[-1][0], merged[-1][1] + s)
            else:
                merged.append((o, s))
        self.free_list = [tuple(m) for m in merged]


class Sem:
    def __init__(self, nc, stack, name):
        self.h = stack.enter_context(nc.semaphore(name))
        self.v = 0

    def inc(self, instr, n=1):
        instr.then_inc(self.h, n)
        self.v += n
        return self.v


def build(taps=(), upto=99, races=True):
    nc = bass.Bass("TRN2", target_bir_lowering=False, detect_race_conditions=races)
    taps = set(taps)

    x_d = nc.dram_tensor("x", [TL, D], F32, kind="ExternalInput")
    out_d = nc.dram_tensor("out", [TL, D], F32, kind="ExternalOutput")

    def din(name, shape):
        return nc.dram_tensor(name, list(shape), F32, kind="ExternalInput")

    w_l0q = din("l0_wq", (D, D)); b_l0q = din("l0_bq", (128, 8))
    w_l0d = din("l0_wdown", (D, 256)); b_l0d = din("l0_bdown", (128, 2))
    w_l0u = din("l0_wup", (256, E)); b_l0u = din("l0_bup", (128, 16))
    w_l0o = din("l0_wout", (D, D)); b_l0o = din("l0_bout", (1, D))
    w_phi = din("phi_wT", (D, E)); b_phi = din("phi_b", (1, E))
    w_expw = din("exp_w", (8, D, D)); w_expb = din("exp_b", (8, D))
    w_l1q = din("l1_wq", (D, D)); b_l1q = din("l1_bq", (128, 8))
    w_l1k = din("l1_wk", (D, 256)); b_l1k = din("l1_bk", (128, 2))
    w_l1v = din("l1_wv", (D, 256)); b_l1v = din("l1_bv", (1, 256))
    w_l1o = din("l1_wo", (D, D)); b_l1o = din("l1_bo", (1, D))
    w_cn1 = din("cn_w1", (D, D)); b_cn1 = din("cn_b1", (128, 8))
    w_cn2 = din("cn_w2", (D, 512)); b_cn2 = din("cn_b2", (128, 4))
    w_cn3 = din("cn_w3", (512, 256)); b_cn3 = din("cn_b3", (128, 2))
    w_cn4 = din("cn_w4", (128, 2)); b_cn4 = din("cn_b4", (1, 1))
    w_gc1 = din("gc_w1", (D, 512)); b_gc1 = din("gc_b1", (128, 4))
    w_gc2 = din("gc_w2", (512, 256)); b_gc2 = din("gc_b2", (128, 2))
    w_gc3 = din("gc_w3", (128, 2)); b_gc3 = din("gc_b3", (1, 1))
    a_rng = din("a_range", (1, 1))
    rcos_d = nc.dram_tensor("rope_cos", [128, TL], BF16, kind="ExternalInput")
    rsin_d = nc.dram_tensor("rope_sin", [128, TL], BF16, kind="ExternalInput")

    ident_d = nc.inline_tensor(np.eye(128, dtype=BF), "ident")
    r_t = np.arange(128) % 8
    bd = (r_t[:, None] == r_t[None, :]).astype(BF)
    mask_d = nc.inline_tensor(np.ascontiguousarray(bd), "bdmask")
    ones_row_d = nc.inline_tensor(np.ones((1, 128), dtype=BF), "onesrow")
    _pj = np.arange(128)
    _j64 = _pj % 64
    _h64 = (_pj // 64) * 64
    _src = np.where(_j64 < 32, _h64 + 2 * _j64 + 1, _h64 + 2 * (_j64 - 32))
    _pm = np.zeros((128, 128), dtype=BF)
    _pm[_src, _pj] = 1.0
    pmat_d = nc.inline_tensor(np.ascontiguousarray(_pm), "permmat")

    c_dram = nc.dram_tensor("c_scr", [TL, E], BF16)
    d_dram = nc.dram_tensor("d_scr", [TL, E], BF16)
    xpart_dram = nc.dram_tensor("xpart_scr", [E, D], F32)
    xown_dram = nc.dram_tensor("xown_scr", [E // 8, D], F32)
    yown_dram = nc.dram_tensor("yown_scr", [E // 8, D], BF16)
    ytil_dram = nc.dram_tensor("ytil_scr", [E, D], BF16)
    k_ag_in = nc.dram_tensor("k_ag_in", [256, 2048], BF16)
    k_ag_out = nc.dram_tensor("k_ag_out", [512, 2048], BF16)
    v_ag_in = nc.dram_tensor("v_ag_in", [2048, 256], BF16)
    v_ag_out = nc.dram_tensor("v_ag_out", [4096, 256], BF16)
    gm_ag_in = nc.dram_tensor("gm_ag_in", [1, D], F32)
    gm_ag_out = nc.dram_tensor("gm_ag_out", [2, D], F32)

    tap_outs = {}
    stack = ExitStack()
    with stack:
        ab = Arena(nc, stack, "AB", 62 * 1024, BF16)
        af = Arena(nc, stack, "AF", 4 * 1024, F32)
        xs_t = stack.enter_context(nc.sbuf_tensor("XS", [128, NTC, D], F32))
        XS = xs_t[:, :, :]

        ps_f = stack.enter_context(nc.psum_tensor("PSF", [128, 2048], F32))
        ps_g = stack.enter_context(nc.psum_tensor("PSG", [128, 1024], F32))
        pst_stack = ExitStack()
        ps_t = pst_stack.enter_context(nc.psum_tensor("PST", [128, 2048], BF16))

        sld = Sem(nc, stack, "sld")
        sl2 = Sem(nc, stack, "sl2")
        sl3 = Sem(nc, stack, "sl3")
        smm = Sem(nc, stack, "smm")
        ssv = Sem(nc, stack, "ssv")
        sss = Sem(nc, stack, "sss")
        ssg = Sem(nc, stack, "ssg")
        sst = Sem(nc, stack, "sst")
        sdt = [Sem(nc, stack, f"sdt{i}") for i in range(4)]

        ident = ab.alloc("ident", (128,))
        maskbd = ab.alloc("maskbd", (128,))
        onesrow = ab.alloc("onesrow", (128,))
        rcos = ab.alloc("rcos", (TL,))
        rsin = ab.alloc("rsin", (TL,))

        stats = af.alloc("stats", (16, 16))
        fscr = af.alloc("fscr", (1024,))

        def tap(name, ap, shape, engine, waits=()):
            if name not in taps:
                return
            t_d = nc.dram_tensor("tap_" + name, list(shape), ap.dtype, kind="ExternalOutput")
            tap_outs[name] = t_d
            for s, v in waits:
                engine.wait_ge(s.h, v)
            sst.inc(engine.dma_start(out=t_d.ap(), in_=ap), 16)

        with nc.Block() as blk:
            @blk.sync
            def _(sy):
                sl2.inc(sy.dma_start(out=XS, in_=x_d.ap().rearrange("(c p) f -> p c f", p=128)), 16)

            @blk.gpsimd
            def _(g):
                for dst, src in ((ident, ident_d[:, :]), (maskbd, mask_d[:, :]),
                                 (onesrow[0:1, :], ones_row_d[:, :]),
                                 (rcos, rcos_d[:, :]), (rsin, rsin_d[:, :])):
                    sld.inc(g.dma_start(out=dst, in_=src), 16)
        consts_ld = sld.v

        def emit_ln_stats(xdep=None):
            with nc.Block() as blk:
                sv0, ss0 = ssv.v, sss.v

                @blk.vector
                def _(v):
                    v.wait_ge(sl2.h, 16)
                    if xdep is not None:
                        v.wait_ge(xdep[0].h, xdep[1])
                    st = stats
                    for tc in range(NTC):
                        v.tensor_scalar(fscr, XS[:, tc, :], 1.0, 0.0, ALU.mult,
                                        ALU.add, accum_out=st[:, 0, tc:tc + 1])
                        v.drain()
                        v.scalar_tensor_tensor(fscr, XS[:, tc, :], 1.0, XS[:, tc, :],
                                               ALU.mult, ALU.mult,
                                               accum_out=st[:, 1, tc:tc + 1])
                        v.drain()
                    v.tensor_scalar_mul(st[:, 2, :], st[:, 0, :], 1.0 / D)
                    v.drain()
                    v.scalar_tensor_tensor(st[:, 3, :], st[:, 2, :], 1.0, st[:, 2, :],
                                           ALU.mult, ALU.mult)
                    v.tensor_scalar_mul(st[:, 4, :], st[:, 1, :], 1.0 / D)
                    v.drain()
                    v.tensor_sub(st[:, 5, :], st[:, 4, :], st[:, 3, :])
                    v.drain()
                    v.tensor_scalar_add(st[:, 5, :], st[:, 5, :], EPS)
                    v.drain()
                    ssv.inc(v.reciprocal(st[:, 6, :], st[:, 5, :]))
                    v.wait_ge(sss.h, ss0 + 1)
                    v.tensor_mul(st[:, 8, :], st[:, 2, :], st[:, 7, :])
                    v.drain()
                    ssv.inc(v.tensor_scalar_mul(st[:, 9, :], st[:, 8, :], -1.0))

                @blk.scalar
                def _(s):
                    s.wait_ge(ssv.h, sv0 + 1)
                    sss.inc(s.activation(stats[:, 7, :], stats[:, 6, :], AF.Sqrt))
            return ssv.v

        def emit_xln_slice(tc0, ntc, tokbuf, fmbuf, stats_dep, fm_off=0):
            with nc.Block() as blk:
                sv0, mm0 = ssv.v, smm.v

                @blk.vector
                def _(v):
                    v.wait_ge(ssv.h, stats_dep)
                    for i in range(ntc):
                        tc = tc0 + i
                        ssv.inc(v.tensor_scalar(tokbuf[:, i, :], XS[:, tc, :],
                                                stats[:, 7, tc:tc + 1], stats[:, 9, tc:tc + 1],
                                                ALU.mult, ALU.add))
                    for i in range(ntc):
                        v.wait_ge(smm.h, mm0 + i + 1)
                        bank = (i % 2) * 1024
                        ssv.inc(v.tensor_copy(
                            fmbuf[:, :, fm_off + i * 128:fm_off + (i + 1) * 128],
                            ps_t[:, bank:bank + DC * 128].rearrange("p (f q) -> p f q", q=128)))

                @blk.tensor
                def _(t):
                    t.wait_ge(sld.h, sld.v)
                    t.wait_ge(ssv.h, sv0)
                    for i in range(ntc):
                        t.wait_ge(ssv.h, sv0 + i + 1)
                        if i >= 2:
                            t.wait_ge(ssv.h, sv0 + ntc + i - 1)
                        bank = (i % 2) * 1024
                        for fc in range(DC):
                            ins = t.transpose(ps_t[:, bank + fc * 128: bank + (fc + 1) * 128],
                                              tokbuf[:, i, fc * 128:(fc + 1) * 128], ident)
                        smm.inc(ins)
            return ssv.v

        def emit_l0_attn(stats_dep):
            TS = 256
            NSL = TL // TS
            wq_sb = ab.alloc("wq_sb", (DC, D))
            wdn_sb = ab.alloc("wdn_sb", (DC, 256))
            wup_sb = ab.alloc("wup_sb", (2, E))
            wout_sb = ab.alloc("wout_sb", (DC, D))
            boutr = ab.alloc("boutr", (D,))
            bq_f = af.alloc("bq_f", (8,))
            bq8_f = af.alloc("bq8_f", (8,))
            bdn_f = af.alloc("bdn_f", (2,))
            bup_f = af.alloc("bup_f", (16,))
            with nc.Block() as blk:
                sd0 = sl2.v

                @blk.gpsimd
                def _(g):
                    sld.inc(g.dma_start(out=wq_sb, in_=w_l0q.ap().rearrange("(i p) o -> p i o", p=128)), 16)
                    sld.inc(g.dma_start(out=wdn_sb, in_=w_l0d.ap().rearrange("(i p) o -> p i o", p=128)), 16)
                    sld.inc(g.dma_start(out=wup_sb, in_=w_l0u.ap().rearrange("(i p) o -> p i o", p=128)), 16)
                    sld.inc(g.dma_start(out=wout_sb, in_=w_l0o.ap().rearrange("(i p) o -> p i o", p=128)), 16)
                    sld.inc(g.dma_start(out=boutr[0:1, :], in_=b_l0o[:, :]), 16)

                @blk.sync
                def _(sy):
                    sl2.inc(sy.dma_start(out=bq_f, in_=b_l0q[:, :]), 16)
                    sl2.inc(sy.dma_start(out=bdn_f, in_=b_l0d[:, :]), 16)
                    sl2.inc(sy.dma_start(out=bup_f, in_=b_l0u[:, :]), 16)

                @blk.vector
                def _(v):
                    v.wait_ge(sl2.h, sd0 + 48)
                    ssv.inc(v.tensor_scalar_mul(bq8_f, bq_f, 0.125))
            wld, bld, b8 = sld.v, sl2.v, ssv.v

            xtok_sl = ab.alloc("xtok_sl", (2, D))
            xlnT_sl = ab.alloc("xlnT_sl", (DC, TS))
            q_fm = ab.alloc("q_fm", (DC, TS))
            kv_fm = ab.alloc("kv_fm", (16, TS))
            lat_fm = ab.alloc("lat_fm", (2, TS))
            rt1 = ab.alloc("rt1", (16, TS))
            rt2 = ab.alloc("rt2", (16, TS))
            bufA = ab.alloc("bufA", (2, 1024))
            bufB = ab.alloc("bufB", (2, 1024))
            ptb = ab.alloc("ptb", (2, 1024))
            vgb = ab.alloc("vgb", (2, 8, 64))
            odmj = ab.alloc("odmj", (2, 1024))
            o_fm = ab.alloc("o_fm", (DC, TS))
            submr = af.alloc("submr", (2, 1024))
            mrow = af.alloc("mrow", (2, 8))
            zrow = af.alloc("zrow", (2, 8))
            zrec = af.alloc("zrec", (2, 8))
            q_grp = ab.alloc("q_grp", (32, 64))
            kg2 = ab.alias("rt1", "kg2", (32, 128))
            vgrpb = ab.alias("rt2", "vgrpb", (32, 128))

            for sl in range(NSL):
                t0 = sl * TS
                emit_xln_slice(sl * 2, 2, ab.view("xtok_sl"), ab.view("xlnT_sl"), stats_dep)
                xlnT_v = ab.view("xlnT_sl")

                with nc.Block() as blk:
                    mm0, sv0 = smm.v, ssv.v

                    @blk.tensor
                    def _(t):
                        t.wait_ge(sld.h, wld)
                        t.wait_ge(ssv.h, sv0)
                        for oc in range(DC):
                            if oc >= 4:
                                t.wait_ge(ssv.h, sv0 + (oc - 4) + 1)
                            pb = ps_f[:, (oc % 4) * 512:(oc % 4) * 512 + TS]
                            for ic in range(DC):
                                ins = t.matmul(pb, wq_sb[:, ic, oc * 128:(oc + 1) * 128],
                                               xlnT_v[:, ic, :],
                                               start=(ic == 0), stop=(ic == DC - 1))
                            smm.inc(ins)
                        for oc in range(2):
                            pb = ps_g[:, oc * 512: oc * 512 + TS]
                            for ic in range(DC):
                                ins = t.matmul(pb, wdn_sb[:, ic, oc * 128:(oc + 1) * 128],
                                               xlnT_v[:, ic, :],
                                               start=(ic == 0), stop=(ic == DC - 1))
                            smm.inc(ins)
                        t.wait_ge(ssv.h, sv0 + 10)
                        for oc in range(16):
                            if oc >= 4:
                                t.wait_ge(ssv.h, sv0 + 10 + (oc - 4) + 1)
                            pb = ps_f[:, (oc % 4) * 512:(oc % 4) * 512 + TS]
                            for ic in range(2):
                                ins = t.matmul(pb, wup_sb[:, ic, oc * 128:(oc + 1) * 128],
                                               lat_fm[:, ic, :],
                                               start=(ic == 0), stop=(ic == 1))
                            smm.inc(ins)

                    @blk.vector
                    def _(v):
                        v.wait_ge(sl2.h, bld)
                        v.wait_ge(ssv.h, b8)
                        for oc in range(DC):
                            v.wait_ge(smm.h, mm0 + oc + 1)
                            pb = ps_f[:, (oc % 4) * 512:(oc % 4) * 512 + TS]
                            ssv.inc(v.tensor_scalar(q_fm[:, oc, :], pb, 0.125,
                                                    bq8_f[:, oc:oc + 1], ALU.mult, ALU.add))
                        for oc in range(2):
                            v.wait_ge(smm.h, mm0 + 8 + oc + 1)
                            pb = ps_g[:, oc * 512: oc * 512 + TS]
                            ssv.inc(v.tensor_scalar(lat_fm[:, oc, :], pb,
                                                    bdn_f[:, oc:oc + 1], 0.0, ALU.add, ALU.add))
                        for oc in range(16):
                            v.wait_ge(smm.h, mm0 + 10 + oc + 1)
                            pb = ps_f[:, (oc % 4) * 512:(oc % 4) * 512 + TS]
                            ssv.inc(v.tensor_scalar(kv_fm[:, oc, :], pb,
                                                    bup_f[:, oc:oc + 1], 0.0, ALU.add, ALU.add))

                with nc.Block() as blk:
                    sv0, sd0 = ssv.v, sl2.v

                    @blk.sync
                    def _(sy):
                        sy.wait_ge(ssv.h, sv0)
                        for dst0, src0 in ((0, 32), (32, 0), (64, 96), (96, 64)):
                            sl2.inc(sy.dma_start(out=rt1[dst0:dst0 + 32, 0:8, :],
                                                 in_=q_fm[src0:src0 + 32, :, :]), 16)
                        for dst0, src0 in ((0, 32), (32, 0)):
                            sl2.inc(sy.dma_start(out=rt2[dst0:dst0 + 32, :, :],
                                                 in_=kv_fm[src0:src0 + 32, :, :]), 16)

                    @blk.vector
                    def _(v):
                        cq = rcos[:, t0:t0 + TS].unsqueeze(1)
                        sq = rsin[:, t0:t0 + TS].unsqueeze(1)
                        ck = rcos[0:64, t0:t0 + TS].unsqueeze(1)
                        sk = rsin[0:64, t0:t0 + TS].unsqueeze(1)
                        v.wait_ge(sl2.h, sd0 + 96)
                        v.tensor_mul(rt1[:, 8:16, :], rt1[:, 0:8, :],
                                     sq.broadcast_to((128, 8, TS)))
                        v.drain()
                        v.tensor_mul(rt1[:, 0:8, :], q_fm, cq.broadcast_to((128, 8, TS)))
                        v.drain()
                        ssv.inc(v.tensor_add(q_fm, rt1[:, 0:8, :], rt1[:, 8:16, :]))
                        v.drain()
                        v.tensor_mul(rt1[0:64, :, :], rt2[0:64, :, :],
                                     sk.broadcast_to((64, 16, TS)))
                        v.drain()
                        v.tensor_mul(rt2[0:64, :, :], kv_fm[0:64, :, :],
                                     ck.broadcast_to((64, 16, TS)))
                        v.drain()
                        ssv.inc(v.tensor_add(kv_fm[0:64, :, :], rt2[0:64, :, :],
                                             rt1[0:64, :, :]))
                        v.drain()
                        v.tensor_copy(q_grp[0:64, :, :].rearrange("p g (c t) -> p g c t", t=8),
                                      q_fm[0:64, :, :].rearrange("p c (g t) -> p g c t", t=8))
                        v.tensor_copy(q_grp[64:128, :, :].rearrange("p g (c t) -> p g c t", t=8),
                                      q_fm[64:128, :, :].rearrange("p c (g t) -> p g c t", t=8))
                        ssv.inc(v.tensor_copy(
                            kg2[0:64, :, :].rearrange("p g (gg t) -> p g gg t", t=8),
                            kv_fm[0:64, :, :].rearrange("p gg (g t) -> p g gg t", t=8)))
                        ssv.inc(v.tensor_copy(
                            vgrpb[0:64, :, :].rearrange("p g (gg t) -> p g gg t", t=8),
                            kv_fm[64:128, :, :].rearrange("p gg (g t) -> p g gg t", t=8)))

                    @blk.gpsimd
                    def _(g):
                        g.wait_ge(ssv.h, sv0 + 3)
                        sld.inc(g.dma_start(out=kg2[64:128, :, :], in_=kg2[0:64, :, :]), 16)

                rope_sv, rope_ld = ssv.v, sld.v

                NB = 4
                with nc.Block() as blk:
                    mm0, sv0, ss0 = smm.v, ssv.v, sss.v

                    @blk.tensor
                    def _(t):
                        t.wait_ge(ssv.h, rope_sv)
                        t.wait_ge(sld.h, rope_ld)
                        for b in range(NB):
                            b2 = b % 2
                            if b >= 2:
                                t.wait_ge(ssv.h, sv0 + 5 * (b - 2) + 1)
                            for j in range(8):
                                grp = b * 8 + j
                                pb = ps_f[:, b2 * 1024 + j * 128: b2 * 1024 + (j + 1) * 128]
                                t.matmul(pb[0:64, :], q_grp[0:64, grp, :],
                                         kg2[0:64, grp, :],
                                         start=True, stop=True, tile_position=(0, 0))
                                ins = t.matmul(pb[64:128, :], q_grp[64:128, grp, :],
                                               kg2[64:128, grp, :],
                                               start=True, stop=True, tile_position=(64, 64))
                            smm.inc(ins)
                            t.wait_ge(ssv.h, sv0 + 5 * b + 2)
                            if b >= 1:
                                t.wait_ge(ssv.h, sv0 + 5 * (b - 1) + 3)
                            for j in range(8):
                                ins = t.transpose(ps_t[:, j * 128:(j + 1) * 128],
                                                  bufA[:, b2, j * 128:(j + 1) * 128], ident)
                            smm.inc(ins)
                            if b >= 1:
                                t.wait_ge(ssv.h, sv0 + 5 * (b - 1) + 4)
                            for j in range(8):
                                grp = b * 8 + j
                                ins = t.transpose(ps_t[:, 1024 + j * 64: 1024 + (j + 1) * 64],
                                                  vgrpb[0:64, grp, :],
                                                  ident[0:64, 0:64])
                            smm.inc(ins)
                            t.wait_ge(ssv.h, sv0 + 5 * b + 4)
                            if b >= 1:
                                t.wait_ge(ssv.h, sv0 + 5 * (b - 1) + 5)
                            for j in range(8):
                                ins = t.matmul(ps_g[0:64, j * 128:(j + 1) * 128],
                                               vgb[:, b2, j, :],
                                               ptb[:, b2, j * 128:(j + 1) * 128],
                                               start=True, stop=True)
                            smm.inc(ins)

                    @blk.vector
                    def _(v):
                        v.wait_ge(sld.h, rope_ld)
                        for b in range(NB):
                            b2 = b % 2
                            if b >= 2:
                                v.wait_ge(sss.h, ss0 + (b - 2) + 1)
                            v.wait_ge(smm.h, mm0 + 4 * b + 1)
                            for hh in range(2):
                                pbv = ps_f[:, b2 * 1024 + hh * 512: b2 * 1024 + (hh + 1) * 512]\
                                    .rearrange("p (g q) -> p g q", q=128)
                                v.reduce_max(mrow[:, b2, hh * 4:(hh + 1) * 4], pbv, axis=AX.X,
                                             negate=True)
                                v.drain()
                                ins = v.tensor_add(
                                    submr[:, b2, hh * 512:(hh + 1) * 512]
                                    .rearrange("p (g q) -> p g q", q=128), pbv,
                                    mrow[:, b2, hh * 4:(hh + 1) * 4].unsqueeze(2)
                                    .broadcast_to((128, 4, 128)))
                            ssv.inc(ins)
                            v.wait_ge(sss.h, ss0 + b + 1)
                            va = bufA[:, b2, :].rearrange("p (g q) -> p g q", q=128)
                            vb = bufB[:, b2, :].rearrange("p (g q) -> p g q", q=128)
                            v.tensor_mul(vb, va, maskbd.unsqueeze(1).broadcast_to((128, 8, 128)))
                            v.drain()
                            v.reduce_sum(zrow[:, b2, :], vb, axis=AX.X)
                            v.drain()
                            v.reciprocal(zrec[:, b2, :], zrow[:, b2, :])
                            v.drain()
                            ssv.inc(v.tensor_mul(va, vb, zrec[:, b2, :].unsqueeze(2)
                                                 .broadcast_to((128, 8, 128))))
                            v.wait_ge(smm.h, mm0 + 4 * b + 2)
                            ssv.inc(v.tensor_copy(ptb[:, b2, :], ps_t[:, 0:1024]))
                            v.wait_ge(smm.h, mm0 + 4 * b + 3)
                            ssv.inc(v.tensor_copy(
                                vgb[:, b2, :, :],
                                ps_t[:, 1024:1536].rearrange("p (j dd) -> p j dd", dd=64)))
                            v.wait_ge(smm.h, mm0 + 4 * b + 4)
                            bp = (b % 2) * 64
                            ssv.inc(v.tensor_copy(
                                odmj[bp:bp + 64, b // 2, :], ps_g[0:64, 0:1024]))

                    @blk.scalar
                    def _(s):
                        for b in range(NB):
                            b2 = b % 2
                            s.wait_ge(ssv.h, sv0 + 5 * b + 1)
                            if b >= 2:
                                s.wait_ge(smm.h, mm0 + 4 * (b - 2) + 2)
                            sss.inc(s.activation(bufA[:, b2, :], submr[:, b2, :], AF.Exp))

                att_sv = ssv.v

                with nc.Block() as blk:
                    mm0, sv0 = smm.v, ssv.v

                    @blk.vector
                    def _(v):
                        v.wait_ge(ssv.h, att_sv)
                        for h in range(16):
                            off = 64 * (h % 2) + (h // 2) * 8
                            hb = (h % 2) * 64
                            for bp in range(2):
                                src_ap = odmj[bp * 64:bp * 64 + 64, :, :].rearrange(
                                    "p bb (j r) -> p bb j r", r=128)[:, :, :, off:off + 8]
                                dst_ap = o_fm[hb:hb + 64, h // 2, :].rearrange(
                                    "p (bb q) -> p bb q", bb=2)[:, :, 64 * bp:64 * bp + 64].rearrange(
                                    "p bb (j t) -> p bb j t", t=8)
                                ssv.inc(v.tensor_copy(dst_ap, src_ap))
                        for ts_ in range(2):
                            v.wait_ge(smm.h, mm0 + ts_ + 1)
                            tci = sl * 2 + ts_
                            pg = ps_g[:, 0:1024] if ts_ == 0 else ps_f[:, 0:1024]
                            ssv.inc(v.tensor_add(XS[:, tci, :], XS[:, tci, :], pg))

                    @blk.tensor
                    def _(t):
                        t.wait_ge(ssv.h, sv0 + 32)
                        for ts_ in range(2):
                            pg = ps_g if ts_ == 0 else ps_f
                            for nn in range(2):
                                pb = pg[:, nn * 512:(nn + 1) * 512]
                                for ic in range(DC):
                                    t.matmul(pb, o_fm[:, ic, ts_ * 128:(ts_ + 1) * 128],
                                             wout_sb[:, ic, nn * 512:(nn + 1) * 512],
                                             start=(ic == 0), stop=False)
                                ins = t.matmul(pb, onesrow[0:1, 0:128],
                                               boutr[0:1, nn * 512:(nn + 1) * 512],
                                               start=False, stop=True)
                            smm.inc(ins)

            DBG["ab"] = dict(ab.live)
            DBG["af"] = dict(af.live)
            with nc.Block() as blk:
                @blk.sync
                def _(sy):
                    w = [(ssv, ssv.v), (smm, smm.v), (sss, sss.v)]
                    tap("p1_qfm", ab.view("q_fm"), (128, DC, 256), sy, waits=w)
                    tap("p1_kvfm", ab.view("kv_fm"), (128, 16, 256), sy, waits=w)
                    tap("p1_qgrp", ab.view("q_grp"), (128, 32, 64), sy, waits=w)
                    tap("p1_kg2", ab.view("kg2"), (128, 32, 128), sy, waits=w)
                    tap("p1_vgrpb", ab.view("vgrpb"), (128, 32, 128), sy, waits=w)
                    tap("p1_ofm", ab.view("o_fm"), (128, DC, 256), sy, waits=w)
                    tap("p1_odmj", ab.view("odmj"), (128, 2, 1024), sy, waits=w)
                    tap("p1_bufA", ab.view("bufA"), (128, 2, 1024), sy, waits=w)
            for nm in ("xtok_sl", "xlnT_sl", "q_fm", "kv_fm", "lat_fm", "rt1", "rt2",
                       "bufA", "bufB", "ptb", "vgb", "odmj", "o_fm", "q_grp",
                       "wq_sb", "wdn_sb", "wup_sb", "wout_sb", "boutr"):
                ab.freea(nm)
            ab.live.pop("kg2", None); ab.live.pop("vgrpb", None)
            for nm in ("bq_f", "bq8_f", "bdn_f", "bup_f", "submr", "mrow", "zrow", "zrec"):
                af.freea(nm)

        def emit_l0_ffn(stats_dep):
            xtok2 = ab.alloc("xtok2", (NTC, D))
            phi_sb = ab.alloc("phi_sb", (DC, E))
            phib_sb = ab.alloc("phib_sb", (E,))
            ebuf = ab.alloc("ebuf", (2, E))
            cstage = ab.alloc("cstage", (2, E))
            xlnT_tc = ab.alloc("xlnT_tc", (2, DC, 128))
            neg10 = af.alloc("neg10", (1,))
            sdrec = af.alloc("sdrec", (2, 64))
            screc = af.alloc("screc", (2, 32))

            with nc.Block() as blk:
                @blk.gpsimd
                def _(g):
                    sld.inc(g.dma_start(out=phi_sb, in_=w_phi.ap().rearrange("(i p) o -> p i o", p=128)), 16)
                    sld.inc(g.dma_start(out=phib_sb[0:1, :], in_=b_phi[:, :]), 16)

                @blk.vector
                def _(v):
                    v.wait_ge(ssv.h, stats_dep)
                    v.memset(neg10, -10.0)
                    for tc in range(NTC):
                        ssv.inc(v.tensor_scalar(xtok2[:, tc, :], XS[:, tc, :],
                                                stats[:, 7, tc:tc + 1], stats[:, 9, tc:tc + 1],
                                                ALU.mult, ALU.add))
            wld3 = sld.v
            xln2_done = ssv.v

            with nc.named_scope("logits"):
                with nc.Block() as blk:
                    mm0, sv0, ss0, sd0, sg0, sgc0 = (smm.v, ssv.v, sss.v, sl2.v,
                                                     sld.v, ssg.v)

                    @blk.tensor
                    def _(t):
                        t.wait_ge(sld.h, wld3)
                        t.wait_ge(ssv.h, xln2_done)
                        for fc in range(DC):
                            ins = t.transpose(ps_t[:, fc * 128:(fc + 1) * 128],
                                              xtok2[:, 0, fc * 128:(fc + 1) * 128], ident)
                        smm.inc(ins)
                        for tc in range(NTC):
                            tp = tc % 2
                            if tc + 1 < NTC:
                                tpn = (tc + 1) % 2
                                if tc >= 1:
                                    t.wait_ge(ssv.h, sv0 + 3 * (tc - 1) + 1)
                                for fc in range(DC):
                                    ins = t.transpose(
                                        ps_t[:, tpn * 1024 + fc * 128:tpn * 1024 + (fc + 1) * 128],
                                        xtok2[:, tc + 1, fc * 128:(fc + 1) * 128], ident)
                                smm.inc(ins)
                            t.wait_ge(ssv.h, sv0 + 3 * tc + 1)
                            if tc >= 1:
                                t.wait_ge(sss.h, ss0 + tc)
                            for nn in range(4):
                                pb = ps_f[:, nn * 512:(nn + 1) * 512]
                                for ic in range(DC):
                                    t.matmul(pb, xlnT_tc[:, tp, ic, :],
                                             phi_sb[:, ic, nn * 512:(nn + 1) * 512],
                                             start=(ic == 0), stop=False)
                                ins = t.matmul(pb, onesrow[0:1, 0:128],
                                               phib_sb[0:1, nn * 512:(nn + 1) * 512],
                                               start=False, stop=True)
                            smm.inc(ins)

                    @blk.scalar
                    def _(s):
                        for tc in range(NTC):
                            tp = tc % 2
                            s.wait_ge(smm.h, mm0 + min(2 * tc + 3, 2 * NTC))
                            if tc >= 2:
                                s.wait_ge(sld.h, sg0 + 16 * (tc - 1))
                            sss.inc(s.activation(ebuf[:, tp, :], ps_f[:, 0:2048], AF.Exp,
                                                 bias=neg10))

                    @blk.vector
                    def _(v):
                        for tc in range(NTC):
                            tp = tc % 2
                            v.wait_ge(smm.h, mm0 + (1 if tc == 0 else 2 * tc))
                            ssv.inc(v.tensor_copy(
                                xlnT_tc[:, tp, :, :],
                                ps_t[:, tp * 1024:tp * 1024 + 1024].rearrange("p (f q) -> p f q", q=128)))
                            v.wait_ge(sss.h, ss0 + tc + 1)
                            if tc >= 2:
                                v.wait_ge(ssg.h, sgc0 + tc - 1)
                            ev = ebuf[:, tp, :].rearrange("p (n q) -> p n q", q=32)
                            ev2 = ebuf[:, tp, :].rearrange("p (n q) -> p q n", q=32)
                            v.reduce_sum(sdrec[:, tp, :], ev, axis=AX.X)
                            v.reduce_sum(screc[:, tp, :], ev2, axis=AX.X)
                            v.drain()
                            v.reciprocal(sdrec[:, tp, :], sdrec[:, tp, :])
                            ssv.inc(v.reciprocal(screc[:, tp, :], screc[:, tp, :]))
                            if tc >= 2:
                                v.wait_ge(sl2.h, sd0 + 16 * (tc - 1))
                            v.drain()
                            ssv.inc(v.tensor_mul(
                                cstage[:, tp, :].rearrange("p (n q) -> p n q", q=32),
                                ev, screc[:, tp, :].unsqueeze(1).broadcast_to((128, 64, 32))))
                            v.drain()

                    @blk.sync
                    def _(sy):
                        for tc in range(NTC):
                            tp = tc % 2
                            sy.wait_ge(ssv.h, sv0 + 3 * tc + 3)
                            sy.wait_ge(sl2.h, sl2.v)
                            sl2.inc(sy.dma_start(
                                out=c_dram.ap().rearrange("(c p) e -> p c e", p=128)[:, tc, :],
                                in_=cstage[:, tp, :]), 16)

                    @blk.gpsimd
                    def _(g):
                        for tc in range(NTC):
                            tp = tc % 2
                            g.wait_ge(ssv.h, sv0 + 3 * tc + 3)
                            ev = ebuf[:, tp, :].rearrange("p (n q) -> p n q", q=32)
                            ssg.inc(g.tensor_mul(
                                ev, ev, sdrec[:, tp, :].unsqueeze(2).broadcast_to((128, 64, 32))))
                            g.wait_ge(ssg.h, ssg.v)
                            g.wait_ge(sld.h, sld.v)
                            sld.inc(g.dma_start(
                                out=d_dram.ap().rearrange("(c p) e -> p c e", p=128)[:, tc, :],
                                in_=ebuf[:, tp, :]), 16)

                d_dmas = (sl2.v, sld.v)
                for nm in ("phi_sb", "phib_sb", "ebuf", "cstage", "xlnT_tc"):
                    ab.freea(nm)

            with nc.named_scope("xpart"):
                dtile = ab.alloc("dtile", (4, 256))
                xstg = ab.alloc("xstg", (2, 2048))
                ab.freea("xstg")
                xstg = af.alloc("xstg", (2, 1024))
                with nc.Block() as blk:
                    mm0, sv0, sd0, st0 = smm.v, ssv.v, sl2.v, sl3.v

                    @blk.sync
                    def _(sy):
                        sy.wait_ge(sld.h, d_dmas[1])
                        for ps in range(8):
                            for tc in range(NTC):
                                idx = ps * NTC + tc
                                if idx >= 4:
                                    sy.wait_ge(smm.h, mm0 + (idx - 3))
                                sdt[idx % 4].inc(sy.dma_start(
                                    out=dtile[:, idx % 4, :],
                                    in_=d_dram[tc * 128:(tc + 1) * 128, ps * 256:(ps + 1) * 256]), 16)

                    @blk.tensor
                    def _(t):
                        for ps in range(8):
                            if ps >= 1:
                                t.wait_ge(ssv.h, sv0 + 2 * ps)
                            for tc in range(NTC):
                                idx = ps * NTC + tc
                                t.wait_ge(sdt[idx % 4].h, 16 * (idx // 4 + 1))
                                for ei in range(2):
                                    for nn in range(2):
                                        ins = t.matmul(
                                            ps_f[:, (ei * 2 + nn) * 512:(ei * 2 + nn + 1) * 512],
                                            dtile[:, idx % 4, ei * 128:(ei + 1) * 128],
                                            xtok2[:, tc, nn * 512:(nn + 1) * 512],
                                            start=(tc == 0), stop=(tc == NTC - 1))
                                smm.inc(ins)

                    @blk.vector
                    def _(v):
                        for ps in range(8):
                            v.wait_ge(smm.h, mm0 + NTC * (ps + 1))
                            for ei in range(2):
                                if ps >= 1:
                                    v.wait_ge(sl3.h, st0 + 16 * (2 * (ps - 1) + ei + 1))
                                ssv.inc(v.tensor_copy(xstg[:, ei, :],
                                                      ps_f[:, ei * 1024:(ei + 1) * 1024]))


                    @blk.scalar
                    def _(s):
                        for ps in range(8):
                            for ei in range(2):
                                s.wait_ge(ssv.h, sv0 + 2 * ps + ei + 1)
                                s.wait_ge(sl3.h, sl3.v)
                                sl3.inc(s.dma_start(
                                    out=xpart_dram[(2 * ps + ei) * 128:(2 * ps + ei + 1) * 128, :],
                                    in_=xstg[:, ei, :]), 16)

                xpart_done = sl3.v
                ab.freea("dtile"); ab.freea("xtok2")
                af.freea("xstg"); af.freea("neg10"); af.freea("sdrec"); af.freea("screc")

            with nc.named_scope("experts"):
                xn = ab.alloc("xn", (2, D))
                xnT = ab.alloc("xnT", (DC, 256))
                expw = ab.alloc("expw", (2, DC, D))
                expb_sb = ab.alloc("expb_sb", (8 * D,))
                ystg = ab.alloc("ystg", (2, D))
                xow = af.alloc("xow", (2, D))
                xst = af.alloc("xst", (12, 2))

                with nc.Block() as blk:
                    sg0, ld0 = ssg.v, sld.v

                    @blk.gpsimd
                    def _(g):
                        g.wait_ge(sl3.h, xpart_done)
                        ssg.inc(g.collective_compute(
                            "ReduceScatter", ALU.add, replica_groups=[list(range(NCORE))],
                            ins=[xpart_dram.ap().opt()], outs=[xown_dram.ap().opt()]))
                        g.wait_ge(ssg.h, sg0 + 1)
                        sld.inc(g.dma_start(out=xow,
                                            in_=xown_dram.ap().rearrange("(c p) f -> p c f", p=128)), 16)
                        sld.inc(g.dma_start(out=expb_sb[0:1, :],
                                            in_=w_expb.ap().rearrange("n h -> (n h)").unsqueeze(0)), 16)
                        g.wait_ge(sld.h, sld.v)
                        sld.inc(g.dma_start(out=expw[:, 0, :, :],
                                            in_=w_expw[0, :, :].rearrange("(i p) o -> p i o", p=128)), 16)
                        g.wait_ge(sld.h, sld.v)
                        sld.inc(g.dma_start(out=expw[:, 1, :, :],
                                            in_=w_expw[1, :, :].rearrange("(i p) o -> p i o", p=128)), 16)
                rs_ld = sld.v

                with nc.Block() as blk:
                    sv0, ss0, mm0 = ssv.v, sss.v, smm.v

                    @blk.vector
                    def _(v):
                        v.wait_ge(sld.h, rs_ld)
                        v.tensor_scalar(xow, xow, 65000.0, -33000.0, ALU.min, ALU.max)
                        v.drain()
                        for i in range(2):
                            v.tensor_scalar(fscr, xow[:, i, :], 1.0, 0.0, ALU.mult,
                                            ALU.add, accum_out=xst[:, 0, i:i + 1])
                            v.drain()
                            v.scalar_tensor_tensor(fscr, xow[:, i, :], 1.0, xow[:, i, :],
                                                   ALU.mult, ALU.mult,
                                                   accum_out=xst[:, 1, i:i + 1])
                            v.drain()
                        v.tensor_scalar_mul(xst[:, 2, :], xst[:, 0, :], 1.0 / D)
                        v.drain()
                        v.scalar_tensor_tensor(xst[:, 3, :], xst[:, 2, :], 1.0, xst[:, 2, :],
                                               ALU.mult, ALU.mult)
                        v.tensor_scalar_mul(xst[:, 4, :], xst[:, 1, :], 1.0 / D)
                        v.drain()
                        v.tensor_sub(xst[:, 5, :], xst[:, 4, :], xst[:, 3, :])
                        v.drain()
                        v.tensor_scalar_add(xst[:, 5, :], xst[:, 5, :], EPS)
                        v.drain()
                        ssv.inc(v.reciprocal(xst[:, 6, :], xst[:, 5, :]))
                        v.wait_ge(sss.h, ss0 + 1)
                        v.tensor_mul(xst[:, 8, :], xst[:, 2, :], xst[:, 7, :])
                        v.drain()
                        v.tensor_scalar_mul(xst[:, 9, :], xst[:, 8, :], -1.0)
                        v.drain()
                        for i in range(2):
                            ssv.inc(v.tensor_scalar(xn[:, i, :], xow[:, i, :],
                                                    xst[:, 7, i:i + 1], xst[:, 9, i:i + 1],
                                                    ALU.mult, ALU.add))
                        for i in range(2):
                            v.wait_ge(smm.h, mm0 + i + 1)
                            ssv.inc(v.tensor_copy(
                                xnT[:, :, i * 128:(i + 1) * 128],
                                ps_t[:, (i % 2) * 1024:(i % 2) * 1024 + 1024]
                                .rearrange("p (f q) -> p f q", q=128)))

                    @blk.scalar
                    def _(s):
                        s.wait_ge(ssv.h, sv0 + 1)
                        sss.inc(s.activation(xst[:, 7, :], xst[:, 6, :], AF.Sqrt))
                        for i in range(2):
                            s.wait_ge(ssv.h, sv0 + 1 + i + 1)
                            sss.inc(s.activation(xn[:, i, :], xn[:, i, :], AF.Tanh))

                    @blk.tensor
                    def _(t):
                        for i in range(2):
                            t.wait_ge(sss.h, ss0 + 1 + i + 1)
                            for fc in range(DC):
                                ins = t.transpose(
                                    ps_t[:, (i % 2) * 1024 + fc * 128:(i % 2) * 1024 + (fc + 1) * 128],
                                    xn[:, i, fc * 128:(fc + 1) * 128], ident)
                            smm.inc(ins)
                xnt_done = ssv.v

                with nc.Block() as blk:
                    mm0, sv0, st0, ld0 = smm.v, ssv.v, sl3.v, sld.v

                    @blk.gpsimd
                    def _(g):
                        for n in range(2, 8):
                            g.wait_ge(smm.h, mm0 + (n - 2) + 1)
                            g.wait_ge(sld.h, sld.v)
                            sld.inc(g.dma_start(out=expw[:, n % 2, :, :],
                                                in_=w_expw[n, :, :].rearrange("(i p) o -> p i o", p=128)), 16)

                    @blk.tensor
                    def _(t):
                        t.wait_ge(ssv.h, xnt_done)
                        for n in range(8):
                            t.wait_ge(sld.h, ld0 + 16 * max(0, n - 1) if n >= 2 else ld0)
                            if n >= 1:
                                t.wait_ge(ssv.h, sv0 + n)
                            for nn in range(2):
                                pb = ps_g[0:32, nn * 512:(nn + 1) * 512]
                                for ic in range(DC):
                                    t.matmul(pb, xnT[:, ic, n * 32:(n + 1) * 32],
                                             expw[:, n % 2, ic, nn * 512:(nn + 1) * 512],
                                             start=(ic == 0), stop=False)
                                ins = t.matmul(pb, onesrow[0:1, 0:32],
                                               expb_sb[0:1, n * D + nn * 512: n * D + (nn + 1) * 512],
                                               start=False, stop=True)
                            smm.inc(ins)

                    @blk.vector
                    def _(v):
                        for n in range(8):
                            v.wait_ge(smm.h, mm0 + n + 1)
                            if n >= 2:
                                v.wait_ge(sl3.h, st0 + 16 * (n - 1))
                            ssv.inc(v.tensor_copy(ystg[0:32, n % 2, :], ps_g[0:32, 0:1024]))

                    @blk.scalar
                    def _(s):
                        for n in range(8):
                            s.wait_ge(ssv.h, sv0 + n + 1)
                            s.wait_ge(sl3.h, sl3.v)
                            sl3.inc(s.dma_start(out=yown_dram[n * 32:(n + 1) * 32, :],
                                                in_=ystg[0:32, n % 2, :]), 16)
                yown_done = sl3.v
                for nm in ("xn", "xnT", "expw", "expb_sb", "ystg"):
                    ab.freea(nm)
                af.freea("xow"); af.freea("xst")

            with nc.named_scope("ymm"):
                ytil = ab.alloc("ytil", (NTC, D))
                ctt = ab.alloc("ctt", (NTC, E))
                with nc.Block() as blk:
                    sg0, ld0, sd0 = ssg.v, sld.v, sl2.v

                    @blk.gpsimd
                    def _(g):
                        g.wait_ge(sl3.h, yown_done)
                        ssg.inc(g.collective_compute(
                            "AllGather", ALU.bypass, replica_groups=[list(range(NCORE))],
                            ins=[yown_dram.ap().opt()], outs=[ytil_dram.ap().opt()]))
                        g.wait_ge(ssg.h, sg0 + 1)
                        sld.inc(g.dma_start(out=ytil,
                                            in_=ytil_dram.ap().rearrange("(c p) f -> p c f", p=128)), 16)

                    @blk.sync
                    def _(sy):
                        sy.wait_ge(sl2.h, d_dmas[0])
                        sy.wait_ge(sl3.h, yown_done)
                        for ec in range(16):
                            sl2.inc(sy.dma_start_transpose(
                                out=ctt[:, ec, :], in_=c_dram[:, ec * 128:(ec + 1) * 128]), 16)
                yc_ld = (sld.v, sl2.v)

                with nc.Block() as blk:
                    mm0, sv0 = smm.v, ssv.v

                    @blk.tensor
                    def _(t):
                        t.wait_ge(sld.h, yc_ld[0])
                        t.wait_ge(sl2.h, yc_ld[1])
                        for tc in range(NTC):
                            tp = tc % 2
                            if tc >= 2:
                                t.wait_ge(ssv.h, sv0 + (tc - 2) + 1)
                            for nn in range(2):
                                pb = ps_f[:, (tp * 2 + nn) * 512:(tp * 2 + nn + 1) * 512]
                                for ec in range(16):
                                    ins = t.matmul(pb, ctt[:, ec, tc * 128:(tc + 1) * 128],
                                                   ytil[:, ec, nn * 512:(nn + 1) * 512],
                                                   start=(ec == 0), stop=(ec == 15))
                            smm.inc(ins)

                    @blk.vector
                    def _(v):
                        for tc in range(NTC):
                            tp = tc % 2
                            v.wait_ge(smm.h, mm0 + tc + 1)
                            ssv.inc(v.tensor_add(XS[:, tc, :], XS[:, tc, :],
                                                 ps_f[:, tp * 1024:(tp + 1) * 1024]))

            ab.freea("ytil"); ab.freea("ctt")
            DBG["ab3"] = dict(ab.live)
            with nc.Block() as blk:
                @blk.sync
                def _(sy):
                    tap("stream_p3", XS, (128, NTC, D), sy, waits=[(ssv, ssv.v)])

        if upto >= 0:
            st_dep = emit_ln_stats()
        if upto >= 2:
            with nc.named_scope("p1"):
                emit_l0_attn(st_dep)
            with nc.Block() as blk:
                @blk.sync
                def _(sy):
                    tap("stream_p1", XS, (128, NTC, D), sy, waits=[(ssv, ssv.v)])
        if upto >= 3:
            with nc.named_scope("p3"):
                st_dep2 = emit_ln_stats(xdep=(ssv, ssv.v))
                emit_l0_ffn(st_dep2)



        def emit_l1_attn(pair):
            xlnT3 = ab.alloc("xlnT3", (DC, TL))
            xtok_s = ab.alloc("xtok_s", (2, D))
            st3 = emit_ln_stats(xdep=(ssv, ssv.v))
            for sl in range(8):
                emit_xln_slice(sl * 2, 2, ab.view("xtok_s"), ab.view("xlnT3"), st3,
                               fm_off=sl * 256)
            ab.freea("xtok_s")
            lnT3 = ssv.v

            with nc.named_scope("qproj"):
                wq1 = ab.alloc("wq1", (DC, D))
                q1 = ab.alloc("q1", (DC, TL))
                bq1 = af.alloc("bq1", (8,))
                bq81 = af.alloc("bq81", (8,))
                with nc.Block() as blk:
                    @blk.gpsimd
                    def _(g):
                        g.wait_ge(sld.h, sld.v)
                        sld.inc(g.dma_start(out=wq1, in_=w_l1q.ap().rearrange("(i p) o -> p i o", p=128)), 16)

                    @blk.sync
                    def _(sy):
                        sy.wait_ge(sl2.h, sl2.v)
                        sl2.inc(sy.dma_start(out=bq1, in_=b_l1q[:, :]), 16)
                wq_ld, bq_ld = sld.v, sl2.v
                with nc.Block() as blk:
                    mm0, sv0 = smm.v, ssv.v

                    @blk.tensor
                    def _(t):
                        t.wait_ge(sld.h, wq_ld)
                        t.wait_ge(ssv.h, lnT3)
                        for oc in range(DC):
                            for qt in range(4):
                                idx = oc * 4 + qt
                                if idx >= 4:
                                    t.wait_ge(ssv.h, sv0 + idx - 3)
                                pb = ps_f[:, (idx % 4) * 512:(idx % 4) * 512 + 512]
                                for ic in range(DC):
                                    ins = t.matmul(pb, wq1[:, ic, oc * 128:(oc + 1) * 128],
                                                   xlnT3[:, ic, qt * 512:(qt + 1) * 512],
                                                   start=(ic == 0), stop=(ic == DC - 1))
                                smm.inc(ins)

                    @blk.vector
                    def _(v):
                        v.wait_ge(sl2.h, bq_ld)
                        v.tensor_scalar_mul(bq81, bq1, 0.125)
                        v.drain()
                        for oc in range(DC):
                            for qt in range(4):
                                idx = oc * 4 + qt
                                v.wait_ge(smm.h, mm0 + idx + 1)
                                pb = ps_f[:, (idx % 4) * 512:(idx % 4) * 512 + 512]
                                ssv.inc(v.tensor_scalar(q1[:, oc, qt * 512:(qt + 1) * 512], pb,
                                                        0.125, bq81[:, oc:oc + 1],
                                                        ALU.mult, ALU.add))
                ab.freea("wq1")

            with nc.named_scope("kvproj"):
                wk1 = ab.alloc("wk1", (DC, 256))
                wv1 = ab.alloc("wv1", (DC, 256))
                k1 = ab.alloc("k1", (2, TL))
                vtok = ab.alloc("vtok", (NTC, 256))
                bvr = ab.alloc("bvr", (256,))
                bk1 = af.alloc("bk1", (2,))
                with nc.Block() as blk:
                    @blk.gpsimd
                    def _(g):
                        g.wait_ge(sld.h, sld.v)
                        sld.inc(g.dma_start(out=wk1, in_=w_l1k.ap().rearrange("(i p) o -> p i o", p=128)), 16)
                        g.wait_ge(sld.h, sld.v)
                        sld.inc(g.dma_start(out=wv1, in_=w_l1v.ap().rearrange("(i p) o -> p i o", p=128)), 16)
                        g.wait_ge(sld.h, sld.v)
                        sld.inc(g.dma_start(out=bvr[0:1, :], in_=b_l1v[:, :]), 16)

                    @blk.sync
                    def _(sy):
                        sy.wait_ge(sl2.h, sl2.v)
                        sl2.inc(sy.dma_start(out=bk1, in_=b_l1k[:, :]), 16)
                wkv_ld, bk_ld = sld.v, sl2.v
                with nc.Block() as blk:
                    mm0, sv0 = smm.v, ssv.v

                    @blk.tensor
                    def _(t):
                        t.wait_ge(sld.h, wkv_ld)
                        for oc in range(2):
                            for qt in range(4):
                                idx = oc * 4 + qt
                                if idx >= 4:
                                    t.wait_ge(ssv.h, sv0 + idx - 3)
                                pb = ps_f[:, (idx % 4) * 512:(idx % 4) * 512 + 512]
                                for ic in range(DC):
                                    ins = t.matmul(pb, wk1[:, ic, oc * 128:(oc + 1) * 128],
                                                   xlnT3[:, ic, qt * 512:(qt + 1) * 512],
                                                   start=(ic == 0), stop=(ic == DC - 1))
                                smm.inc(ins)
                        t.wait_ge(ssv.h, sv0 + 8)
                        for tc in range(NTC):
                            if tc >= 2:
                                t.wait_ge(ssv.h, sv0 + 8 + tc - 1)
                            pb = ps_g[:, (tc % 2) * 512:(tc % 2) * 512 + 256]
                            for ic in range(DC):
                                t.matmul(pb, xlnT3[:, ic, tc * 128:(tc + 1) * 128],
                                         wv1[:, ic, :], start=(ic == 0), stop=False)
                            ins = t.matmul(pb, onesrow[0:1, 0:128], bvr[0:1, :],
                                           start=False, stop=True)
                            smm.inc(ins)

                    @blk.vector
                    def _(v):
                        v.wait_ge(sl2.h, bk_ld)
                        for oc in range(2):
                            for qt in range(4):
                                idx = oc * 4 + qt
                                v.wait_ge(smm.h, mm0 + idx + 1)
                                pb = ps_f[:, (idx % 4) * 512:(idx % 4) * 512 + 512]
                                ssv.inc(v.tensor_scalar(k1[:, oc, qt * 512:(qt + 1) * 512], pb,
                                                        bk1[:, oc:oc + 1], 0.0, ALU.add, ALU.add))
                        for tc in range(NTC):
                            v.wait_ge(smm.h, mm0 + 8 + tc + 1)
                            pb = ps_g[:, (tc % 2) * 512:(tc % 2) * 512 + 256]
                            ssv.inc(v.tensor_copy(vtok[:, tc, :], pb))
                ab.freea("wk1"); ab.freea("wv1"); ab.freea("xlnT3"); ab.freea("bvr")

            with nc.named_scope("rope"):
                pmt = ab.alloc("pmt", (128,))
                rtmp = ab.alloc("rtmp", (512,))
                with nc.Block() as blk:
                    @blk.gpsimd
                    def _(g):
                        g.wait_ge(sld.h, sld.v)
                        sld.inc(g.dma_start(out=pmt, in_=pmat_d[:, :]), 16)
                pm_ld = sld.v
                rp_slices = [("q", oc, qt) for oc in range(DC) for qt in range(4)] + \
                            [("k", oc, qt) for oc in range(2) for qt in range(4)]
                with nc.Block() as blk:
                    mm0, sv0 = smm.v, ssv.v

                    @blk.tensor
                    def _(t):
                        t.wait_ge(sld.h, pm_ld)
                        t.wait_ge(ssv.h, sv0)
                        for idx, (which, oc, qt) in enumerate(rp_slices):
                            if idx >= 4:
                                t.wait_ge(ssv.h, sv0 + idx - 3)
                            src_ = (q1 if which == "q" else k1)[:, oc, qt * 512:(qt + 1) * 512]
                            ins = t.matmul(ps_f[:, (idx % 4) * 512:(idx % 4) * 512 + 512],
                                           pmt, src_, start=True, stop=True)
                            smm.inc(ins)

                    @blk.vector
                    def _(v):
                        for idx, (which, oc, qt) in enumerate(rp_slices):
                            v.wait_ge(smm.h, mm0 + idx + 1)
                            sl_ = (q1 if which == "q" else k1)[:, oc, qt * 512:(qt + 1) * 512]
                            pb = ps_f[:, (idx % 4) * 512:(idx % 4) * 512 + 512]
                            v.tensor_mul(sl_, sl_, rcos[:, qt * 512:(qt + 1) * 512])
                            v.tensor_mul(rtmp, pb, rsin[:, qt * 512:(qt + 1) * 512])
                            v.drain()
                            ssv.inc(v.tensor_add(sl_, sl_, rtmp))
                            v.drain()
                rope_k = ssv.v
                ab.freea("pmt"); ab.freea("rtmp")
                ab.freea("rcos"); ab.freea("rsin")

            with nc.named_scope("bias8"):
                qbar = ab.alloc("qbar", (DC, 64))
                kbar0 = ab.alloc("kbar0", (2, 64))
                kbarg = ab.alloc("kbarg", (4, 64))
                m8b = ab.alloc("m8b", (8,))
                mg = ab.alloc("mg", (1024,))
                mr8 = ab.alloc("mr8", (8,))
                m8f = af.alloc("m8f", (8,))
                mr8f = af.alloc("mr8f", (8,))
                bias8 = af.alloc("bias8", (8,))
                with nc.Block() as blk:
                    mm0, sv0, sd0 = smm.v, ssv.v, sl2.v

                    @blk.vector
                    def _(v):
                        v.wait_ge(ssv.h, rope_k)
                        with nc.allow_low_precision(reason="coarse softmax-shift estimate only"):
                            v.reduce_sum(qbar, q1.rearrange("p h (bb t) -> p h bb t", t=32),
                                         axis=AX.X)
                            ssv.inc(v.reduce_sum(kbar0,
                                                 k1.rearrange("p h (bb t) -> p h bb t", t=32),
                                                 axis=AX.X))
                        v.wait_ge(smm.h, mm0 + 1)
                        v.reduce_max(m8f, ps_g[:, 0:512].rearrange("p (h q) -> p h q", q=64),
                                     axis=AX.X)
                        v.drain()
                        ssv.inc(v.tensor_copy(m8b, m8f))
                        v.wait_ge(sl2.h, sd0 + 9 * 16)
                        v.reduce_max(mr8f[0:1, :], mg.rearrange("p (q h) -> p h q", h=8)[0:1, :, :],
                                     axis=AX.X)
                        v.drain()
                        ssv.inc(v.tensor_copy(mr8[0:1, :], mr8f[0:1, :]))
                        v.wait_ge(smm.h, mm0 + 2)
                        ssv.inc(v.tensor_scalar(bias8, ps_g[:, 512:520], -1.0 / 1024.0, -8.0,
                                                ALU.mult, ALU.add))

                    @blk.sync
                    def _(sy):
                        sy.wait_ge(ssv.h, sv0 + 1)
                        for g_ in range(4):
                            for half in range(2):
                                sy.wait_ge(sl2.h, sl2.v)
                                sl2.inc(sy.dma_start(
                                    out=kbarg[half * 64:half * 64 + 64, g_, :],
                                    in_=kbar0[(g_ % 2) * 64:(g_ % 2) * 64 + 64, g_ // 2, :]), 16)
                        sy.wait_ge(ssv.h, sv0 + 2)
                        sy.wait_ge(sl2.h, sl2.v)
                        sl2.inc(sy.dma_start(out=mg[0:1, :], in_=m8b[:, :]), 16)

                    @blk.tensor
                    def _(t):
                        t.wait_ge(sl2.h, sd0 + 8 * 16)
                        for hp in range(8):
                            g_ = hp // 2
                            t.matmul(ps_g[0:64, hp * 64:(hp + 1) * 64],
                                     qbar[0:64, hp, :], kbarg[0:64, g_, :],
                                     start=True, stop=True, tile_position=(0, 0))
                            ins = t.matmul(ps_g[64:128, hp * 64:(hp + 1) * 64],
                                           qbar[64:128, hp, :], kbarg[64:128, g_, :],
                                           start=True, stop=True, tile_position=(64, 64))
                        smm.inc(ins)
                        t.wait_ge(ssv.h, sv0 + 3)
                        ins = t.matmul(ps_g[:, 512:520], onesrow[0:1, 0:128], mr8[0:1, :],
                                       start=True, stop=True)
                        smm.inc(ins)
                bias_done = ssv.v

            with nc.named_scope("kvag"):
                with nc.Block() as blk:
                    sg0, sd0 = ssg.v, sl2.v

                    @blk.sync
                    def _(sy):
                        sy.wait_ge(ssv.h, rope_k)
                        sy.wait_ge(sl2.h, sl2.v)
                        sl2.inc(sy.dma_start(out=k_ag_in.ap().rearrange("(c p) s -> p c s", p=128),
                                             in_=k1), 16)
                        sy.wait_ge(sl2.h, sl2.v)
                        sl2.inc(sy.dma_start(out=v_ag_in.ap().rearrange("(c p) f -> p c f", p=128),
                                             in_=vtok), 16)

                    @blk.gpsimd
                    def _(g):
                        g.wait_ge(sl2.h, sd0 + 32)
                        ssg.inc(g.collective_compute(
                            "AllGather", ALU.bypass, replica_groups=pair,
                            ins=[k_ag_in.ap().opt()], outs=[k_ag_out.ap().opt()]))
                        ssg.inc(g.collective_compute(
                            "AllGather", ALU.bypass, replica_groups=pair,
                            ins=[v_ag_in.ap().opt()], outs=[v_ag_out.ap().opt()]))
                ag_done = ssg.v
                ab.freea("vtok"); ab.freea("k1")
                for nm in ("qbar", "kbar0", "kbarg", "m8b", "mg", "mr8"):
                    ab.freea(nm)

            pst_stack.close()
            psx_stack = ExitStack()
            ps_x = psx_stack.enter_context(nc.psum_tensor("PSX", [128, 1024], F32))

            with nc.named_scope("mloop"):
                kdup = ab.alloc("kdup", (2, 4096))
                vaug = ab.alloc("vaug", (2, 32, 65))
                bufP = ab.alloc("bufP", (2, 2, 1024))
                obuf = ab.alloc("obuf", (DC, TL))
                zfm = af.alloc("zfm", (2048,))
                g_ld = []

                with nc.Block() as blk:
                    ld0 = sld.v

                    @blk.gpsimd
                    def _(g):
                        g.wait_ge(ssg.h, ag_done)
                        for g_ in range(2):
                            for half in range(2):
                                for bh in range(2):
                                    g.wait_ge(sld.h, sld.v)
                                    sld.inc(g.dma_start(
                                        out=kdup[half * 64:half * 64 + 64, g_ % 2,
                                                 bh * 2048:(bh + 1) * 2048],
                                        in_=k_ag_out[bh * 256 + g_ * 64: bh * 256 + (g_ + 1) * 64, :]), 16)
                            g.wait_ge(sld.h, sld.v)
                            sld.inc(g.dma_start(
                                out=vaug[:, g_ % 2, :, 0:64],
                                in_=v_ag_out.ap().rearrange("(sc p) f -> p sc f", p=128)
                                [:, :, g_ * 64:(g_ + 1) * 64]), 16)
                            g_ld.append(sld.v)

                    @blk.vector
                    def _(v):
                        v.memset(vaug[:, 0, :, 64:65], 1.0)
                        ssv.inc(v.memset(vaug[:, 1, :, 64:65], 1.0))
                ones_set = ssv.v

                with nc.Block() as blk:
                    mm0, sv0, ss0, zd0 = smm.v, ssv.v, sss.v, sl2.v
                    gmark = [mm0 + (gg + 1) * 4 * 96 for gg in range(4)]

                    @blk.gpsimd
                    def _(g):
                        for g_ in range(2, 4):
                            g.wait_ge(smm.h, gmark[g_ - 2])
                            for half in range(2):
                                for bh in range(2):
                                    g.wait_ge(sld.h, sld.v)
                                    sld.inc(g.dma_start(
                                        out=kdup[half * 64:half * 64 + 64, g_ % 2,
                                                 bh * 2048:(bh + 1) * 2048],
                                        in_=k_ag_out[bh * 256 + g_ * 64: bh * 256 + (g_ + 1) * 64, :]), 16)
                            g.wait_ge(sld.h, sld.v)
                            sld.inc(g.dma_start(
                                out=vaug[:, g_ % 2, :, 0:64],
                                in_=v_ag_out.ap().rearrange("(sc p) f -> p sc f", p=128)
                                [:, :, g_ * 64:(g_ + 1) * 64]), 16)
                            g_ld.append(sld.v)

                    @blk.tensor
                    def _(t):
                        t.wait_ge(ssv.h, ones_set)
                        t.wait_ge(ssv.h, bias_done)
                        for hp in range(8):
                            g_ = hp // 2
                            gp = g_ % 2
                            t.wait_ge(sld.h, g_ld[g_])
                            for qh in range(2):
                                it = hp * 2 + qh
                                base_t = mm0 + it * 96
                                base_s = ss0 + it * 64
                                for sc in range(32):
                                    for hx in range(2):
                                        if sc >= 1:
                                            t.wait_ge(sss.h, base_s + 2 * (sc - 1) + hx + 1)
                                        elif it >= 1 and hx == 0:
                                            t.wait_ge(sss.h, base_s)
                                        q_sl = q1[hx * 64:hx * 64 + 64, hp,
                                                  qh * 1024:(qh + 1) * 1024]
                                        k_sl = kdup[hx * 64:hx * 64 + 64, gp,
                                                    sc * 128:(sc + 1) * 128]
                                        for q2 in range(2):
                                            ins = t.matmul(
                                                ps_f[:, (hx * 2 + q2) * 512:(hx * 2 + q2 + 1) * 512],
                                                k_sl, q_sl[:, q2 * 512:(q2 + 1) * 512],
                                                start=True, stop=True,
                                                tile_position=(hx * 64, 0))
                                        smm.inc(ins)
                                    if sc >= 1:
                                        if sc == 1 and it >= 1:
                                            t.wait_ge(ssv.h, sv0 + it * 4)
                                        t.wait_ge(sss.h, base_s + 2 * sc)
                                        for hx in range(2):
                                            for q2 in range(2):
                                                pv = (ps_g if hx == 0 else ps_x)
                                                ins = t.matmul(
                                                    pv[0:65, q2 * 512:(q2 + 1) * 512],
                                                    vaug[:, gp, sc - 1, :],
                                                    bufP[:, (sc - 1) % 2, hx, q2 * 512:(q2 + 1) * 512],
                                                    start=(sc - 1 == 0), stop=False)
                                        smm.inc(ins)
                                t.wait_ge(sss.h, base_s + 64)
                                for hx in range(2):
                                    for q2 in range(2):
                                        pv = (ps_g if hx == 0 else ps_x)
                                        ins = t.matmul(pv[0:65, q2 * 512:(q2 + 1) * 512],
                                                       vaug[:, gp, 31, :],
                                                       bufP[:, 31 % 2, hx, q2 * 512:(q2 + 1) * 512],
                                                       start=False, stop=True)
                                smm.inc(ins)

                    @blk.scalar
                    def _(s):
                        for hp in range(8):
                            for qh in range(2):
                                it = hp * 2 + qh
                                base_t = mm0 + it * 96
                                base_s = ss0 + it * 64
                                for sc in range(32):
                                    for hx in range(2):
                                        sw = (base_t + hx + 1 if sc == 0
                                              else base_t + 2 + 3 * (sc - 1) + hx + 1)
                                        s.wait_ge(smm.h, sw)
                                        sss.inc(s.activation(
                                            bufP[:, sc % 2, hx, :],
                                            ps_f[:, hx * 1024:(hx + 1) * 1024], AF.Exp,
                                            bias=bias8[:, hp:hp + 1]))

                    @blk.vector
                    def _(v):
                        for hp in range(8):
                            for qh in range(2):
                                it = hp * 2 + qh
                                base_t = mm0 + it * 96
                                v.wait_ge(smm.h, base_t + 96)
                                if it >= 1:
                                    v.wait_ge(sl2.h, zd0 + it * 32)
                                v.tensor_copy(zfm[64:65, 0:1024], ps_g[64:65, 0:1024])
                                v.tensor_copy(zfm[64:65, 1024:2048], ps_x[64:65, 0:1024])
                                for hx in range(2):
                                    pv = (ps_g if hx == 0 else ps_x)
                                    for q2 in range(2):
                                        col = qh * 1024 + q2 * 512
                                        ssv.inc(v.tensor_copy(
                                            obuf[hx * 64:hx * 64 + 64, hp, col:col + 512],
                                            pv[0:64, q2 * 512:(q2 + 1) * 512]))

                    @blk.sync
                    def _(sy):
                        for hp in range(8):
                            for qh in range(2):
                                it = hp * 2 + qh
                                sy.wait_ge(ssv.h, sv0 + it * 4 + 4)
                                for hx in range(2):
                                    h = 2 * hp + hx
                                    sy.wait_ge(sl2.h, sl2.v)
                                    sl2.inc(sy.dma_start(
                                        out=zfm[h:h + 1, qh * 1024:(qh + 1) * 1024],
                                        in_=zfm[64:65, hx * 1024:(hx + 1) * 1024]), 16)
                att5 = ssv.v
                zfm_done = sl2.v
                ab.freea("kdup"); ab.freea("vaug"); ab.freea("bufP")

            with nc.named_scope("norm"):
                wo1 = ab.alloc("wo1", (DC, D))
                bor = ab.alloc("bor", (D,))
                zrb = ab.alloc("zrb", (2048,))
                zr0 = ab.alloc("zr0", (2048,))
                with nc.Block() as blk:
                    ld0 = sld.v

                    @blk.gpsimd
                    def _(g):
                        g.wait_ge(sld.h, sld.v)
                        sld.inc(g.dma_start(out=wo1, in_=w_l1o.ap().rearrange("(i p) o -> p i o", p=128)), 16)
                        g.wait_ge(sld.h, sld.v)
                        sld.inc(g.dma_start(out=bor[0:1, :], in_=b_l1o[:, :]), 16)
                wo_ld = sld.v

                with nc.Block() as blk:
                    mm0, sv0, sd0 = smm.v, ssv.v, sl2.v

                    @blk.vector
                    def _(v):
                        v.wait_ge(ssv.h, att5)
                        v.wait_ge(sl2.h, zfm_done)
                        v.reciprocal(zfm[0:16, :], zfm[0:16, :])
                        v.drain()
                        ssv.inc(v.tensor_copy(zrb[0:16, :], zfm[0:16, :]))
                        for h in range(16):
                            v.wait_ge(smm.h, mm0 + h + 1)
                            hb = (h % 2) * 64
                            for qt in range(4):
                                pv = ps_g if qt < 2 else ps_x
                                ins = v.tensor_mul(
                                    obuf[hb:hb + 64, h // 2, qt * 512:(qt + 1) * 512],
                                    obuf[hb:hb + 64, h // 2, qt * 512:(qt + 1) * 512],
                                    pv[hb:hb + 64, (qt % 2) * 512:(qt % 2) * 512 + 512])
                            ssv.inc(ins)

                    @blk.sync
                    def _(sy):
                        for h in range(16):
                            sy.wait_ge(ssv.h, sv0 + 1)
                            sy.wait_ge(sl2.h, sl2.v)
                            if h >= 1:
                                sy.wait_ge(smm.h, mm0 + h)
                            sl2.inc(sy.dma_start(out=zr0[0:1, :], in_=zrb[h:h + 1, :]), 16)

                    @blk.tensor
                    def _(t):
                        t.wait_ge(sld.h, wo_ld)
                        for h in range(16):
                            t.wait_ge(sl2.h, sd0 + 16 * (h + 1))
                            if h >= 1:
                                t.wait_ge(ssv.h, sv0 + h + 1)
                            hb = (h % 2) * 64
                            for qt in range(4):
                                pv = ps_g if qt < 2 else ps_x
                                ins = t.matmul(pv[hb:hb + 64, (qt % 2) * 512:(qt % 2) * 512 + 512],
                                               onesrow[0:1, 0:64],
                                               zr0[0:1, qt * 512:(qt + 1) * 512],
                                               start=True, stop=True,
                                               tile_position=(0, hb))
                            smm.inc(ins)
                zdone = ssv.v
                ab.freea("zrb")

                with nc.Block() as blk:
                    mm0, sv0 = smm.v, ssv.v

                    @blk.tensor
                    def _(t):
                        t.wait_ge(ssv.h, zdone)
                        for tc in range(NTC):
                            tp = tc % 2
                            if tc >= 2:
                                t.wait_ge(ssv.h, sv0 + tc - 1)
                            for nn in range(2):
                                pb = ps_f[:, (tp * 2 + nn) * 512:(tp * 2 + nn + 1) * 512]
                                for ic in range(DC):
                                    t.matmul(pb, obuf[:, ic, tc * 128:(tc + 1) * 128],
                                             wo1[:, ic, nn * 512:(nn + 1) * 512],
                                             start=(ic == 0), stop=False)
                                ins = t.matmul(pb, onesrow[0:1, 0:128],
                                               bor[0:1, nn * 512:(nn + 1) * 512],
                                               start=False, stop=True)
                            smm.inc(ins)

                    @blk.vector
                    def _(v):
                        for tc in range(NTC):
                            tp = tc % 2
                            v.wait_ge(smm.h, mm0 + tc + 1)
                            ssv.inc(v.tensor_add(XS[:, tc, :], XS[:, tc, :],
                                                 ps_f[:, tp * 1024:(tp + 1) * 1024]))

                for nm in ("q1", "obuf", "wo1", "bor", "zr0"):
                    ab.freea(nm)
                for nm in ("bq1", "bq81", "bk1", "m8f", "mr8f", "bias8", "zfm"):
                    af.freea(nm)
                psx_stack.close()
            with nc.Block() as blk:
                @blk.sync
                def _(sy):
                    tap("stream_p5", XS, (128, NTC, D), sy, waits=[(ssv, ssv.v)])


        def emit_l1_ffn(pair):
            pst2 = stack.enter_context(nc.psum_tensor("PST2", [128, 2048], BF16))
            cn1 = ab.alloc("cn1", (DC, D))
            cn2 = ab.alloc("cn2", (DC, 512))
            cn3 = ab.alloc("cn3", (4, 256))
            cn4 = ab.alloc("cn4", (2,))
            gc1 = ab.alloc("gc1", (DC, 512))
            gc2 = ab.alloc("gc2", (4, 256))
            gc3 = ab.alloc("gc3", (2,))
            onescol = ab.alloc("onescol", (1,))
            gxr = ab.alloc("gxr", (1024,))
            gfm = ab.alloc("gfm", (8,))
            g1b = ab.alloc("g1b", (4,))
            g2b = ab.alloc("g2b", (2,))
            abrow = ab.alloc("abrow", (3,))
            xg = ab.alloc("xg", (2, D))
            cb1 = af.alloc("cb1", (8,)); cb2 = af.alloc("cb2", (4,)); cb3 = af.alloc("cb3", (2,))
            b4r = af.alloc("b4r", (1,)); gb1 = af.alloc("gb1", (4,)); gb2 = af.alloc("gb2", (2,))
            gb3 = af.alloc("gb3", (1,)); arr = af.alloc("arr", (1,))
            tiny = af.alloc("tiny", (8,))
            acbc = af.alloc("acbc", (3,))
            garow = af.alloc("garow", (1024,)); gbrow = af.alloc("gbrow", (1024,))

            with nc.Block() as blk:
                @blk.gpsimd
                def _(g):
                    for dst, srcd in ((cn1, w_cn1), (cn2, w_cn2), (cn3, w_cn3),
                                      (gc1, w_gc1), (gc2, w_gc2)):
                        g.wait_ge(sld.h, sld.v)
                        sld.inc(g.dma_start(out=dst,
                                            in_=srcd.ap().rearrange("(i p) o -> p i o", p=128)), 16)
                    for dst, srcd in ((cn4, w_cn4), (gc3, w_gc3)):
                        g.wait_ge(sld.h, sld.v)
                        sld.inc(g.dma_start(out=dst, in_=srcd[:, :]), 16)

                @blk.sync
                def _(sy):
                    for dst, srcd in ((cb1, b_cn1), (cb2, b_cn2), (cb3, b_cn3),
                                      (gb1, b_gc1), (gb2, b_gc2)):
                        sy.wait_ge(sl2.h, sl2.v)
                        sl2.inc(sy.dma_start(out=dst, in_=srcd[:, :]), 16)
                    for dst, srcd in ((b4r, b_cn4), (gb3, b_gc3), (arr, a_rng)):
                        sy.wait_ge(sl2.h, sl2.v)
                        sl2.inc(sy.dma_start(out=dst[0:1, :], in_=srcd[:, :]), 16)

                @blk.vector
                def _(v):
                    ssv.inc(v.memset(onescol, 1.0))
                    v.wait_ge(sld.h, sld.v)
                    v.tensor_scalar_mul(cn4, cn4, 0.5)
                    ssv.inc(v.tensor_scalar_mul(gc3, gc3, 0.5))
            w7_ld, b7_ld = sld.v, sl2.v
            st4 = emit_ln_stats(xdep=(ssv, ssv.v))

            with nc.Block() as blk:
                mm0, sv0, sd0 = smm.v, ssv.v, sl2.v

                @blk.vector
                def _(v):
                    v.wait_ge(ssv.h, st4)
                    for tc in range(NTC):
                        tp = tc % 2
                        if tc >= 2:
                            v.wait_ge(smm.h, mm0 + tc - 1)
                        ssv.inc(v.tensor_scalar(xg[:, tp, :], XS[:, tc, :],
                                                stats[:, 7, tc:tc + 1], stats[:, 9, tc:tc + 1],
                                                ALU.mult, ALU.add))
                    v.wait_ge(smm.h, mm0 + NTC)
                    ssv.inc(v.tensor_copy(garow[0:1, :], ps_g[0:1, 0:1024]))

                @blk.tensor
                def _(t):
                    for tc in range(NTC):
                        tp = tc % 2
                        t.wait_ge(ssv.h, sv0 + tc + 1)
                        for nn in range(2):
                            ins = t.matmul(ps_g[0:1, nn * 512:(nn + 1) * 512],
                                           onescol[:, 0:1], xg[:, tp, nn * 512:(nn + 1) * 512],
                                           start=(tc == 0), stop=(tc == NTC - 1))
                        smm.inc(ins)

                @blk.sync
                def _(sy):
                    sy.wait_ge(ssv.h, sv0 + NTC + 1)
                    sy.wait_ge(sl2.h, sl2.v)
                    sl2.inc(sy.dma_start(out=gm_ag_in[:, :], in_=garow[0:1, :]), 16)

                @blk.gpsimd
                def _(g):
                    g.wait_ge(sl2.h, sd0 + 16)
                    ssg.inc(g.collective_compute(
                        "AllGather", ALU.bypass, replica_groups=pair,
                        ins=[gm_ag_in.ap().opt()], outs=[gm_ag_out.ap().opt()]))
                    g.wait_ge(ssg.h, ssg.v)
                    g.wait_ge(sld.h, sld.v)
                    sld.inc(g.dma_start(out=garow[0:1, :], in_=gm_ag_out[0:1, :]), 16)
                    g.wait_ge(sld.h, sld.v)
                    sld.inc(g.dma_start(out=gbrow[0:1, :], in_=gm_ag_out[1:2, :]), 16)
            gm_ld = sld.v
            if P7SUB < 1:
                return

            with nc.Block() as blk:
                mm0, sv0, ss0, sd0 = smm.v, ssv.v, sss.v, sl2.v

                @blk.vector
                def _(v):
                    v.wait_ge(sld.h, gm_ld)
                    v.wait_ge(sl2.h, b7_ld)
                    v.tensor_add(garow[0:1, :], garow[0:1, :], gbrow[0:1, :])
                    v.drain()
                    ssv.inc(v.tensor_scalar(gxr[0:1, :], garow[0:1, :], 1.0 / 4096.0, 0.0,
                                            ALU.mult, ALU.add))
                    v.tensor_scalar(tiny[0:1, 6:7], b4r[0:1, 0:1], 0.5, 0.0, ALU.mult, ALU.add)
                    ssv.inc(v.tensor_scalar(tiny[0:1, 7:8], gb3[0:1, 0:1], 0.5, 0.0,
                                            ALU.mult, ALU.add))
                    v.wait_ge(smm.h, mm0 + 1)
                    ssv.inc(v.tensor_copy(gfm, ps_g[:, 700:708]))
                    v.wait_ge(sss.h, ss0 + 7)
                    v.tensor_mul(tiny[0:1, 0:1], arr[0:1, 0:1], tiny[0:1, 2:3])
                    v.drain()
                    v.tensor_scalar(tiny[0:1, 4:5], tiny[0:1, 0:1], 0.7, 0.1, ALU.mult, ALU.add)
                    v.tensor_scalar(tiny[0:1, 5:6], arr[0:1, 0:1], 0.3, 0.0, ALU.mult, ALU.add)
                    v.drain()
                    ssv.inc(v.tensor_copy(abrow[0:1, :], tiny[0:1, 4:7]))
                    v.wait_ge(smm.h, mm0 + 5)
                    ssv.inc(v.tensor_copy(acbc, ps_g[:, 600:603]))

                @blk.tensor
                def _(t):
                    t.wait_ge(sld.h, w7_ld)
                    t.wait_ge(ssv.h, sv0 + 1)
                    for j in range(8):
                        ins = t.matmul(ps_g[:, 700 + j:701 + j],
                                       gxr[0:1, j * 128:(j + 1) * 128],
                                       onesrow[0:1, 0:1], start=True, stop=True)
                    smm.inc(ins)
                    t.wait_ge(ssv.h, sv0 + 3)
                    for oc in range(4):
                        for ic in range(DC):
                            ins = t.matmul(ps_f[:, oc * 512:oc * 512 + 1],
                                           gc1[:, ic, oc * 128:(oc + 1) * 128],
                                           gfm[:, ic:ic + 1],
                                           start=(ic == 0), stop=(ic == DC - 1))
                    smm.inc(ins)
                    t.wait_ge(sss.h, ss0 + 4)
                    for oc in range(2):
                        for ic in range(4):
                            ins = t.matmul(ps_g[:, oc * 512:oc * 512 + 1],
                                           gc2[:, ic, oc * 128:(oc + 1) * 128],
                                           g1b[:, ic:ic + 1], start=(ic == 0), stop=(ic == 3))
                    smm.inc(ins)
                    t.wait_ge(sss.h, ss0 + 6)
                    for ic in range(2):
                        ins = t.matmul(ps_f[0:1, 1024:1025], gc3[:, ic:ic + 1],
                                       g2b[:, ic:ic + 1], start=(ic == 0), stop=(ic == 1))
                    smm.inc(ins)
                    t.wait_ge(ssv.h, sv0 + 4)
                    ins = t.matmul(ps_g[:, 600:603], onesrow[0:1, 0:128], abrow[0:1, :],
                                   start=True, stop=True)
                    smm.inc(ins)

                @blk.scalar
                def _(s):
                    s.wait_ge(smm.h, mm0 + 2)
                    for oc in range(4):
                        sss.inc(s.activation(g1b[:, oc:oc + 1], ps_f[:, oc * 512:oc * 512 + 1],
                                             AF.Gelu, bias=gb1[:, oc:oc + 1]))
                    s.wait_ge(smm.h, mm0 + 3)
                    for oc in range(2):
                        sss.inc(s.activation(g2b[:, oc:oc + 1], ps_g[:, oc * 512:oc * 512 + 1],
                                             AF.Gelu, bias=gb2[:, oc:oc + 1]))
                    s.wait_ge(smm.h, mm0 + 4)
                    s.wait_ge(ssv.h, sv0 + 2)
                    sss.inc(s.activation(tiny[0:1, 2:3], ps_f[0:1, 1024:1025], AF.Tanh,
                                         bias=tiny[0:1, 7:8]))
            ab_done = ssv.v
            if P7SUB < 2:
                return

            af.freea("fscr"); af.freea("garow"); af.freea("gbrow")
            osf0 = af.alloc("osf0", (2, 1024))
            osb1 = ab.alloc("osb1", (2, 2048))
            osf1 = ab.view("osb1").bitcast(F32)
            osfs = [osf0, osf1]
            otok = [ab.alloc(f"otok{s}", (2, D)) for s in range(2)]
            outT = [ab.alloc(f"outT{s}", (DC, 256)) for s in range(2)]
            h1b = [ab.alloc(f"h1b{s}", (DC, 256)) for s in range(2)]
            h2b = [ab.alloc(f"h2b{s}", (4, 256)) for s in range(2)]
            h3b = [ab.alloc(f"h3b{s}", (2, 256)) for s in range(2)]
            ucol = af.alloc("ucol", (4,))
            ccol = af.alloc("ccol", (2,))
            ncol = af.alloc("ncol", (2,))
            twq = af.alloc("twq", (512,))
            DBG["ab7"] = dict(ab.live); DBG["af7"] = dict(af.live)

            def h1reg(s, j):
                return ps_f[:, ((2 * j + s) % 4) * 512:((2 * j + s) % 4) * 512 + 256]

            def h2reg(s):
                return ps_g[:, s * 512:s * 512 + 256]

            def h3reg(s, j):
                return ps_f[:, ((2 * j + s) % 4) * 512:((2 * j + s) % 4) * 512 + 256]

            def h4reg(s, i):
                ps = ps_g if s == 0 else ps_f
                return ps[:, 300 + i * 512:301 + i * 512]

            for pj in range(min(4, max(0, P7SUB - 1))):
                tcA, tcB = pj * 4, pj * 4 + 2
                tc0s = [tcA, tcB]
                with nc.Block() as blk:
                    mm0, sv0, ss0 = smm.v, ssv.v, sss.v

                    @blk.vector
                    def _(v):
                        v.wait_ge(ssv.h, ab_done)
                        for s in range(2):
                            for i in range(2):
                                ssv.inc(v.tensor_scalar(
                                    osfs[s][:, i, :], XS[:, tc0s[s] + i, :],
                                    stats[:, 7, tc0s[s] + i:tc0s[s] + i + 1],
                                    stats[:, 9, tc0s[s] + i:tc0s[s] + i + 1],
                                    ALU.mult, ALU.add))
                        v.drain()
                        for cyc in range(3):
                            cm = mm0 + cyc * 36
                            cs = ss0 + cyc * 32
                            for s in range(2):
                                for i in range(2):
                                    ssv.inc(v.tensor_copy(otok[s][:, i, :], osfs[s][:, i, :]))
                            for i in range(2):
                                for s in range(2):
                                    v.wait_ge(smm.h, cm + 2 * i + s + 1)
                                    ssv.inc(v.tensor_copy(
                                        outT[s][:, :, i * 128:(i + 1) * 128],
                                        pst2[:, s * 1024:s * 1024 + 1024]
                                        .rearrange("p (f q) -> p f q", q=128)))
                            for s in range(2):
                                for i in range(2):
                                    v.wait_ge(sss.h, cs + 28 + 2 * s + i + 1)
                                    v.tensor_scalar(ccol[:, i:i + 1], ucol[:, 2 * s + i:2 * s + i + 1],
                                                    acbc[:, 1:2], acbc[:, 0:1],
                                                    ALU.mult, ALU.add)
                                    v.drain()
                                    v.tensor_scalar_mul(ncol[:, i:i + 1], ccol[:, i:i + 1], -1.0)
                                    os_i = osfs[s][:, i, :]
                                    v.tensor_copy(twq, os_i[:, 0:512])
                                    v.drain()
                                    v.scalar_tensor_tensor(os_i[:, 0:256], os_i[:, 512:768],
                                                           ccol[:, i:i + 1], os_i[:, 0:256],
                                                           ALU.mult, ALU.add)
                                    v.scalar_tensor_tensor(os_i[:, 256:512], os_i[:, 768:1024],
                                                           ncol[:, i:i + 1], os_i[:, 256:512],
                                                           ALU.mult, ALU.add)
                                    v.drain()
                                    v.scalar_tensor_tensor(os_i[:, 512:768], twq[:, 0:256],
                                                           ncol[:, i:i + 1], os_i[:, 512:768],
                                                           ALU.mult, ALU.add)
                                    ssv.inc(v.scalar_tensor_tensor(
                                        os_i[:, 768:1024], twq[:, 256:512],
                                        ccol[:, i:i + 1], os_i[:, 768:1024],
                                        ALU.mult, ALU.add))
                                    v.drain()
                        for s in range(2):
                            for i in range(2):
                                ssv.inc(v.tensor_add(XS[:, tc0s[s] + i, :],
                                                     XS[:, tc0s[s] + i, :], osfs[s][:, i, :]))

                    @blk.tensor
                    def _(t):
                        t.wait_ge(sld.h, w7_ld)
                        for cyc in range(3):
                            cm = mm0 + cyc * 36
                            cs = ss0 + cyc * 32
                            cb0 = sv0 + 4 + cyc * 12
                            for i in range(2):
                                for s in range(2):
                                    if i == 0:
                                        t.wait_ge(ssv.h, cb0 + 2 * s + 1)
                                    else:
                                        t.wait_ge(ssv.h, cb0 + 4 + s + 1)
                                    for fc in range(DC):
                                        ins = t.transpose(
                                            pst2[:, s * 1024 + fc * 128:s * 1024 + (fc + 1) * 128],
                                            otok[s][:, i, fc * 128:(fc + 1) * 128], ident)
                                    smm.inc(ins)
                            for j in range(DC):
                                for s in range(2):
                                    if j == 0:
                                        t.wait_ge(ssv.h, cb0 + 6 + s + 1)
                                    if j >= 2:
                                        t.wait_ge(sss.h, cs + 2 * (j - 2) + s + 1)
                                    for ic in range(DC):
                                        ins = t.matmul(h1reg(s, j),
                                                       cn1[:, ic, j * 128:(j + 1) * 128],
                                                       outT[s][:, ic, :], start=(ic == 0),
                                                       stop=(ic == DC - 1))
                                    smm.inc(ins)
                            for j in range(4):
                                for s in range(2):
                                    if j == 0:
                                        t.wait_ge(sss.h, cs + 14 + s + 1)
                                    else:
                                        t.wait_ge(sss.h, cs + 16 + 2 * (j - 1) + s + 1)
                                    for ic in range(DC):
                                        ins = t.matmul(h2reg(s),
                                                       cn2[:, ic, j * 128:(j + 1) * 128],
                                                       h1b[s][:, ic, :], start=(ic == 0),
                                                       stop=(ic == DC - 1))
                                    smm.inc(ins)
                            for j in range(2):
                                for s in range(2):
                                    t.wait_ge(sss.h, cs + 22 + s + 1)
                                    for ic in range(4):
                                        ins = t.matmul(h3reg(s, j),
                                                       cn3[:, ic, j * 128:(j + 1) * 128],
                                                       h2b[s][:, ic, :], start=(ic == 0),
                                                       stop=(ic == 3))
                                    smm.inc(ins)
                            for s in range(2):
                                t.wait_ge(sss.h, cs + 26 + s + 1)
                                for i in range(2):
                                    for ic in range(2):
                                        ins = t.matmul(h4reg(s, i),
                                                       h3b[s][:, ic, i * 128:(i + 1) * 128],
                                                       cn4[:, ic:ic + 1],
                                                       start=(ic == 0), stop=(ic == 1))
                                    smm.inc(ins)

                    @blk.scalar
                    def _(s_):
                        for cyc in range(3):
                            cm = mm0 + cyc * 36
                            for j in range(DC):
                                for s in range(2):
                                    s_.wait_ge(smm.h, cm + 4 + 2 * j + s + 1)
                                    sss.inc(s_.activation(h1b[s][:, j, :], h1reg(s, j),
                                                          AF.Gelu, bias=cb1[:, j:j + 1]))
                            for j in range(4):
                                for s in range(2):
                                    s_.wait_ge(smm.h, cm + 20 + 2 * j + s + 1)
                                    sss.inc(s_.activation(h2b[s][:, j, :], h2reg(s),
                                                          AF.Gelu, bias=cb2[:, j:j + 1]))
                            for j in range(2):
                                for s in range(2):
                                    s_.wait_ge(smm.h, cm + 28 + 2 * j + s + 1)
                                    sss.inc(s_.activation(h3b[s][:, j, :], h3reg(s, j),
                                                          AF.Gelu, bias=cb3[:, j:j + 1]))
                            for s in range(2):
                                for i in range(2):
                                    s_.wait_ge(smm.h, cm + 32 + 2 * s + i + 1)
                                    sss.inc(s_.activation(ucol[:, 2 * s + i:2 * s + i + 1],
                                                          h4reg(s, i),
                                                          AF.Tanh, bias=acbc[:, 2:3]))

            for nm in ("cn1", "cn2", "cn3", "cn4", "gc1", "gc2", "gc3", "onescol",
                       "gxr", "gfm", "g1b", "g2b", "abrow", "xg", "osb1",
                       "otok0", "otok1", "outT0", "outT1", "h1b0", "h1b1",
                       "h2b0", "h2b1", "h3b0", "h3b1"):
                ab.freea(nm)
            for nm in ("cb1", "cb2", "cb3", "b4r", "gb1", "gb2", "gb3", "arr",
                       "tiny", "acbc", "osf0", "ucol", "ccol", "ncol", "twq"):
                af.freea(nm)
            with nc.Block() as blk:
                @blk.sync
                def _(sy):
                    tap("stream_p7", XS, (128, NTC, D), sy, waits=[(ssv, ssv.v)])

        pg = [[0, 1], [2, 3], [4, 5], [6, 7]]
        if upto >= 4:
            with nc.named_scope("p5"):
                emit_l1_attn(pg)
        if upto >= 5:
            with nc.named_scope("p7"):
                emit_l1_ffn(pg)

        with nc.Block() as blk:
            @blk.sync
            def _(sy):
                sy.wait_ge(ssv.h, ssv.v)
                sy.wait_ge(smm.h, smm.v)
                sy.wait_ge(sss.h, sss.v)
                sy.wait_ge(sld.h, sld.v)
                sy.wait_ge(sl2.h, sl2.v)
                sst.inc(sy.dma_start(out=out_d.ap().rearrange("(c p) f -> p c f", p=128),
                                     in_=XS), 16)
                sy.wait_ge(sst.h, sst.v)

    return nc, tap_outs


def kernel(**inputs):
    maps = _host_prep(inputs)
    nc, _ = build()
    res = run_bass_kernel_spmd(nc, maps, core_ids=list(range(NCORE)))
    out = np.concatenate([r["out"] for r in res.results], axis=0)
    return out.reshape(4, 4096, D).astype(np.float32)


def dev_run(inputs, taps=(), upto=99):
    maps = _host_prep(inputs)
    nc, tap_outs = build(taps=taps, upto=upto)
    res = run_bass_kernel_spmd(nc, maps, core_ids=list(range(NCORE)))
    return res.results
